# version 13
# baseline (speedup 1.0000x reference)
import numpy as np
import ml_dtypes
import concourse.bass as bass
import concourse.mybir as mybir
from concourse.bass_utils import run_bass_kernel_spmd

F32 = mybir.dt.float32
BF16 = mybir.dt.bfloat16
AF = mybir.ActivationFunctionType
ALU = mybir.AluOpType
AX = mybir.AxisListType
BF = ml_dtypes.bfloat16

B, S, D = 4, 4096, 2048
CH = 512
T_OWN = 2048
P_ALL = 4096
NSLOT = 4
KS = [8, 16, 24, 32]
SOFT_SHIFT = -8.0

_TAPS = False
_PHASES = 5
_CACHE = {}


class Prog:
    ENGINES = ("sync", "scalar", "vector", "tensor", "gpsimd")

    def __init__(self, nc):
        self.nc = nc
        self.steps = []
        self.sems = {}
        self.counts = {}
        self.last_ev = {}
        self._pending = {}

    def sem(self, name):
        if name not in self.sems:
            self.sems[name] = self.nc.alloc_semaphore(name)
            self.counts[name] = 0
        return self.sems[name]

    def emit(self, engine, fn, waits=(), inc=None, n=1, nochain=False):
        w = [x for x in waits if x is not None]
        if engine in self._pending:
            w += self._pending.pop(engine)
        ev = None
        incs = []
        if inc is not None:
            self.sem(inc)
            self.counts[inc] += n
            ev = (inc, self.counts[inc])
            incs = [(inc, n)]
            self.last_ev[engine] = ev
        self.steps.append((engine, fn, w, incs, nochain))
        return ev

    def barrier(self):
        evs = list(self.last_ev.values())
        for e in self.ENGINES:
            self._pending[e] = list(evs)

    def build(self):
        nc = self.nc
        with nc.Block() as block:
            for eng_name in self.ENGINES:
                if not any(s[0] == eng_name for s in self.steps):
                    continue

                def make(en):
                    def emit_stream(eng):
                        waited = {}
                        own = {"vector": "dve", "scalar": "act"}.get(en)
                        cnt = 0
                        for (e, fn, waits, incs, nochain) in self.steps:
                            if e != en:
                                continue
                            if own is not None and cnt > 0 and own in self.sems \
                                    and not nochain and waited.get(own, -1) < cnt:
                                eng.wait_ge(self.sems[own], cnt)
                                waited[own] = cnt
                            for (s, v) in waits:
                                if waited.get(s, -1) < v:
                                    eng.wait_ge(self.sems[s], v)
                                    waited[s] = v
                            ins = fn(eng)
                            for (s, a) in incs:
                                ins = ins.then_inc(self.sems[s], a)
                                if s == own:
                                    cnt += a
                    return emit_stream

                getattr(block, eng_name)(make(eng_name))


def build_nc():
    nc = bass.Bass("TRN2", target_bir_lowering=False)
    p = Prog(nc)

    di = {}

    def inp(name, shape, dt):
        di[name] = nc.dram_tensor(name, list(shape), dt, kind="ExternalInput").ap()

    inp("xp", [P_ALL, D], F32)
    inp("ccss", [64, P_ALL], F32)
    inp("sscc", [64, P_ALL], F32)
    inp("biasmat", [128, 80], F32)
    inp("masks", [128, 4096], BF16)
    inp("grot", [128, 128], BF16)
    inp("gpass", [128, 128], BF16)
    inp("ident", [128, 128], BF16)
    inp("ones1", [1, 64], F32)
    inp("onestok", [1, 128], BF16)
    inp("wq", [16, 128, 16, 128], F32)
    inp("wk", [4, 128, 16, 128], F32)
    inp("wv", [4, 128, 16, 128], F32)
    inp("wout", [16, 128, D], F32)
    inp("wbaseT", [16, 128, D], F32)
    inp("loraA", [128, 16, 128], F32)
    inp("loraB", [128, D], F32)
    inp("routerT", [128, 16, 8], BF16)
    inp("bq", [128, 16], F32)
    inp("bk", [128, 4], F32)
    inp("bv", [128, 4], F32)
    inp("rbb", [128, 8], F32)
    inp("bout_row", [1, D], BF16)
    inp("bbase_row", [1, D], BF16)

    ikind = "ExternalOutput" if _TAPS else "Internal"
    xn1_dram = nc.dram_tensor("xn1_dram", [P_ALL, D], BF16, kind=ikind).ap()
    qt_dram = nc.dram_tensor("qt_dram", [16, 128, T_OWN], BF16, kind=ikind).ap()
    at_dram = nc.dram_tensor("at_dram", [16, 128, T_OWN], BF16, kind=ikind).ap()
    x2_dram = nc.dram_tensor("x2_dram", [T_OWN, D], F32, kind=ikind).ap()
    xn2_dram = nc.dram_tensor("xn2_dram", [T_OWN, D], BF16, kind=ikind).ap()
    dn_dram = nc.dram_tensor("dn_dram", [NSLOT, 16384], F32, kind=ikind).ap()
    rr_dram = nc.dram_tensor("rr_dram", [NSLOT, 16384], F32, kind="Internal").ap()
    out_dram = nc.dram_tensor("out", [T_OWN, D], F32, kind="ExternalOutput").ap()

    sb = {}
    cursors = {}

    def sba(name, shape, dt, region, reg_start):
        if region not in cursors:
            cursors[region] = reg_start
        esz = 2 if dt == BF16 else 4
        nb = 1
        for s_ in shape[1:]:
            nb *= s_
        nb *= esz
        off = cursors[region]
        cursors[region] = off + ((nb + 31) // 32) * 32
        sb[name] = nc.alloc_sbuf_tensor_at(name, list(shape), dt, offset=off)
        return sb[name]

    K0 = 18432
    A0 = 38912
    B0 = 108544
    C0 = 178176
    LIMIT = 229376

    sba("ident", [128, 128], BF16, "k", K0)
    sba("grot", [128, 128], BF16, "k", K0)
    sba("gpass", [128, 128], BF16, "k", K0)
    sba("ones1", [1, 64], F32, "k", K0)
    sba("onestok", [1, 128], BF16, "k", K0)
    sba("bq", [128, 16], F32, "k", K0)
    sba("bk", [128, 4], F32, "k", K0)
    sba("bv", [128, 4], F32, "k", K0)
    sba("rbb", [128, 8], F32, "k", K0)
    sba("biasmat", [128, 80], F32, "k", K0)
    sba("masks", [128, 4096], BF16, "k", K0)
    sba("trep", [128, 8], F32, "k", K0)
    sba("mvall", [128, 64], F32, "k", K0)
    sba("sqw", [128, 16], F32, "k", K0)
    sba("var32", [128, 32], F32, "k", K0)
    sba("rstd32", [128, 32], F32, "k", K0)
    sba("m32", [128, 32], F32, "k", K0)
    sba("ccs", [64, 2, 512], F32, "k", K0)
    sba("ssc", [64, 2, 512], F32, "k", K0)

    sba("KT", [128, 4 * P_ALL], BF16, "A23", A0)
    sba("VP", [128, 32 * 520], BF16, "A23", A0)
    sba("rrsl", [128, 32], F32, "A23", A0)
    sba("u", [64, 512], F32, "A23", A0)
    sba("vf", [128, 512], BF16, "A23", A0)
    sba("xn2T", [128, 16 * T_OWN], BF16, "A5", A0)
    assert max(cursors["A23"], cursors["A5"]) <= B0

    sba("wkv", [128, 8 * D], BF16, "B2", B0)
    sba("xn1T", [128, 2, 16 * 512], BF16, "B2", B0)
    sba("dnrow", [1, 1024], F32, "B2", B0)
    sba("wout", [128, 16 * D], BF16, "B4", B0)
    sba("wbase", [128, 16 * D], BF16, "B5", B0)
    assert max(cursors["B2"], cursors["B4"], cursors["B5"]) <= C0

    sba("xpt", [128, 2, D], F32, "C1", C0)
    sba("xnt", [128, 2, D], BF16, "C1", C0)
    sba("xsq", [128, D], BF16, "C1", C0)
    sba("mv", [128, 2], F32, "k", None)
    sba("sq1", [128, 1], F32, "k", None)
    sba("stdc", [128, 1], F32, "k", None)
    sba("rstd", [128, 1], F32, "k", None)
    C2S = C0 + 32768
    sba("wqs", [128, 2, 2048], BF16, "C2", C2S)
    sba("wrope", [128, 512], BF16, "C2", C2S)
    sba("qrope", [128, 512], BF16, "C2", C2S)
    sba("qstage", [128, 2, 512], BF16, "C2", C2S)
    sba("PT", [128, 4, 1024], BF16, "C3a", C0)
    C3B = C0 + 8192
    sba("QTp", [128, 2, 512], BF16, "C3b", C3B)
    sba("Anum", [128, 4 * 512], F32, "C3b", C3B)
    sba("AT", [128, 4 * 512], BF16, "C3b", C3B)
    sba("rrow", [128, 2, 1024], F32, "C3b", C3B)
    sba("atg", [128, 2, 16 * 512], BF16, "A4", A0)
    sba("x2t", [128, 2, D], F32, "C4", C0)
    sba("xn2t", [128, 2, D], BF16, "C4", C0)
    sba("xsq4", [128, D], F32, "C4", C0)
    sba("bout_row", [1, D], BF16, "C4", C0)
    sba("xti", [128, 2, D], F32, "C5", C0)
    sba("outt", [128, 2, D], F32, "C5", C0)
    sba("loraA", [128, 16 * 128], BF16, "C5", C0)
    sba("loraB", [128, D], BF16, "C5", C0)
    sba("routerT", [128, 128], BF16, "C5", C0)
    sba("midall", [128, 16 * 128], BF16, "C5", C0)
    sba("midT2", [128, 2, 128], BF16, "C5", C0)
    sba("bbase_row", [1, D], BF16, "C5", C0)
    C5G = C0 + 24576
    sba("lgall", [128, 128], F32, "C5g", C5G)
    sba("gmaskb", [128, 128], F32, "C5g", C5G)
    sba("gateb", [128, 128], F32, "C5g", C5G)
    sba("mx16", [128, 16], F32, "C5g", C5G)
    sba("m2_16", [128, 16], F32, "C5g", C5G)
    sba("gs16", [128, 16], F32, "C5g", C5G)
    sba("rg16", [128, 16], F32, "C5g", C5G)
    assert cursors["C5g"] <= C0 + 32768
    assert cursors["k"] <= A0, cursors["k"]
    for r in ("C1", "C2", "C3b", "C4", "C5", "k", "A23"):
        assert cursors[r] <= LIMIT, (r, cursors[r])

    PSBIG = nc.alloc_psum_tensor("PSBIG", [128, 2048], F32)
    PN_A = nc.alloc_psum_tensor("PN_A", [128, 512], F32)
    PN_B = nc.alloc_psum_tensor("PN_B", [128, 512], F32)
    PM1 = nc.alloc_psum_tensor("PM1", [128, 512], F32)
    PM2 = nc.alloc_psum_tensor("PM2", [128, 512], F32)
    PM1b = PM1[:, :].bitcast(BF16)
    PM2b = PM2[:, :].bitcast(BF16)
    PN_Ab = PN_A[:, :].bitcast(BF16)
    PN_Bb = PN_B[:, :].bitcast(BF16)

    for name in ["ident", "grot", "gpass", "ones1", "onestok", "bq", "bk", "bv", "rbb",
                 "biasmat", "masks"]:
        p.emit("sync", lambda eng, n=name: eng.dma_start(out=sb[n][:, :], in_=di[n]),
               inc="qsp", n=16)
    ld0 = ("qsp", p.counts["qsp"])
    p.emit("vector", lambda eng: eng.memset(sb["trep"][:, 2:8], -1.0), inc="dve")
    p.emit("vector", lambda eng: eng.memset(sb["qrope"][0:64, :], 0.0), inc="dve")
    ev_vones = p.emit("vector", lambda eng: eng.memset(
        sb["VP"][:, :].rearrange("p (k g c) -> p k g c", k=32, g=8)[:, :, :, 64:65], 1.0),
        inc="dve")

    NT1 = P_ALL // 128
    ev_spill = [None] * NT1
    ev_sqr = [None] * (NT1 + 16)
    ev_xn = [None] * NT1

    def emit_p1a_tile(i):
        bf = i % 2
        evl = p.emit("sync", lambda eng, i=i, bf=bf: eng.dma_start(
            out=sb["xpt"][:, bf], in_=di["xp"][i * 128:(i + 1) * 128, :]),
            waits=[ev_sqr[i - 2]] if i % 8 >= 2 else [ev_xn[i - 2]], inc="qsp", n=16)
        ev_q = p.emit("scalar", lambda eng, bf=bf: eng.activation(
            sb["xsq"][:, :], sb["xpt"][:, bf], AF.Square),
            waits=[evl, ev_sqr[i - 1] if i >= 1 else None], inc="act", nochain=True)
        p.emit("vector", lambda eng, i=i, bf=bf: eng.reduce_sum(
            sb["mvall"][:, 2 * (i % 16):2 * (i % 16) + 1], sb["xpt"][:, bf], AX.X),
            waits=[evl], inc="dve", nochain=True)
        ev_sqr[i] = p.emit("vector", lambda eng, i=i: eng.reduce_sum(
            sb["mvall"][:, 2 * (i % 16) + 1:2 * (i % 16) + 2], sb["xsq"][:, :], AX.X),
            waits=[ev_q], inc="dve", nochain=True)

    def emit_p1b_wave(w):
        c0 = 16 * (w % 2)
        ev_o1 = p.emit("vector", lambda eng, c0=c0: eng.scalar_tensor_tensor(
            sb["sqw"][:, 0:8], sb["mvall"][:, c0:c0 + 16:2],
            -1.0 / (2048.0 * 2048.0), sb["mvall"][:, c0:c0 + 16:2],
            ALU.mult, ALU.mult), waits=[ev_sqr[8 * w + 7]], inc="dve")
        ev_o2 = p.emit("vector", lambda eng, w=w, c0=c0: eng.scalar_tensor_tensor(
            sb["var32"][:, 8 * w:8 * w + 8], sb["mvall"][:, c0 + 1:c0 + 16:2],
            1.0 / 2048.0, sb["sqw"][:, 0:8], ALU.mult, ALU.add), inc="dve")
        ev_o3 = p.emit("vector", lambda eng, w=w: eng.tensor_scalar(
            sb["var32"][:, 8 * w:8 * w + 8], sb["var32"][:, 8 * w:8 * w + 8],
            1e-5, None, ALU.add), inc="dve")
        ev_ln = p.emit("scalar", lambda eng, w=w: eng.activation(
            sb["var32"][:, 8 * w:8 * w + 8], sb["var32"][:, 8 * w:8 * w + 8],
            AF.Ln), waits=[ev_o3], inc="act")
        ev_ex = p.emit("scalar", lambda eng, w=w: eng.activation(
            sb["rstd32"][:, 8 * w:8 * w + 8], sb["var32"][:, 8 * w:8 * w + 8],
            AF.Exp, scale=-0.5), inc="act")
        ev_m = p.emit("vector", lambda eng, w=w, c0=c0: eng.tensor_scalar(
            sb["m32"][:, 8 * w:8 * w + 8], sb["mvall"][:, c0:c0 + 16:2],
            1.0 / 2048.0, None, ALU.mult), waits=[ev_ex], inc="dve")
        return ev_m

    def emit_p1c_tile(i, ev_w):
        bf = i % 2
        evl = p.emit("sync", lambda eng, i=i, bf=bf: eng.dma_start(
            out=sb["xpt"][:, bf], in_=di["xp"][i * 128:(i + 1) * 128, :]),
            waits=[ev_spill[i - 2], ev_sqr[(i // 8) * 8 + 7]], inc="qsp", n=16)
        ev_xn[i] = p.emit("vector", lambda eng, i=i, bf=bf: eng.tensor_scalar(
            sb["xnt"][:, bf], sb["xpt"][:, bf], sb["m32"][:, i:i + 1],
            sb["rstd32"][:, i:i + 1], ALU.subtract, ALU.mult),
            waits=[evl, ev_w, ev_spill[i - 2]], inc="dve", nochain=True)
        ev_spill[i] = p.emit("sync", lambda eng, i=i, bf=bf: eng.dma_start(
            out=xn1_dram[i * 128:(i + 1) * 128, :], in_=sb["xnt"][:, bf]),
            waits=[ev_xn[i]], inc="qsp", n=16)

    for t in range(4):
        p.emit("gpsimd", lambda eng, t=t: eng.dma_start(
            out=sb["wkv"][:, t * 2048:(t + 1) * 2048],
            in_=di["wk"][t].rearrange("p c j -> p (c j)")), inc="qgp", n=16)
        p.emit("gpsimd", lambda eng, t=t: eng.dma_start(
            out=sb["wkv"][:, (4 + t) * 2048:(5 + t) * 2048],
            in_=di["wv"][t].rearrange("p c j -> p (c j)")), inc="qgp", n=16)
    ev_wkv = ("qgp", p.counts["qgp"])

    NG = P_ALL // 512
    ev_kv_ready = [None] * NG
    ev_xn1t = [None] * NG
    ev_xn1t_done = [None] * NG
    ev_cs = [None] * NG
    ev_cs_done = [None] * NG
    ev_qspill = {}
    qload = {}
    qdone = {}
    prev_perm = None
    prev_vtr = None
    ps_free = {"PN_A": None, "PN_B": None, "PM1": None, "PM2": None}

    def emit_qload(qi):
        g, t = divmod(qi, 16)
        w = [qdone.get(qi - 2), ev_wkv if qi == 0 else None]
        qload[qi] = p.emit("gpsimd", lambda eng, t=t: eng.dma_start(
            out=sb["wqs"][:, t % 2],
            in_=di["wq"][t].rearrange("p c j -> p (c j)")),
            waits=w, inc="qgp", n=16)

    emit_qload(0)
    emit_qload(1)

    def emit_p2_group(g):
        nonlocal prev_vtr
        own = g < 4
        bf = g % 2
        wls = [ev_spill[4 * g + j] for j in range(4)]
        if g >= 2:
            wls.append(ev_xn1t_done[g - 2])
        for c in range(16):
            ev_xn1t[g] = p.emit("scalar", lambda eng, g=g, c=c, bf=bf: eng.dma_start_transpose(
                sb["xn1T"][:, bf][:, c * 512:(c + 1) * 512],
                xn1_dram[g * 512:(g + 1) * 512, c * 128:(c + 1) * 128]),
                waits=wls if c == 0 else [], inc="qsc", n=16)
        csw = [ev_cs_done[g - 2]] if g >= 2 else []
        p.emit("sync", lambda eng, g=g, bf=bf: eng.dma_start(
            out=sb["ccs"][:, bf], in_=di["ccss"][:, g * 512:(g + 1) * 512]),
            waits=csw, inc="qsp", n=16)
        ev_cs[g] = p.emit("sync", lambda eng, g=g, bf=bf: eng.dma_start(
            out=sb["ssc"][:, bf], in_=di["sscc"][:, g * 512:(g + 1) * 512]),
            inc="qsp", n=16)

        def proj_mms(psum_ap, w_fn, fw):
            ev = None
            for c in range(16):
                ev = p.emit("tensor", lambda eng, c=c, psum_ap=psum_ap, w_fn=w_fn, bf=bf: eng.matmul(
                    psum_ap, w_fn(c), sb["xn1T"][:, bf][:, c * 512:(c + 1) * 512],
                    start=(c == 0), stop=(c == 15)),
                    waits=list(fw) if c == 0 else [], inc="pe")
            return ev

        def rope_evac(psrc, bias_col, dst_copy_fn, extra_pm_waits, ev_mm):
            nonlocal prev_perm
            ev_u = p.emit("vector", lambda eng, psrc=psrc, bias_col=bias_col: eng.tensor_scalar(
                sb["u"][:, :], psrc[0:64, :], bias_col[0:64, :], None, ALU.add),
                waits=[ev_cs[g], prev_perm, ev_mm], inc="dve")
            ev_pass = p.emit("vector", lambda eng, psrc=psrc, bias_col=bias_col: eng.tensor_scalar(
                sb["qrope"][64:128, :], psrc[64:128, :], bias_col[64:128, :], None, ALU.add),
                inc="dve")
            p.emit("vector", lambda eng, bf=bf: eng.tensor_mul(
                sb["wrope"][0:64, :], sb["u"][:, :], sb["ccs"][:, bf]), inc="dve")
            ev_b = p.emit("vector", lambda eng, bf=bf: eng.tensor_mul(
                sb["wrope"][64:128, :], sb["u"][:, :], sb["ssc"][:, bf]), inc="dve")
            ev_cs_done[g] = ev_b
            p.emit("tensor", lambda eng: eng.matmul(
                PM2[:, :], sb["grot"][:, :], sb["wrope"][:, :], start=True, stop=False),
                waits=[ev_b, ps_free["PM2"]] + list(extra_pm_waits), inc="pe")
            ev_pm = p.emit("tensor", lambda eng: eng.matmul(
                PM2[:, :], sb["gpass"][:, :], sb["qrope"][:, :], start=False, stop=True),
                inc="pe")
            prev_perm = ev_pm
            ev_cp = dst_copy_fn(ev_pm)
            ps_free["PM2"] = ev_cp
            return ev_pass, ev_cp

        for kt in range(4):
            fw = [ev_wkv, ev_xn1t[g], ps_free["PN_A"]]
            if g == 0 and kt == 0:
                fw.append(ld0)
            ev_mm = proj_mms(PN_A[:, :],
                             lambda c, kt=kt: sb["wkv"][:, kt * 2048 + c * 128: kt * 2048 + (c + 1) * 128],
                             fw)
            def kcopy(ev_pm, kt=kt, g=g):
                return p.emit("vector", lambda eng, kt=kt, g=g: eng.tensor_copy(
                    sb["KT"][:, kt * P_ALL + g * 512: kt * P_ALL + (g + 1) * 512], PM2[:, :]),
                    waits=[ev_pm], inc="dve")
            ev_pass, _ = rope_evac(PN_A, sb["bk"][:, kt:kt + 1], kcopy, [], ev_mm)
            ps_free["PN_A"] = ev_pass

        for kt in range(4):
            fw = [ev_wkv, ev_xn1t[g], ps_free["PN_B"]]
            ev_mm = proj_mms(PN_B[:, :],
                             lambda c, kt=kt: sb["wkv"][:, (4 + kt) * 2048 + c * 128: (4 + kt) * 2048 + (c + 1) * 128],
                             fw)
            ev_vf = p.emit("vector", lambda eng, kt=kt: eng.tensor_scalar(
                sb["vf"][:, :], PN_B[:, :], sb["bv"][:, kt:kt + 1], None, ALU.add),
                waits=[ev_mm, ev_vones, prev_vtr], inc="dve")
            ps_free["PN_B"] = ev_vf
            for j in range(4):
                ktile = g * 4 + j
                ev_tr = p.emit("tensor", lambda eng, j=j: eng.transpose(
                    PM1b[:, 0:128], sb["vf"][:, j * 128:(j + 1) * 128], sb["ident"][:, :]),
                    waits=[ev_vf, ps_free["PM1"]], inc="pe")
                prev_vtr = ev_tr
                e2 = p.emit("vector", lambda eng, ktile=ktile, kt=kt: eng.tensor_copy(
                    sb["VP"][:, ktile * 520 + (2 * kt) * 65: ktile * 520 + (2 * kt) * 65 + 130]
                    .rearrange("q (h c) -> q h c", h=2)[:, :, 0:64],
                    PM1b[:, 0:128].rearrange("q (h c) -> q h c", h=2)),
                    waits=[ev_tr], inc="dve")
                ps_free["PM1"] = e2

        if own:
            for t in range(16):
                qi = g * 16 + t
                fw = [qload[qi], ev_xn1t[g], ps_free["PM1"]]
                ev_mm = None
                for c in range(16):
                    ev_mm = p.emit("tensor", lambda eng, c=c, t=t, bf=bf: eng.matmul(
                        PM1[:, :], sb["wqs"][:, t % 2][:, c * 128:(c + 1) * 128],
                        sb["xn1T"][:, bf][:, c * 512:(c + 1) * 512],
                        start=(c == 0), stop=(c == 15)),
                        waits=list(fw) if c == 0 else [], inc="pe")
                qdone[qi] = ev_mm

                def qcopy(ev_pm, g=g, t=t, qi=qi):
                    ev_cp = p.emit("vector", lambda eng, t=t: eng.tensor_copy(
                        sb["qstage"][:, t % 2], PM2[:, :]),
                        waits=[ev_pm, ev_qspill.get(qi - 2)], inc="dve")
                    ev_qspill[qi] = p.emit("sync", lambda eng, t=t, g=g: eng.dma_start(
                        out=qt_dram[t][:, g * 512:(g + 1) * 512], in_=sb["qstage"][:, t % 2]),
                        waits=[ev_cp], inc="qsp", n=16)
                    return ev_cp

                ev_pass, _ = rope_evac(PM1, sb["bq"][:, t:t + 1], qcopy, [], ev_mm)
                ps_free["PM1"] = ev_pass
                if qi + 2 < 64:
                    emit_qload(qi + 2)
        ev_xn1t_done[g] = p.last_ev["tensor"]
        ev_kv_ready[g] = p.last_ev["vector"]

    slot_base = [0, 8, 24, 48]
    ev_exp_hist = []
    ev_av_hist = []
    prev_evac = {0: None, 1: None}
    prev_nm_w = [None] * 4
    prev_at_w = [None] * 4
    prev_at_spill = [None] * 16
    prev_dn = None
    ev_rr_prev = [None, None]
    qt_load = {}
    qt_done = {}

    def emit_qtload(s, t):
        sti = s * 16 + t
        w = [qt_done.get(sti - 2), ev_qspill[s * 16 + t]]
        if sti == 0:
            w.append(ev_spill[NT1 - 1])
        qt_load[sti] = p.emit("sync", lambda eng, s=s, t=t: eng.dma_start(
            out=sb["QTp"][:, t % 2], in_=qt_dram[t][:, s * 512:(s + 1) * 512]),
            waits=w, inc="qsp", n=16)

    def emit_p3_slot(s):
        nonlocal prev_dn
        Ks = KS[s]
        if s == 0:
            emit_qtload(0, 0)
            emit_qtload(0, 1)
        ev_evacB = [None] * 16
        ev_dn_ev = [None] * 16
        for t in range(16):
            sti = s * 16 + t
            kt = t // 4
            ev_exp_u = {}
            ev_dep_u = {}
            ev_av_u = {}

            NB_A = PN_A if t % 2 == 0 else PM1
            NB_B = PN_B if t % 2 == 0 else PM2

            def emit_av(pp, NB_A=NB_A, NB_B=NB_B):
                nown = 4 * (s + 1)
                kb = pp if pp < nown else 16 + (pp - nown)
                bufP = pp % 4
                avw = [ev_dep_u[pp]]
                if pp == 0:
                    avw.append(prev_evac[t % 2])
                p.emit("tensor", lambda eng, kb=kb, kt=kt, bufP=bufP, pp=pp, Ks=Ks: eng.matmul(
                    NB_A[0:65, :],
                    sb["VP"][:, kb * 520 + (2 * kt) * 65: kb * 520 + (2 * kt) * 65 + 65],
                    sb["PT"][:, bufP][:, 0:512], start=(pp == 0), stop=(pp == Ks - 1)),
                    waits=avw, inc="pe")
                ev_av_u[pp] = p.emit("tensor", lambda eng, kb=kb, kt=kt, bufP=bufP, pp=pp, Ks=Ks: eng.matmul(
                    NB_B[0:65, :],
                    sb["VP"][:, kb * 520 + (2 * kt + 1) * 65: kb * 520 + (2 * kt + 1) * 65 + 65],
                    sb["PT"][:, bufP][:, 512:1024], start=(pp == 0), stop=(pp == Ks - 1)),
                    inc="pe")

            for pp in range(Ks):
                nown = 4 * (s + 1)
                kb = pp if pp < nown else 16 + (pp - nown)
                diag = (4 * s <= pp < nown)
                mj = pp - 4 * s if diag else 0
                bufS = pp % 2
                bufP = pp % 4
                w = [ev_kv_ready[kb // 4]]
                if pp == 0:
                    w.append(qt_load[sti])
                if pp >= 2:
                    w.append(ev_exp_u[pp - 2])
                elif len(ev_exp_hist) >= 2:
                    w.append(ev_exp_hist[-2])
                p.emit("tensor", lambda eng, kt=kt, kb=kb, bufS=bufS, t=t: eng.matmul(
                    PSBIG[:, bufS * 1024: bufS * 1024 + 512],
                    sb["KT"][0:64, kt * P_ALL + kb * 128: kt * P_ALL + (kb + 1) * 128],
                    sb["QTp"][0:64, t % 2], start=True, stop=True),
                    waits=w, inc="pe")
                ev_s2 = p.emit("tensor", lambda eng, kt=kt, kb=kb, bufS=bufS, t=t: eng.matmul(
                    PSBIG[:, bufS * 1024 + 512: bufS * 1024 + 1024],
                    sb["KT"][64:128, kt * P_ALL + kb * 128: kt * P_ALL + (kb + 1) * 128],
                    sb["QTp"][64:128, t % 2], start=True, stop=True), inc="pe")
                ew = [ev_s2]
                if pp >= 4:
                    ew.append(ev_av_u[pp - 4])
                elif ev_av_hist:
                    ew.append(ev_av_hist[-1])
                if not ev_exp_hist:
                    ew.append(ld0)
                    ew.append(ev_spill[NT1 - 1])
                ev_exp = p.emit("scalar", lambda eng, s=s, pp=pp, bufS=bufS, bufP=bufP: eng.activation(
                    sb["PT"][:, bufP], PSBIG[:, bufS * 1024:(bufS + 1) * 1024], AF.Exp,
                    bias=sb["biasmat"][:, slot_base[s] + pp: slot_base[s] + pp + 1],
                    scale=0.125), waits=ew, inc="act", nochain=True)
                ev_exp_hist.append(ev_exp)
                ev_exp_u[pp] = ev_exp
                dep = ev_exp
                if diag:
                    dep = p.emit("vector", lambda eng, mj=mj, bufP=bufP: eng.tensor_mul(
                        sb["PT"][:, bufP], sb["PT"][:, bufP],
                        sb["masks"][:, mj * 1024:(mj + 1) * 1024]),
                        waits=[ev_exp], inc="dve")
                ev_dep_u[pp] = dep
                if pp >= 1:
                    emit_av(pp - 1)
            emit_av(Ks - 1)
            ev_av_hist.append(ev_av_u[Ks - 1])
            qt_done[sti] = ev_av_u[Ks - 1]
            if sti + 2 < 64:
                emit_qtload((sti + 2) // 16, (sti + 2) % 16)
            j = t % 4
            p.emit("vector", lambda eng, j=j, NB_A=NB_A: eng.tensor_copy(
                sb["Anum"][0:64, j * 512:(j + 1) * 512], NB_A[0:64, :]),
                waits=[ev_av_hist[-1], prev_nm_w[j]], inc="dve")
            p.emit("vector", lambda eng, j=j, NB_B=NB_B: eng.tensor_copy(
                sb["Anum"][64:128, j * 512:(j + 1) * 512], NB_B[0:64, :]), inc="dve")
            p.emit("vector", lambda eng, NB_A=NB_A: eng.tensor_copy(
                sb["dnrow"][0:1, 0:512], NB_A[64:65, :]), waits=[prev_dn], inc="dve")
            ev_d2 = p.emit("vector", lambda eng, NB_B=NB_B: eng.tensor_copy(
                sb["dnrow"][0:1, 512:1024], NB_B[64:65, :]), inc="dve")
            ev_evacB[t] = ev_d2
            prev_evac[t % 2] = ev_d2
            ev_dn_ev[t] = p.emit("sync", lambda eng, s=s, t=t: eng.dma_start(
                out=dn_dram[s:s + 1, t * 1024:(t + 1) * 1024], in_=sb["dnrow"][0:1, :]),
                waits=[ev_d2], inc="qsp", n=16)
            prev_dn = ev_dn_ev[t]
            if t % 4 == 3:
                t0 = t - 3
                ev_ld = p.emit("sync", lambda eng, s=s, t0=t0: eng.dma_start(
                    out=sb["rrsl"][:, :],
                    in_=dn_dram[s, t0 * 1024:(t0 + 4) * 1024].rearrange("(p f) -> p f", p=128)),
                    waits=[ev_dn_ev[t]], inc="qsp", n=16)
                ev_rc = p.emit("vector", lambda eng: eng.reciprocal(
                    sb["rrsl"][:, :], sb["rrsl"][:, :]), waits=[ev_ld], inc="dve")
                ev_st = p.emit("sync", lambda eng, s=s, t0=t0: eng.dma_start(
                    out=rr_dram[s, t0 * 1024:(t0 + 4) * 1024].rearrange("(p f) -> p f", p=128),
                    in_=sb["rrsl"][:, :]), waits=[ev_rc], inc="qsp", n=16)
                for u in range(t0, t0 + 4):
                    j2 = u % 4
                    ev_rr = p.emit("sync", lambda eng, s=s, u=u: eng.dma_start(
                        out=sb["rrow"][:, u % 2],
                        in_=rr_dram[s, u * 1024:(u + 1) * 1024].unsqueeze(0).broadcast_to([128, 1024])),
                        waits=[ev_st, ev_rr_prev[u % 2]], inc="qsp", n=16)
                    p.emit("vector", lambda eng, j2=j2, u=u: eng.tensor_mul(
                        sb["AT"][0:64, j2 * 512:(j2 + 1) * 512],
                        sb["Anum"][0:64, j2 * 512:(j2 + 1) * 512],
                        sb["rrow"][0:64, u % 2][:, 0:512]),
                        waits=[ev_rr, prev_at_w[j2]], inc="dve")
                    ev_nm = p.emit("vector", lambda eng, j2=j2, u=u: eng.tensor_mul(
                        sb["AT"][64:128, j2 * 512:(j2 + 1) * 512],
                        sb["Anum"][64:128, j2 * 512:(j2 + 1) * 512],
                        sb["rrow"][64:128, u % 2][:, 512:1024]), inc="dve")
                    ev_rr_prev[u % 2] = ev_nm
                    prev_nm_w[j2] = ev_nm
                    evsp = p.emit("sync", lambda eng, s=s, u=u, j2=j2: eng.dma_start(
                        out=at_dram[u][:, s * 512:(s + 1) * 512],
                        in_=sb["AT"][:, j2 * 512:(j2 + 1) * 512]),
                        waits=[ev_nm], inc="qsp", n=16)
                    prev_at_w[j2] = evsp
                    prev_at_spill[u] = evsp

    for w in range(4):
        for i in range(8 * w, 8 * w + 8):
            emit_p1a_tile(i)
        ev_w = emit_p1b_wave(w)
        for i in range(8 * w, 8 * w + 8):
            emit_p1c_tile(i, ev_w)
        emit_p2_group(2 * w)
        emit_p2_group(2 * w + 1)
    emit_p3_slot(0)
    emit_p3_slot(1)
    emit_p3_slot(2)
    emit_p3_slot(3)
    p.barrier()

    for c in range(16):
        p.emit("gpsimd", lambda eng, c=c: eng.dma_start(
            out=sb["wout"][:, c * 2048:(c + 1) * 2048], in_=di["wout"][c]), inc="qgp", n=16)
    p.emit("gpsimd", lambda eng: eng.dma_start(
        out=sb["bout_row"][:, :], in_=di["bout_row"]), inc="qgp", n=16)
    ev_wout = ("qgp", p.counts["qgp"])
    ev_lorab = p.emit("gpsimd", lambda eng: eng.dma_start(
        out=sb["loraB"][:, :], in_=di["loraB"]), inc="qgp", n=16)
    ev_rt = p.emit("gpsimd", lambda eng: eng.dma_start(
        out=sb["routerT"][:, :], in_=di["routerT"].rearrange("p c j -> p (c j)")),
        inc="qgp", n=16)
    ev_w5s = p.emit("gpsimd", lambda eng: eng.dma_start(
        out=sb["bbase_row"][:, :], in_=di["bbase_row"]), inc="qgp", n=16)

    ev_gdone = [None] * 4
    ev_gload = [None] * 4
    ev_x2sp = [None] * 16
    ev_r4 = [None] * 16
    ev_s4 = [None] * 16
    prev_psbig = None
    prev_sqr4 = None
    prev_tile = {}
    for gg in range(4):
        bfg = gg % 2
        w = list(prev_at_spill) if gg == 0 else []
        if gg >= 2:
            w.append(ev_gdone[gg - 2])
        ev_at = None
        for c in range(16):
            ev_at = p.emit("sync", lambda eng, gg=gg, c=c, bfg=bfg: eng.dma_start(
                out=sb["atg"][:, bfg][:, c * 512:(c + 1) * 512],
                in_=at_dram[c][:, gg * 512:(gg + 1) * 512]),
                waits=w if c == 0 else [], inc="qsp", n=16)
        ev_gload[gg] = ev_at
        last_mm = None
        for ti in range(4):
            tix = gg * 4 + ti
            row0 = tix * 128
            bft = tix % 2
            ev_x = p.emit("sync", lambda eng, row0=row0, bft=bft: eng.dma_start(
                out=sb["x2t"][:, bft], in_=di["xp"][row0:row0 + 128, :]),
                waits=[prev_tile.get(("sp", bft)), prev_tile.get(("rd", bft))],
                inc="qsp", n=16)
            ev_mm = None
            for c in range(16):
                wmm = []
                if c == 0:
                    wmm = [ev_gload[gg], ev_wout, prev_psbig]
                for nn in range(4):
                    ev_mm = p.emit("tensor", lambda eng, c=c, nn=nn, ti=ti, bfg=bfg: eng.matmul(
                        PSBIG[:, nn * 512:(nn + 1) * 512],
                        sb["atg"][:, bfg][:, c * 512 + ti * 128: c * 512 + (ti + 1) * 128],
                        sb["wout"][:, c * 2048 + nn * 512: c * 2048 + (nn + 1) * 512],
                        start=(c == 0), stop=False),
                        waits=wmm if nn == 0 else [], inc="pe")
            for nn in range(4):
                ev_mm = p.emit("tensor", lambda eng, nn=nn: eng.matmul(
                    PSBIG[:, nn * 512:(nn + 1) * 512], sb["onestok"][0:1, :],
                    sb["bout_row"][0:1, nn * 512:(nn + 1) * 512],
                    start=False, stop=True), inc="pe")
            last_mm = ev_mm
            evs_x2 = []
            for nn in range(4):
                w2 = [ev_mm, ev_x] if nn == 0 else []
                evs_x2.append(p.emit("vector", lambda eng, nn=nn, bft=bft: eng.tensor_add(
                    sb["x2t"][:, bft][:, nn * 512:(nn + 1) * 512],
                    PSBIG[:, nn * 512:(nn + 1) * 512],
                    sb["x2t"][:, bft][:, nn * 512:(nn + 1) * 512]), waits=w2, inc="dve"))
            prev_psbig = evs_x2[3]
            ev_q4 = p.emit("scalar", lambda eng, bft=bft: eng.activation(
                sb["xsq4"][:, :], sb["x2t"][:, bft], AF.Square),
                waits=[evs_x2[3], prev_sqr4], inc="act")
            ev_r4[tix] = p.emit("vector", lambda eng, bft=bft, tix=tix: eng.reduce_sum(
                sb["mvall"][:, 2 * tix:2 * tix + 1], sb["x2t"][:, bft], AX.X),
                waits=[evs_x2[3]], inc="dve", nochain=True)
            ev_s4[tix] = p.emit("vector", lambda eng, tix=tix: eng.reduce_sum(
                sb["mvall"][:, 2 * tix + 1:2 * tix + 2], sb["xsq4"][:, :], AX.X),
                waits=[ev_q4], inc="dve", nochain=True)
            prev_sqr4 = ev_s4[tix]
            ev_x2sp[tix] = p.emit("sync", lambda eng, row0=row0, bft=bft: eng.dma_start(
                out=x2_dram[row0:row0 + 128, :], in_=sb["x2t"][:, bft]),
                waits=[evs_x2[3]], inc="qsp", n=16)
            prev_tile[("sp", bft)] = ev_x2sp[tix]
            prev_tile[("rd", bft)] = ev_r4[tix]
        ev_gdone[gg] = last_mm
    last_mm15 = ev_gdone[3]
    ev_la = p.emit("gpsimd", lambda eng: eng.dma_start(
        out=sb["loraA"][:, :], in_=di["loraA"].rearrange("p c j -> p (c j)")),
        waits=[last_mm15], inc="qgp", n=16)
    ev_wb = [None] * 16
    for c in range(16):
        ev_wb[c] = p.emit("gpsimd", lambda eng, c=c: eng.dma_start(
            out=sb["wbase"][:, c * 2048:(c + 1) * 2048], in_=di["wbaseT"][c]),
            inc="qgp", n=16)

    p.emit("vector", lambda eng: eng.scalar_tensor_tensor(
        sb["sqw"][:, 0:16], sb["mvall"][:, 0:32:2],
        -1.0 / (2048.0 * 2048.0), sb["mvall"][:, 0:32:2],
        ALU.mult, ALU.mult), waits=[ev_s4[15]], inc="dve")
    p.emit("vector", lambda eng: eng.scalar_tensor_tensor(
        sb["var32"][:, 0:16], sb["mvall"][:, 1:32:2],
        1.0 / 2048.0, sb["sqw"][:, 0:16], ALU.mult, ALU.add), inc="dve")
    ev_o3 = p.emit("vector", lambda eng: eng.tensor_scalar(
        sb["var32"][:, 0:16], sb["var32"][:, 0:16], 1e-5, None, ALU.add), inc="dve")
    ev_ln4 = p.emit("scalar", lambda eng: eng.activation(
        sb["var32"][:, 0:16], sb["var32"][:, 0:16], AF.Ln), waits=[ev_o3], inc="act")
    ev_ex4 = p.emit("scalar", lambda eng: eng.activation(
        sb["rstd32"][:, 0:16], sb["var32"][:, 0:16], AF.Exp, scale=-0.5), inc="act")
    ev_m4 = p.emit("vector", lambda eng: eng.tensor_scalar(
        sb["m32"][:, 0:16], sb["mvall"][:, 0:32:2], 1.0 / 2048.0, None, ALU.mult),
        waits=[ev_ex4], inc="dve")

    ev_ld4 = [None] * 16
    ev_ts2 = [None] * 16
    ev_tr2 = [None] * 16
    ev_ev2 = [None] * 16
    ev_mme = [None] * 16
    ev_lge = [None] * 16
    for i in range(16):
        bf = i % 2
        enq, qn = ("sync", "qsp") if bf == 0 else ("scalar", "qsc")
        wl = [ev_x2sp[i]]
        if i >= 2:
            wl.append(ev_ts2[i - 2])
        else:
            wl += [ev_r4[14 + i], ev_x2sp[14 + i]]
        ev_ld4[i] = p.emit(enq, lambda eng, i=i, bf=bf: eng.dma_start(
            out=sb["x2t"][:, bf], in_=x2_dram[i * 128:(i + 1) * 128, :]),
            waits=wl, inc=qn, n=16)
        ev_ts2[i] = p.emit("vector", lambda eng, i=i, bf=bf: eng.tensor_scalar(
            sb["xn2t"][:, bf], sb["x2t"][:, bf], sb["m32"][:, i:i + 1],
            sb["rstd32"][:, i:i + 1], ALU.subtract, ALU.mult),
            waits=[ev_ld4[i], ev_m4, ev_tr2[i - 2] if i >= 2 else None],
            inc="dve", nochain=True)
        bks = (PM1b, PM2b) if bf == 0 else (PN_Ab, PN_Bb)
        ev_tr7 = None
        evt = None
        for c in range(16):
            wtr = []
            if c == 0:
                wtr = [ev_ts2[i], ev_ev2[i - 2] if i >= 2 else None]
            evt = p.emit("tensor", lambda eng, c=c, bf=bf, bks=bks: eng.transpose(
                bks[c // 8][:, (c % 8) * 128:(c % 8 + 1) * 128],
                sb["xn2t"][:, bf][:, c * 128:(c + 1) * 128], sb["ident"][:, :]),
                waits=wtr, inc="pe")
            if c == 7:
                ev_tr7 = evt
        ev_tr2[i] = evt
        p.emit("scalar", lambda eng, i=i, bks=bks: eng.activation(
            sb["xn2T"][:, i * 2048:i * 2048 + 1024], bks[0][:, 0:1024], AF.Copy),
            waits=[ev_tr7], inc="act")
        ev_ev2[i] = p.emit("scalar", lambda eng, i=i, bks=bks: eng.activation(
            sb["xn2T"][:, i * 2048 + 1024:(i + 1) * 2048], bks[1][:, 0:1024], AF.Copy),
            waits=[ev_tr2[i]], inc="act")
        mcol = 0 if bf == 0 else 1024
        lcol = 512 if bf == 0 else 1536
        evm = None
        evlg = None
        for c in range(16):
            wm = []
            if c == 0:
                wm = [ev_ev2[i], ev_la,
                      ev_mme[i - 2] if i >= 2 else prev_psbig]
            evm = p.emit("tensor", lambda eng, c=c, i=i, mcol=mcol: eng.matmul(
                PSBIG[:, mcol:mcol + 128],
                sb["xn2T"][:, i * 2048 + c * 128:i * 2048 + (c + 1) * 128],
                sb["loraA"][:, c * 128:(c + 1) * 128],
                start=(c == 0), stop=(c == 15)),
                waits=wm, inc="pe")
            wl2 = []
            if c == 0:
                wl2 = [ev_rt, ev_lge[i - 2] if i >= 2 else None]
            evlg = p.emit("tensor", lambda eng, c=c, i=i, lcol=lcol: eng.matmul(
                PSBIG[:, lcol:lcol + 8],
                sb["xn2T"][:, i * 2048 + c * 128:i * 2048 + (c + 1) * 128],
                sb["routerT"][:, c * 8:(c + 1) * 8],
                start=(c == 0), stop=(c == 15)),
                waits=wl2, inc="pe")
        ev_mme[i] = p.emit("vector", lambda eng, i=i, mcol=mcol: eng.tensor_copy(
            sb["midall"][:, i * 128:(i + 1) * 128], PSBIG[:, mcol:mcol + 128]),
            waits=[evm], inc="dve", nochain=True)
        ev_lge[i] = p.emit("vector", lambda eng, i=i, lcol=lcol: eng.tensor_add(
            sb["lgall"][:, i * 8:(i + 1) * 8], PSBIG[:, lcol:lcol + 8], sb["rbb"][:, :]),
            waits=[evlg], inc="dve", nochain=True)
    def v16(ap):
        return ap.rearrange("p (t e) -> p t e", t=16)

    p.emit("vector", lambda eng: eng.reduce_max(
        sb["mx16"][:, :].unsqueeze(2), v16(sb["lgall"][:, :]), AX.X),
        waits=[ev_lge[15]], inc="dve")
    ev_p2s = p.emit("vector", lambda eng: eng.tensor_sub(
        v16(sb["lgall"][:, :]), v16(sb["lgall"][:, :]),
        sb["mx16"][:, :].unsqueeze(2).broadcast_to([128, 16, 8])), inc="dve")
    ev_p2e = p.emit("scalar", lambda eng: eng.activation(
        sb["lgall"][:, :], sb["lgall"][:, :], AF.Exp), waits=[ev_p2s], inc="act")
    p.emit("vector", lambda eng: eng.reduce_max(
        sb["mx16"][:, :].unsqueeze(2), v16(sb["lgall"][:, :]), AX.X),
        waits=[ev_p2e], inc="dve")
    p.emit("vector", lambda eng: eng.tensor_tensor(
        v16(sb["gmaskb"][:, :]), v16(sb["lgall"][:, :]),
        sb["mx16"][:, :].unsqueeze(2).broadcast_to([128, 16, 8]), ALU.is_ge), inc="dve")
    p.emit("vector", lambda eng: eng.scalar_tensor_tensor(
        sb["gateb"][:, :], sb["gmaskb"][:, :], -2.0, sb["lgall"][:, :],
        ALU.mult, ALU.add), inc="dve")
    p.emit("vector", lambda eng: eng.reduce_max(
        sb["m2_16"][:, :].unsqueeze(2), v16(sb["gateb"][:, :]), AX.X), inc="dve")
    p.emit("vector", lambda eng: eng.tensor_tensor(
        v16(sb["gmaskb"][:, :]), v16(sb["lgall"][:, :]),
        sb["m2_16"][:, :].unsqueeze(2).broadcast_to([128, 16, 8]), ALU.is_ge), inc="dve")
    p.emit("vector", lambda eng: eng.tensor_mul(
        sb["gateb"][:, :], sb["lgall"][:, :], sb["gmaskb"][:, :]), inc="dve")
    p.emit("vector", lambda eng: eng.reduce_sum(
        sb["gs16"][:, :].unsqueeze(2), v16(sb["gateb"][:, :]), AX.X), inc="dve")
    p.emit("vector", lambda eng: eng.reciprocal(sb["rg16"][:, :], sb["gs16"][:, :]), inc="dve")
    ev_g2f = p.emit("vector", lambda eng: eng.scalar_tensor_tensor(
        v16(sb["gateb"][:, :]), v16(sb["gateb"][:, :]), 2.0,
        sb["rg16"][:, :].unsqueeze(2).broadcast_to([128, 16, 8]),
        ALU.mult, ALU.mult), inc="dve")
    ev_mg = [None] * 16
    for i2 in range(16):
        ev_mg[i2] = p.emit("vector", lambda eng, i2=i2: eng.tensor_tensor(
            sb["midall"][:, i2 * 128:(i2 + 1) * 128].rearrange("p (e r) -> p e r", e=8),
            sb["midall"][:, i2 * 128:(i2 + 1) * 128].rearrange("p (e r) -> p e r", e=8),
            sb["gateb"][:, i2 * 8:(i2 + 1) * 8].unsqueeze(2).broadcast_to([128, 8, 16]),
            ALU.mult), waits=[ev_g2f], inc="dve", nochain=True)

    ev_store = [None] * 16
    ev_mtc = [None] * 16
    ev_lbm = [None] * 16
    prev_epn = None
    for ti in range(16):
        row0 = ti * 128
        bf = ti % 2
        enq, qn = ("sync", "qsp") if bf == 0 else ("scalar", "qsc")
        wx = [ev_ts2[14 + bf]] if ti < 2 else [ev_epn_map[ti - 2][0]]
        ev_xld = p.emit(enq, lambda eng, row0=row0, bf=bf: eng.dma_start(
            out=sb["xti"][:, bf], in_=x2_dram[row0:row0 + 128, :]),
            waits=wx, inc=qn, n=16)
        evb = None
        for c in range(16):
            for nn in range(4):
                wb_ = []
                if nn == 0:
                    wb_ = [ev_wb[c]]
                if c == 0:
                    wb_.append(ev_lge[15] if ti == 0 else prev_epn[nn])
                evb = p.emit("tensor", lambda eng, c=c, nn=nn, ti=ti: eng.matmul(
                    PSBIG[:, nn * 512:(nn + 1) * 512],
                    sb["xn2T"][:, ti * 2048 + c * 128:ti * 2048 + (c + 1) * 128],
                    sb["wbase"][:, c * 2048 + nn * 512:c * 2048 + (nn + 1) * 512],
                    start=(c == 0), stop=False),
                    waits=wb_, inc="pe")
        for nn in range(4):
            evb = p.emit("tensor", lambda eng, nn=nn: eng.matmul(
                PSBIG[:, nn * 512:(nn + 1) * 512], sb["onestok"][0:1, :],
                sb["bbase_row"][0:1, nn * 512:(nn + 1) * 512],
                start=False, stop=False), waits=[ev_w5s] if nn == 0 else [], inc="pe")
        evtr5 = p.emit("tensor", lambda eng, ti=ti, bf=bf: eng.transpose(
            (PM1b if bf == 0 else PN_Ab)[:, 0:128],
            sb["midall"][:, ti * 128:(ti + 1) * 128], sb["ident"][:, :]),
            waits=[ev_mg[ti], ev_mtc[ti - 2] if ti >= 2 else None], inc="pe")
        ev_mtc[ti] = p.emit("vector", lambda eng, ti=ti, bf=bf: eng.tensor_copy(
            sb["midT2"][:, bf], (PM1b if bf == 0 else PN_Ab)[:, 0:128]),
            waits=[evtr5, ev_lbm[ti - 2] if ti >= 2 else None], inc="dve", nochain=True)
        evlb = None
        for nn in range(4):
            wlb = [ev_mtc[ti], ev_lorab] if nn == 0 else []
            evlb = p.emit("tensor", lambda eng, nn=nn, bf=bf: eng.matmul(
                PSBIG[:, nn * 512:(nn + 1) * 512], sb["midT2"][:, bf],
                sb["loraB"][:, nn * 512:(nn + 1) * 512],
                start=False, stop=True), waits=wlb, inc="pe")
        ev_lbm[ti] = evlb
        ev_epn = []
        for nn in range(4):
            w2 = [evlb, ev_xld] if nn == 0 else []
            if nn == 0 and ti >= 2:
                w2.append(ev_store[ti - 2])
            ev_epn.append(p.emit("vector", lambda eng, nn=nn, bf=bf: eng.tensor_add(
                sb["outt"][:, bf][:, nn * 512:(nn + 1) * 512],
                PSBIG[:, nn * 512:(nn + 1) * 512],
                sb["xti"][:, bf][:, nn * 512:(nn + 1) * 512]), waits=w2, inc="dve"))
        prev_epn = ev_epn
        if ti == 0:
            ev_epn_map = {}
        ev_epn_map[ti] = ev_epn
        ev_store[ti] = p.emit(enq, lambda eng, row0=row0, bf=bf: eng.dma_start(
            out=out_dram[row0:row0 + 128, :], in_=sb["outt"][:, bf]),
            waits=[ev_epn[3]], inc=qn, n=16)

    p.emit("vector", lambda eng: eng.engine_nop(), waits=[ev_store[-1]])
    p.build()
    return nc


def _rope_tables():
    e = np.arange(0, 32, 2, dtype=np.float32) / np.float32(32)
    inv = (np.float32(1.0) / np.float32(10000.0) ** e).astype(np.float32)
    t = np.arange(S, dtype=np.float32)
    fr = np.outer(t, inv).astype(np.float32)
    return np.cos(fr).astype(np.float32), np.sin(fr).astype(np.float32)


def _pair_heads(t):
    kt, i = divmod(t, 4)
    return 8 * kt + i, 8 * kt + 4 + i


def _qcol_perm():
    cols = []
    for t in range(16):
        hA, hB = _pair_heads(t)
        cols += list(range(hA * 64, hA * 64 + 16))
        cols += list(range(hB * 64, hB * 64 + 16))
        cols += list(range(hA * 64 + 16, hA * 64 + 32))
        cols += list(range(hB * 64 + 16, hB * 64 + 32))
        cols += list(range(hA * 64 + 32, hA * 64 + 64))
        cols += list(range(hB * 64 + 32, hB * 64 + 64))
    return np.array(cols)


def _kcol_perm():
    cols = []
    for kt in range(4):
        a, b = 2 * kt, 2 * kt + 1
        cols += list(range(a * 64, a * 64 + 16))
        cols += list(range(b * 64, b * 64 + 16))
        cols += list(range(a * 64 + 16, a * 64 + 32))
        cols += list(range(b * 64 + 16, b * 64 + 32))
        cols += list(range(a * 64 + 32, a * 64 + 64))
        cols += list(range(b * 64 + 32, b * 64 + 64))
    return np.array(cols)


def _grot_gpass():
    G = np.zeros((128, 128), dtype=np.float32)
    P = np.zeros((128, 128), dtype=np.float32)
    for i in range(16):
        G[i, i] = 1.0
        G[32 + i, i] = -1.0
        G[64 + i, 16 + i] = 1.0
        G[96 + i, 16 + i] = 1.0
        G[16 + i, 64 + i] = 1.0
        G[48 + i, 64 + i] = -1.0
        G[80 + i, 80 + i] = 1.0
        G[112 + i, 80 + i] = 1.0
    for i in range(32):
        P[64 + i, 32 + i] = 1.0
        P[96 + i, 96 + i] = 1.0
    return G.astype(BF), P.astype(BF)


def _wout_row_perm():
    rows = np.zeros(2048, dtype=np.int64)
    for t in range(16):
        hA, hB = _pair_heads(t)
        rows[t * 128: t * 128 + 64] = np.arange(hA * 64, hA * 64 + 64)
        rows[t * 128 + 64: t * 128 + 128] = np.arange(hB * 64, hB * 64 + 64)
    return rows


def _tile4(w):
    d, n = w.shape
    nt = n // 128
    r = w.reshape(16, 128, nt, 128)
    return np.ascontiguousarray(r.transpose(2, 1, 0, 3))


def _masks():
    k = np.arange(128)[:, None]
    q = np.arange(512)[None, :]
    out = []
    for j in range(4):
        mj = (q >= 128 * j + k).astype(np.float32).astype(BF)
        out.append(np.concatenate([mj, mj], axis=1))
    return np.ascontiguousarray(np.concatenate(out, axis=1))


def _biasmat(par):
    bm = np.full((128, 80), SOFT_SHIFT, dtype=np.float32)
    if par == 0:
        base = 0
        for s_ in range(4):
            Ks = KS[s_]
            bm[:, base + Ks - 4: base + Ks] = -10000.0
            base += Ks
    return bm


def host_prepare(inputs):
    x = np.asarray(inputs["L0_attn_x"], dtype=np.float32)
    Wqkv = np.asarray(inputs["L0_attn_Wqkv"], dtype=np.float32)
    bqkv = np.asarray(inputs["L0_attn_bqkv"], dtype=np.float32)
    Wout = np.asarray(inputs["L0_attn_Wout"], dtype=np.float32)
    bout = np.asarray(inputs["L0_attn_bout"], dtype=np.float32)
    base_w = np.asarray(inputs["L0_ffn_base_w"], dtype=np.float32)
    base_b = np.asarray(inputs["L0_ffn_base_b"], dtype=np.float32)
    router_w = np.asarray(inputs["L0_ffn_router_w"], dtype=np.float32)
    router_b = np.asarray(inputs["L0_ffn_router_b"], dtype=np.float32)
    lora_A = np.asarray(inputs["L0_ffn_lora_A"], dtype=np.float32)
    lora_B = np.asarray(inputs["L0_ffn_lora_B"], dtype=np.float32)

    cos, sin = _rope_tables()
    qperm = _qcol_perm()
    kperm = _kcol_perm()
    woutperm = _wout_row_perm()

    wq_h = _tile4(np.ascontiguousarray(Wqkv[:, :2048][:, qperm]))
    wk_h = _tile4(np.ascontiguousarray(Wqkv[:, 2048:2560][:, kperm]))
    wv_h = _tile4(np.ascontiguousarray(Wqkv[:, 2560:3072]))
    wout_h = np.ascontiguousarray(Wout[woutperm].reshape(16, 128, D))
    wbaseT_h = np.ascontiguousarray(base_w.T.reshape(16, 128, D))
    A_all = np.ascontiguousarray(lora_A.transpose(1, 0, 2).reshape(D, 128))
    loraA_h = np.ascontiguousarray(A_all.reshape(16, 128, 128).transpose(1, 0, 2))
    loraB_h = np.ascontiguousarray(lora_B.reshape(128, D))
    routerT_h = np.ascontiguousarray(router_w.T.reshape(16, 128, 8).transpose(1, 0, 2)).astype(BF)

    bq_h = np.ascontiguousarray(bqkv[:2048][qperm].reshape(16, 128).T)
    bk_h = np.ascontiguousarray(bqkv[2048:2560][kperm].reshape(4, 128).T)
    bv_h = np.ascontiguousarray(bqkv[2560:3072].reshape(4, 128).T)
    rbb_h = np.ascontiguousarray(np.broadcast_to(router_b[None, :], (128, 8)))
    grot, gpass = _grot_gpass()
    shared = {
        "masks": _masks(), "grot": grot, "gpass": gpass, "ident": np.eye(128, dtype=BF),
        "ones1": np.ones((1, 64), dtype=np.float32),
        "onestok": np.ones((1, 128), dtype=BF),
        "wq": wq_h, "wk": wk_h, "wv": wv_h, "wout": wout_h,
        "wbaseT": wbaseT_h, "loraA": loraA_h, "loraB": loraB_h,
        "routerT": routerT_h, "bq": bq_h, "bk": bk_h, "bv": bv_h,
        "rbb": rbb_h, "bout_row": bout[None, :].astype(BF),
        "bbase_row": base_b[None, :].astype(BF),
    }

    in_maps = []
    metas = []
    for core in range(8):
        b, par = divmod(core, 2)
        own = [2 * i + par for i in range(4)]
        partner = [2 * i + (1 - par) for i in range(4)]
        rows = np.concatenate([np.arange(k * CH, (k + 1) * CH) for k in own + partner])
        xp = np.ascontiguousarray(x[b][rows])
        ct = cos[rows].T
        st_ = sin[rows].T
        ccss = np.ascontiguousarray(np.concatenate([ct, ct, st_, st_], axis=0))
        sscc = np.ascontiguousarray(np.concatenate([st_, st_, ct, ct], axis=0))
        m = dict(shared)
        m.update({"xp": xp, "ccss": ccss, "sscc": sscc, "biasmat": _biasmat(par)})
        in_maps.append(m)
        metas.append((b, own))
    return in_maps, metas


def kernel(**inputs):
    if "nc" not in _CACHE:
        _CACHE["nc"] = build_nc()
    nc = _CACHE["nc"]
    in_maps, metas = host_prepare(inputs)
    res = run_bass_kernel_spmd(nc, in_maps, core_ids=list(range(8)))
    _CACHE["last"] = res
    out = np.zeros((B, S, D), dtype=np.float32)
    for core in range(8):
        b, own = metas[core]
        o = np.asarray(res.results[core]["out"], dtype=np.float32)
        for i, k in enumerate(own):
            out[b, k * CH:(k + 1) * CH] = o[i * CH:(i + 1) * CH]
    return out


# version 15
# speedup vs baseline: 1.2307x; 1.2307x over previous
import numpy as np
import ml_dtypes
import concourse.bass as bass
import concourse.mybir as mybir
from concourse.bass_utils import run_bass_kernel_spmd

F32 = mybir.dt.float32
BF16 = mybir.dt.bfloat16
AF = mybir.ActivationFunctionType
ALU = mybir.AluOpType
AX = mybir.AxisListType
BF = ml_dtypes.bfloat16

B, S, D = 4, 4096, 2048
CH = 512
T_OWN = 2048
P_ALL = 4096
NSLOT = 4
KS = [8, 16, 24, 32]
SOFT_SHIFT = -8.0

_TAPS = False
_PHASES = 5
_CACHE = {}


class Prog:
    ENGINES = ("sync", "scalar", "vector", "tensor", "gpsimd")

    def __init__(self, nc):
        self.nc = nc
        self.steps = []
        self.sems = {}
        self.counts = {}
        self.last_ev = {}
        self._pending = {}

    def sem(self, name):
        if name not in self.sems:
            self.sems[name] = self.nc.alloc_semaphore(name)
            self.counts[name] = 0
        return self.sems[name]

    def emit(self, engine, fn, waits=(), inc=None, n=1, nochain=False):
        w = [x for x in waits if x is not None]
        if engine in self._pending:
            w += self._pending.pop(engine)
        ev = None
        incs = []
        if inc is not None:
            self.sem(inc)
            self.counts[inc] += n
            ev = (inc, self.counts[inc])
            incs = [(inc, n)]
            self.last_ev[engine] = ev
        self.steps.append((engine, fn, w, incs, nochain))
        return ev

    def barrier(self):
        evs = list(self.last_ev.values())
        for e in self.ENGINES:
            self._pending[e] = list(evs)

    def build(self):
        nc = self.nc
        with nc.Block() as block:
            for eng_name in self.ENGINES:
                if not any(s[0] == eng_name for s in self.steps):
                    continue

                def make(en):
                    def emit_stream(eng):
                        waited = {}
                        own = {"vector": "dve", "scalar": "act"}.get(en)
                        cnt = 0
                        for (e, fn, waits, incs, nochain) in self.steps:
                            if e != en:
                                continue
                            if own is not None and cnt > 0 and own in self.sems \
                                    and not nochain and waited.get(own, -1) < cnt:
                                eng.wait_ge(self.sems[own], cnt)
                                waited[own] = cnt
                            for (s, v) in waits:
                                if waited.get(s, -1) < v:
                                    eng.wait_ge(self.sems[s], v)
                                    waited[s] = v
                            ins = fn(eng)
                            for (s, a) in incs:
                                ins = ins.then_inc(self.sems[s], a)
                                if s == own:
                                    cnt += a
                    return emit_stream

                getattr(block, eng_name)(make(eng_name))


def build_nc():
    nc = bass.Bass("TRN2", target_bir_lowering=False)
    p = Prog(nc)

    di = {}

    def inp(name, shape, dt):
        di[name] = nc.dram_tensor(name, list(shape), dt, kind="ExternalInput").ap()

    inp("xp", [P_ALL, D], F32)
    inp("ccss", [64, P_ALL], F32)
    inp("sscc", [64, P_ALL], F32)
    inp("biasmat", [128, 80], F32)
    inp("masks", [128, 4096], BF16)
    inp("grot", [128, 128], BF16)
    inp("gpass", [128, 128], BF16)
    inp("ident", [128, 128], BF16)
    inp("ones1", [1, 64], F32)
    inp("onestok", [1, 128], BF16)
    inp("wq", [16, 128, 16, 128], F32)
    inp("wk", [4, 128, 16, 128], F32)
    inp("wv", [4, 128, 16, 128], F32)
    inp("wout", [16, 128, D], F32)
    inp("wbaseT", [16, 128, D], F32)
    inp("loraA", [128, 16, 128], F32)
    inp("loraB", [128, D], F32)
    inp("routerT", [128, 16, 8], BF16)
    inp("bq", [128, 16], F32)
    inp("bk", [128, 4], F32)
    inp("bv", [128, 4], F32)
    inp("rbb", [128, 8], F32)
    inp("bout_row", [1, D], BF16)
    inp("bbase_row", [1, D], BF16)

    ikind = "ExternalOutput" if _TAPS else "Internal"
    qt_dram = nc.dram_tensor("qt_dram", [16, 128, T_OWN], BF16, kind=ikind).ap()
    at_dram = nc.dram_tensor("at_dram", [16, 128, T_OWN], BF16, kind=ikind).ap()
    x2_dram = nc.dram_tensor("x2_dram", [T_OWN, D], F32, kind=ikind).ap()
    xn2_dram = nc.dram_tensor("xn2_dram", [T_OWN, D], BF16, kind=ikind).ap()
    dn_dram = nc.dram_tensor("dn_dram", [NSLOT, 16384], F32, kind=ikind).ap()
    rr_dram = nc.dram_tensor("rr_dram", [NSLOT, 16384], F32, kind="Internal").ap()
    out_dram = nc.dram_tensor("out", [T_OWN, D], F32, kind="ExternalOutput").ap()

    sb = {}
    cursors = {}

    def sba(name, shape, dt, region, reg_start):
        if region not in cursors:
            cursors[region] = reg_start
        esz = 2 if dt == BF16 else 4
        nb = 1
        for s_ in shape[1:]:
            nb *= s_
        nb *= esz
        off = cursors[region]
        cursors[region] = off + ((nb + 31) // 32) * 32
        sb[name] = nc.alloc_sbuf_tensor_at(name, list(shape), dt, offset=off)
        return sb[name]

    K0 = 18432
    A0 = 38912
    B0 = 108544
    C0 = 178176
    LIMIT = 229376

    sba("ident", [128, 128], BF16, "k", K0)
    sba("grot", [128, 128], BF16, "k", K0)
    sba("gpass", [128, 128], BF16, "k", K0)
    sba("ones1", [1, 64], F32, "k", K0)
    sba("onestok", [1, 128], BF16, "k", K0)
    sba("bq", [128, 16], F32, "k", K0)
    sba("bk", [128, 4], F32, "k", K0)
    sba("bv", [128, 4], F32, "k", K0)
    sba("rbb", [128, 8], F32, "k", K0)
    sba("biasmat", [128, 80], F32, "k", K0)
    sba("masks", [128, 4096], BF16, "k", K0)
    sba("trep", [128, 8], F32, "k", K0)
    sba("mvall", [128, 64], F32, "k", K0)
    sba("sqw", [128, 16], F32, "k", K0)
    sba("var32", [128, 32], F32, "k", K0)
    sba("rstd32", [128, 32], F32, "k", K0)
    sba("m32", [128, 32], F32, "k", K0)
    off_ccs = cursors["k"]
    sba("ccs", [64, 2, 512], F32, "k", K0)
    sba("ssc", [64, 2, 512], F32, "k", K0)

    sba("KT", [128, 4 * P_ALL], BF16, "A23", A0)
    sba("VP", [128, 32 * 520], BF16, "A23", A0)
    sba("rrsl", [128, 32], F32, "A23", A0)
    sba("u", [64, 512], F32, "A23", A0)
    sba("vf", [128, 512], BF16, "A23", A0)
    sba("xn2T", [128, 16 * T_OWN], BF16, "A5", A0)
    assert max(cursors["A23"], cursors["A5"]) <= B0

    sba("wkv", [128, 8 * D], BF16, "B2", B0)
    sba("xn1T", [128, 2, 16 * 512], BF16, "B2", B0)
    sba("dnrow", [1, 1024], F32, "B2", B0)
    sba("wout", [128, 16 * D], BF16, "B4", B0)
    sba("wbase", [128, 16 * D], BF16, "B5", B0)
    assert max(cursors["B2"], cursors["B4"], cursors["B5"]) <= C0

    sba("xpt", [128, 2, D], F32, "C1", C0)
    sba("xnt", [128, 2, D], BF16, "C1", C0)
    sba("xsq", [128, D], BF16, "C1", C0)
    sba("mv", [128, 2], F32, "k", None)
    sba("sq1", [128, 1], F32, "k", None)
    sba("stdc", [128, 1], F32, "k", None)
    sba("rstd", [128, 1], F32, "k", None)
    C2S = C0 + 32768
    sba("wqs", [128, 2, 2048], BF16, "C2", C2S)
    sba("wrope", [128, 512], BF16, "C2", C2S)
    sba("qrope", [128, 512], BF16, "C2", C2S)
    sba("qstage", [128, 2, 512], BF16, "C2", C2S)
    sba("PT", [128, 4, 1024], BF16, "C3a", C0)
    C3B = C0 + 8192
    sba("QTp", [128, 2, 512], BF16, "C3b", C3B)
    sba("Anum", [128, 4 * 512], F32, "C3b", C3B)
    sba("AT", [128, 4 * 512], BF16, "C3b", C3B)
    sba("rrow", [128, 2, 1024], F32, "C3b", C3B)
    sb["atg0"] = nc.alloc_sbuf_tensor_at("atg0", [128, 16 * 512], BF16, offset=C2S)
    sb["atg1"] = nc.alloc_sbuf_tensor_at("atg1", [128, 16 * 512], BF16, offset=A0 + 32768)
    sba("x2t", [128, 2, D], F32, "C4", C0)
    sba("xn2t", [128, 2, D], BF16, "C4", C0)
    sba("xsq4", [128, D], F32, "C4", C0)
    sb["bout_row"] = nc.alloc_sbuf_tensor_at("bout_row", [1, D], BF16, offset=off_ccs)
    sba("xti", [128, 2, D], F32, "C5", C0)
    sba("outt", [128, 2, D], F32, "C5", C0)
    sba("loraA", [128, 16 * 128], BF16, "C5", C0)
    sba("loraB", [128, D], BF16, "C5", C0)
    sba("routerT", [128, 128], BF16, "C5", C0)
    sba("midall", [128, 16 * 128], BF16, "C5", C0)
    sba("midT2", [128, 2, 128], BF16, "C5", C0)
    sba("bbase_row", [1, D], BF16, "C5", C0)
    C5G = C0 + 24576
    sba("lgall", [128, 128], F32, "C5g", C5G)
    sba("gmaskb", [128, 128], F32, "C5g", C5G)
    sba("gateb", [128, 128], F32, "C5g", C5G)
    sba("mx16", [128, 16], F32, "C5g", C5G)
    sba("m2_16", [128, 16], F32, "C5g", C5G)
    sba("gs16", [128, 16], F32, "C5g", C5G)
    sba("rg16", [128, 16], F32, "C5g", C5G)
    assert cursors["C5g"] <= C0 + 32768
    assert cursors["k"] <= A0, cursors["k"]
    for r in ("C1", "C2", "C3b", "C4", "C5", "k", "A23"):
        assert cursors[r] <= LIMIT, (r, cursors[r])

    PSBIG = nc.alloc_psum_tensor("PSBIG", [128, 2048], F32)
    PN_A = nc.alloc_psum_tensor("PN_A", [128, 512], F32)
    PN_B = nc.alloc_psum_tensor("PN_B", [128, 512], F32)
    PM1 = nc.alloc_psum_tensor("PM1", [128, 512], F32)
    PM2 = nc.alloc_psum_tensor("PM2", [128, 512], F32)
    PM1b = PM1[:, :].bitcast(BF16)
    PM2b = PM2[:, :].bitcast(BF16)
    PN_Ab = PN_A[:, :].bitcast(BF16)
    PN_Bb = PN_B[:, :].bitcast(BF16)
    PSBIGb = PSBIG[:, :].bitcast(BF16)

    for name in ["ident", "grot", "gpass", "ones1", "onestok", "bq", "bk", "bv", "rbb",
                 "biasmat", "masks"]:
        p.emit("sync", lambda eng, n=name: eng.dma_start(out=sb[n][:, :], in_=di[n]),
               inc="qsp", n=16)
    ld0 = ("qsp", p.counts["qsp"])
    p.emit("vector", lambda eng: eng.memset(sb["trep"][:, 2:8], -1.0), inc="dve")
    p.emit("vector", lambda eng: eng.memset(sb["qrope"][0:64, :], 0.0), inc="dve")
    ev_vones = p.emit("vector", lambda eng: eng.memset(
        sb["VP"][:, :].rearrange("p (k g c) -> p k g c", k=32, g=8)[:, :, :, 64:65], 1.0),
        inc="dve")

    NT1 = P_ALL // 128
    ev_tr1 = [None] * NT1
    ev_ev1 = [None] * NT1
    ev_sqr = [None] * (NT1 + 16)
    ev_xn = [None] * NT1

    def emit_p1a_tile(i):
        bf = i % 2
        evl = p.emit("sync", lambda eng, i=i, bf=bf: eng.dma_start(
            out=sb["xpt"][:, bf], in_=di["xp"][i * 128:(i + 1) * 128, :]),
            waits=[ev_sqr[i - 2]] if i % 8 >= 2 else [ev_xn[i - 2]], inc="qsp", n=16)
        ev_q = p.emit("scalar", lambda eng, bf=bf: eng.activation(
            sb["xsq"][:, :], sb["xpt"][:, bf], AF.Square),
            waits=[evl, ev_sqr[i - 1] if i >= 1 else None], inc="act", nochain=True)
        p.emit("vector", lambda eng, i=i, bf=bf: eng.reduce_sum(
            sb["mvall"][:, 2 * (i % 16):2 * (i % 16) + 1], sb["xpt"][:, bf], AX.X),
            waits=[evl], inc="dve", nochain=True)
        ev_sqr[i] = p.emit("vector", lambda eng, i=i: eng.reduce_sum(
            sb["mvall"][:, 2 * (i % 16) + 1:2 * (i % 16) + 2], sb["xsq"][:, :], AX.X),
            waits=[ev_q], inc="dve", nochain=True)

    def emit_p1b_wave(w):
        c0 = 16 * (w % 2)
        ev_o1 = p.emit("vector", lambda eng, c0=c0: eng.scalar_tensor_tensor(
            sb["sqw"][:, 0:8], sb["mvall"][:, c0:c0 + 16:2],
            -1.0 / (2048.0 * 2048.0), sb["mvall"][:, c0:c0 + 16:2],
            ALU.mult, ALU.mult), waits=[ev_sqr[8 * w + 7]], inc="dve")
        ev_o2 = p.emit("vector", lambda eng, w=w, c0=c0: eng.scalar_tensor_tensor(
            sb["var32"][:, 8 * w:8 * w + 8], sb["mvall"][:, c0 + 1:c0 + 16:2],
            1.0 / 2048.0, sb["sqw"][:, 0:8], ALU.mult, ALU.add), inc="dve")
        ev_o3 = p.emit("vector", lambda eng, w=w: eng.tensor_scalar(
            sb["var32"][:, 8 * w:8 * w + 8], sb["var32"][:, 8 * w:8 * w + 8],
            1e-5, None, ALU.add), inc="dve")
        ev_ln = p.emit("scalar", lambda eng, w=w: eng.activation(
            sb["var32"][:, 8 * w:8 * w + 8], sb["var32"][:, 8 * w:8 * w + 8],
            AF.Ln), waits=[ev_o3], inc="act")
        ev_ex = p.emit("scalar", lambda eng, w=w: eng.activation(
            sb["rstd32"][:, 8 * w:8 * w + 8], sb["var32"][:, 8 * w:8 * w + 8],
            AF.Exp, scale=-0.5), inc="act")
        ev_m = p.emit("vector", lambda eng, w=w, c0=c0: eng.tensor_scalar(
            sb["m32"][:, 8 * w:8 * w + 8], sb["mvall"][:, c0:c0 + 16:2],
            1.0 / 2048.0, None, ALU.mult), waits=[ev_ex], inc="dve")
        return ev_m

    def emit_p1c_tile(i, ev_w):
        bf = i % 2
        g, j = divmod(i, 4)
        evl = p.emit("scalar", lambda eng, i=i, bf=bf: eng.dma_start(
            out=sb["xpt"][:, bf], in_=di["xp"][i * 128:(i + 1) * 128, :]),
            waits=[ev_xn[i - 2], ev_sqr[(i // 8) * 8 + 7]], inc="qsc", n=16)
        ev_xn[i] = p.emit("vector", lambda eng, i=i, bf=bf: eng.tensor_scalar(
            sb["xnt"][:, bf], sb["xpt"][:, bf], sb["m32"][:, i:i + 1],
            sb["rstd32"][:, i:i + 1], ALU.subtract, ALU.mult),
            waits=[evl, ev_w, ev_tr1[i - 2]], inc="dve", nochain=True)
        evt = None
        for c in range(16):
            wtr = [ev_xn[i], ev_ev1[i - 2]] if c == 0 else []
            evt = p.emit("tensor", lambda eng, i=i, c=c, bf=bf: eng.transpose(
                PSBIGb[:, (i % 2) * 2048 + c * 128:(i % 2) * 2048 + (c + 1) * 128],
                sb["xnt"][:, bf][:, c * 128:(c + 1) * 128], sb["ident"][:, :]),
                waits=wtr, inc="pe")
        ev_tr1[i] = evt
        wev = [ev_tr1[i]]
        if j == 0 and g >= 2:
            wev.append(ev_xn1t_done[g - 2])
        ev_ev1[i] = p.emit("vector", lambda eng, i=i, g=g, j=j: eng.tensor_copy(
            sb["xn1T"][:, g % 2].rearrange("p (c x) -> p c x", c=16)[:, :, j * 128:(j + 1) * 128],
            PSBIGb[:, (i % 2) * 2048:(i % 2 + 1) * 2048].rearrange("p (c x) -> p c x", c=16)),
            waits=wev, inc="dve", nochain=True)

    for t in range(4):
        p.emit("gpsimd", lambda eng, t=t: eng.dma_start(
            out=sb["wkv"][:, t * 2048:(t + 1) * 2048],
            in_=di["wk"][t].rearrange("p c j -> p (c j)")), inc="qgp", n=16)
        p.emit("gpsimd", lambda eng, t=t: eng.dma_start(
            out=sb["wkv"][:, (4 + t) * 2048:(5 + t) * 2048],
            in_=di["wv"][t].rearrange("p c j -> p (c j)")), inc="qgp", n=16)
    ev_wkv = ("qgp", p.counts["qgp"])

    NG = P_ALL // 512
    ev_kv_ready = [None] * NG
    ev_xn1t = [None] * NG
    ev_xn1t_done = [None] * NG
    ev_cs = [None] * NG
    ev_cs_done = [None] * NG
    ev_qspill = {}
    qload = {}
    qdone = {}
    prev_perm = None
    prev_vtr = None
    ps_free = {"PN_A": None, "PN_B": None, "PM1": None, "PM2": None}

    def emit_qload(qi):
        g, t = divmod(qi, 16)
        w = [qdone.get(qi - 2), ev_wkv if qi == 0 else None]
        qload[qi] = p.emit("gpsimd", lambda eng, t=t: eng.dma_start(
            out=sb["wqs"][:, t % 2],
            in_=di["wq"][t].rearrange("p c j -> p (c j)")),
            waits=w, inc="qgp", n=16)

    emit_qload(0)
    emit_qload(1)

    def emit_p2_group(g):
        nonlocal prev_vtr
        own = g < 4
        bf = g % 2
        ev_xn1t[g] = ev_ev1[4 * g + 3]
        csw = [ev_cs_done[g - 2]] if g >= 2 else []
        p.emit("sync", lambda eng, g=g, bf=bf: eng.dma_start(
            out=sb["ccs"][:, bf], in_=di["ccss"][:, g * 512:(g + 1) * 512]),
            waits=csw, inc="qsp", n=16)
        ev_cs[g] = p.emit("sync", lambda eng, g=g, bf=bf: eng.dma_start(
            out=sb["ssc"][:, bf], in_=di["sscc"][:, g * 512:(g + 1) * 512]),
            inc="qsp", n=16)

        def proj_mms(psum_ap, w_fn, fw):
            ev = None
            for c in range(16):
                ev = p.emit("tensor", lambda eng, c=c, psum_ap=psum_ap, w_fn=w_fn, bf=bf: eng.matmul(
                    psum_ap, w_fn(c), sb["xn1T"][:, bf][:, c * 512:(c + 1) * 512],
                    start=(c == 0), stop=(c == 15)),
                    waits=list(fw) if c == 0 else [], inc="pe")
            return ev

        def rope_evac(psrc, bias_col, dst_copy_fn, extra_pm_waits, ev_mm):
            nonlocal prev_perm
            ev_u = p.emit("vector", lambda eng, psrc=psrc, bias_col=bias_col: eng.tensor_scalar(
                sb["u"][:, :], psrc[0:64, :], bias_col[0:64, :], None, ALU.add),
                waits=[ev_cs[g], prev_perm, ev_mm], inc="dve")
            ev_pass = p.emit("vector", lambda eng, psrc=psrc, bias_col=bias_col: eng.tensor_scalar(
                sb["qrope"][64:128, :], psrc[64:128, :], bias_col[64:128, :], None, ALU.add),
                inc="dve")
            p.emit("vector", lambda eng, bf=bf: eng.tensor_mul(
                sb["wrope"][0:64, :], sb["u"][:, :], sb["ccs"][:, bf]), inc="dve")
            ev_b = p.emit("vector", lambda eng, bf=bf: eng.tensor_mul(
                sb["wrope"][64:128, :], sb["u"][:, :], sb["ssc"][:, bf]), inc="dve")
            ev_cs_done[g] = ev_b
            p.emit("tensor", lambda eng: eng.matmul(
                PM2[:, :], sb["grot"][:, :], sb["wrope"][:, :], start=True, stop=False),
                waits=[ev_b, ps_free["PM2"]] + list(extra_pm_waits), inc="pe")
            ev_pm = p.emit("tensor", lambda eng: eng.matmul(
                PM2[:, :], sb["gpass"][:, :], sb["qrope"][:, :], start=False, stop=True),
                inc="pe")
            prev_perm = ev_pm
            ev_cp = dst_copy_fn(ev_pm)
            ps_free["PM2"] = ev_cp
            return ev_pass, ev_cp

        for kt in range(4):
            fw = [ev_wkv, ev_xn1t[g], ps_free["PN_A"]]
            if g == 0 and kt == 0:
                fw.append(ld0)
            ev_mm = proj_mms(PN_A[:, :],
                             lambda c, kt=kt: sb["wkv"][:, kt * 2048 + c * 128: kt * 2048 + (c + 1) * 128],
                             fw)
            def kcopy(ev_pm, kt=kt, g=g):
                return p.emit("vector", lambda eng, kt=kt, g=g: eng.tensor_copy(
                    sb["KT"][:, kt * P_ALL + g * 512: kt * P_ALL + (g + 1) * 512], PM2[:, :]),
                    waits=[ev_pm], inc="dve")
            ev_pass, _ = rope_evac(PN_A, sb["bk"][:, kt:kt + 1], kcopy, [], ev_mm)
            ps_free["PN_A"] = ev_pass

        for kt in range(4):
            fw = [ev_wkv, ev_xn1t[g], ps_free["PN_B"]]
            ev_mm = proj_mms(PN_B[:, :],
                             lambda c, kt=kt: sb["wkv"][:, (4 + kt) * 2048 + c * 128: (4 + kt) * 2048 + (c + 1) * 128],
                             fw)
            ev_vf = p.emit("vector", lambda eng, kt=kt: eng.tensor_scalar(
                sb["vf"][:, :], PN_B[:, :], sb["bv"][:, kt:kt + 1], None, ALU.add),
                waits=[ev_mm, ev_vones, prev_vtr], inc="dve")
            ps_free["PN_B"] = ev_vf
            for j in range(4):
                ktile = g * 4 + j
                ev_tr = p.emit("tensor", lambda eng, j=j: eng.transpose(
                    PM1b[:, 0:128], sb["vf"][:, j * 128:(j + 1) * 128], sb["ident"][:, :]),
                    waits=[ev_vf, ps_free["PM1"]], inc="pe")
                prev_vtr = ev_tr
                e2 = p.emit("vector", lambda eng, ktile=ktile, kt=kt: eng.tensor_copy(
                    sb["VP"][:, ktile * 520 + (2 * kt) * 65: ktile * 520 + (2 * kt) * 65 + 130]
                    .rearrange("q (h c) -> q h c", h=2)[:, :, 0:64],
                    PM1b[:, 0:128].rearrange("q (h c) -> q h c", h=2)),
                    waits=[ev_tr], inc="dve")
                ps_free["PM1"] = e2

        if own:
            for t in range(16):
                qi = g * 16 + t
                fw = [qload[qi], ev_xn1t[g], ps_free["PM1"]]
                ev_mm = None
                for c in range(16):
                    ev_mm = p.emit("tensor", lambda eng, c=c, t=t, bf=bf: eng.matmul(
                        PM1[:, :], sb["wqs"][:, t % 2][:, c * 128:(c + 1) * 128],
                        sb["xn1T"][:, bf][:, c * 512:(c + 1) * 512],
                        start=(c == 0), stop=(c == 15)),
                        waits=list(fw) if c == 0 else [], inc="pe")
                qdone[qi] = ev_mm

                def qcopy(ev_pm, g=g, t=t, qi=qi):
                    ev_cp = p.emit("vector", lambda eng, t=t: eng.tensor_copy(
                        sb["qstage"][:, t % 2], PM2[:, :]),
                        waits=[ev_pm, ev_qspill.get(qi - 2)], inc="dve")
                    ev_qspill[qi] = p.emit("sync", lambda eng, t=t, g=g: eng.dma_start(
                        out=qt_dram[t][:, g * 512:(g + 1) * 512], in_=sb["qstage"][:, t % 2]),
                        waits=[ev_cp], inc="qsp", n=16)
                    return ev_cp

                ev_pass, _ = rope_evac(PM1, sb["bq"][:, t:t + 1], qcopy, [], ev_mm)
                ps_free["PM1"] = ev_pass
                if qi + 2 < 64:
                    emit_qload(qi + 2)
        ev_xn1t_done[g] = p.last_ev["tensor"]
        ev_kv_ready[g] = p.last_ev["vector"]

    slot_base = [0, 8, 24, 48]
    ev_exp_hist = []
    ev_av_hist = []
    prev_evac = {0: None, 1: None}
    prev_nm_w = [None] * 4
    prev_at_w = [None] * 4
    at_spill = {}
    prev_dn = None
    ev_rr_prev = [None, None]
    qt_load = {}
    qt_done = {}

    def emit_qtload(s, t):
        sti = s * 16 + t
        w = [qt_done.get(sti - 2), ev_qspill[s * 16 + t]]
        if sti == 0:
            w.append(ev_tr1[NT1 - 1])
        qt_load[sti] = p.emit("sync", lambda eng, s=s, t=t: eng.dma_start(
            out=sb["QTp"][:, t % 2], in_=qt_dram[t][:, s * 512:(s + 1) * 512]),
            waits=w, inc="qsp", n=16)

    def emit_p3_slot(s):
        nonlocal prev_dn
        Ks = KS[s]
        if s == 0:
            emit_qtload(0, 0)
            emit_qtload(0, 1)
        ev_evacB = [None] * 16
        ev_dn_ev = [None] * 16
        for t in range(16):
            sti = s * 16 + t
            kt = t // 4
            ev_exp_u = {}
            ev_dep_u = {}
            ev_av_u = {}

            NB_A = PN_A if t % 2 == 0 else PM1
            NB_B = PN_B if t % 2 == 0 else PM2

            def emit_av(pp, NB_A=NB_A, NB_B=NB_B):
                nown = 4 * (s + 1)
                kb = pp if pp < nown else 16 + (pp - nown)
                bufP = pp % 4
                avw = [ev_dep_u[pp]]
                if pp == 0:
                    avw.append(prev_evac[t % 2])
                p.emit("tensor", lambda eng, kb=kb, kt=kt, bufP=bufP, pp=pp, Ks=Ks: eng.matmul(
                    NB_A[0:65, :],
                    sb["VP"][:, kb * 520 + (2 * kt) * 65: kb * 520 + (2 * kt) * 65 + 65],
                    sb["PT"][:, bufP][:, 0:512], start=(pp == 0), stop=(pp == Ks - 1)),
                    waits=avw, inc="pe")
                ev_av_u[pp] = p.emit("tensor", lambda eng, kb=kb, kt=kt, bufP=bufP, pp=pp, Ks=Ks: eng.matmul(
                    NB_B[0:65, :],
                    sb["VP"][:, kb * 520 + (2 * kt + 1) * 65: kb * 520 + (2 * kt + 1) * 65 + 65],
                    sb["PT"][:, bufP][:, 512:1024], start=(pp == 0), stop=(pp == Ks - 1)),
                    inc="pe")

            for pp in range(Ks):
                nown = 4 * (s + 1)
                kb = pp if pp < nown else 16 + (pp - nown)
                diag = (4 * s <= pp < nown)
                mj = pp - 4 * s if diag else 0
                bufS = pp % 2
                bufP = pp % 4
                w = [ev_kv_ready[kb // 4]]
                if pp == 0:
                    w.append(qt_load[sti])
                    if sti == 0:
                        w.append(ev_ev1[NT1 - 1])
                if pp >= 2:
                    w.append(ev_exp_u[pp - 2])
                elif len(ev_exp_hist) >= 2:
                    w.append(ev_exp_hist[-2])
                p.emit("tensor", lambda eng, kt=kt, kb=kb, bufS=bufS, t=t: eng.matmul(
                    PSBIG[:, bufS * 1024: bufS * 1024 + 512],
                    sb["KT"][0:64, kt * P_ALL + kb * 128: kt * P_ALL + (kb + 1) * 128],
                    sb["QTp"][0:64, t % 2], start=True, stop=True),
                    waits=w, inc="pe")
                ev_s2 = p.emit("tensor", lambda eng, kt=kt, kb=kb, bufS=bufS, t=t: eng.matmul(
                    PSBIG[:, bufS * 1024 + 512: bufS * 1024 + 1024],
                    sb["KT"][64:128, kt * P_ALL + kb * 128: kt * P_ALL + (kb + 1) * 128],
                    sb["QTp"][64:128, t % 2], start=True, stop=True), inc="pe")
                ew = [ev_s2]
                if pp >= 4:
                    ew.append(ev_av_u[pp - 4])
                elif ev_av_hist:
                    ew.append(ev_av_hist[-1])
                if not ev_exp_hist:
                    ew.append(ld0)
                    ew.append(ev_tr1[NT1 - 1])
                ev_exp = p.emit("scalar", lambda eng, s=s, pp=pp, bufS=bufS, bufP=bufP: eng.activation(
                    sb["PT"][:, bufP], PSBIG[:, bufS * 1024:(bufS + 1) * 1024], AF.Exp,
                    bias=sb["biasmat"][:, slot_base[s] + pp: slot_base[s] + pp + 1],
                    scale=0.125), waits=ew, inc="act", nochain=True)
                ev_exp_hist.append(ev_exp)
                ev_exp_u[pp] = ev_exp
                dep = ev_exp
                if diag:
                    dep = p.emit("vector", lambda eng, mj=mj, bufP=bufP: eng.tensor_mul(
                        sb["PT"][:, bufP], sb["PT"][:, bufP],
                        sb["masks"][:, mj * 1024:(mj + 1) * 1024]),
                        waits=[ev_exp], inc="dve")
                ev_dep_u[pp] = dep
                if pp >= 1:
                    emit_av(pp - 1)
            emit_av(Ks - 1)
            ev_av_hist.append(ev_av_u[Ks - 1])
            qt_done[sti] = ev_av_u[Ks - 1]
            if sti + 2 < 64:
                emit_qtload((sti + 2) // 16, (sti + 2) % 16)
            j = t % 4
            p.emit("vector", lambda eng, j=j, NB_A=NB_A: eng.tensor_copy(
                sb["Anum"][0:64, j * 512:(j + 1) * 512], NB_A[0:64, :]),
                waits=[ev_av_hist[-1], prev_nm_w[j]], inc="dve")
            p.emit("vector", lambda eng, j=j, NB_B=NB_B: eng.tensor_copy(
                sb["Anum"][64:128, j * 512:(j + 1) * 512], NB_B[0:64, :]), inc="dve")
            p.emit("vector", lambda eng, NB_A=NB_A: eng.tensor_copy(
                sb["dnrow"][0:1, 0:512], NB_A[64:65, :]), waits=[prev_dn], inc="dve")
            ev_d2 = p.emit("vector", lambda eng, NB_B=NB_B: eng.tensor_copy(
                sb["dnrow"][0:1, 512:1024], NB_B[64:65, :]), inc="dve")
            ev_evacB[t] = ev_d2
            prev_evac[t % 2] = ev_d2
            ev_dn_ev[t] = p.emit("sync", lambda eng, s=s, t=t: eng.dma_start(
                out=dn_dram[s:s + 1, t * 1024:(t + 1) * 1024], in_=sb["dnrow"][0:1, :]),
                waits=[ev_d2], inc="qsp", n=16)
            prev_dn = ev_dn_ev[t]
            if t % 4 == 3:
                t0 = t - 3
                ev_ld = p.emit("sync", lambda eng, s=s, t0=t0: eng.dma_start(
                    out=sb["rrsl"][:, :],
                    in_=dn_dram[s, t0 * 1024:(t0 + 4) * 1024].rearrange("(p f) -> p f", p=128)),
                    waits=[ev_dn_ev[t]], inc="qsp", n=16)
                ev_rc = p.emit("vector", lambda eng: eng.reciprocal(
                    sb["rrsl"][:, :], sb["rrsl"][:, :]), waits=[ev_ld], inc="dve")
                ev_st = p.emit("sync", lambda eng, s=s, t0=t0: eng.dma_start(
                    out=rr_dram[s, t0 * 1024:(t0 + 4) * 1024].rearrange("(p f) -> p f", p=128),
                    in_=sb["rrsl"][:, :]), waits=[ev_rc], inc="qsp", n=16)
                for u in range(t0, t0 + 4):
                    j2 = u % 4
                    ev_rr = p.emit("sync", lambda eng, s=s, u=u: eng.dma_start(
                        out=sb["rrow"][:, u % 2],
                        in_=rr_dram[s, u * 1024:(u + 1) * 1024].unsqueeze(0).broadcast_to([128, 1024])),
                        waits=[ev_st, ev_rr_prev[u % 2]], inc="qsp", n=16)
                    p.emit("vector", lambda eng, j2=j2, u=u: eng.tensor_mul(
                        sb["AT"][0:64, j2 * 512:(j2 + 1) * 512],
                        sb["Anum"][0:64, j2 * 512:(j2 + 1) * 512],
                        sb["rrow"][0:64, u % 2][:, 0:512]),
                        waits=[ev_rr, prev_at_w[j2]], inc="dve")
                    ev_nm = p.emit("vector", lambda eng, j2=j2, u=u: eng.tensor_mul(
                        sb["AT"][64:128, j2 * 512:(j2 + 1) * 512],
                        sb["Anum"][64:128, j2 * 512:(j2 + 1) * 512],
                        sb["rrow"][64:128, u % 2][:, 512:1024]), inc="dve")
                    ev_rr_prev[u % 2] = ev_nm
                    prev_nm_w[j2] = ev_nm
                    evsp = p.emit("sync", lambda eng, s=s, u=u, j2=j2: eng.dma_start(
                        out=at_dram[u][:, s * 512:(s + 1) * 512],
                        in_=sb["AT"][:, j2 * 512:(j2 + 1) * 512]),
                        waits=[ev_nm], inc="qsp", n=16)
                    prev_at_w[j2] = evsp
                    at_spill[(s, u)] = evsp

    for w in range(4):
        for i in range(8 * w, 8 * w + 8):
            emit_p1a_tile(i)
        ev_w = emit_p1b_wave(w)
        for i in range(8 * w, 8 * w + 8):
            emit_p1c_tile(i, ev_w)
        emit_p2_group(2 * w)
        emit_p2_group(2 * w + 1)
    emit_p3_slot(0)
    emit_p3_slot(1)
    emit_p3_slot(2)
    emit_p3_slot(3)
    p.barrier()

    for c in range(16):
        p.emit("gpsimd", lambda eng, c=c: eng.dma_start(
            out=sb["wout"][:, c * 2048:(c + 1) * 2048], in_=di["wout"][c]), inc="qgp", n=16)
    p.emit("gpsimd", lambda eng: eng.dma_start(
        out=sb["bout_row"][:, :], in_=di["bout_row"]),
        waits=[ev_cs_done[7]], inc="qgp", n=16)
    ev_wout = ("qgp", p.counts["qgp"])

    ev_gdone = [None] * 4
    ev_gload = [None] * 4
    ev_x2sp = [None] * 16
    ev_r4 = [None] * 16
    ev_s4 = [None] * 16
    prev_psbig = None
    prev_sqr4 = None
    prev_tile = {}
    for gg in range(4):
        atb = "atg0" if gg % 2 == 0 else "atg1"
        if gg == 0:
            w0 = [ev_qspill[63], ev_xn1t_done[7]]
        elif gg == 1:
            w0 = [qt_done[63]]
        else:
            w0 = [ev_gdone[gg - 2]]
        ev_at = None
        for c in range(16):
            ev_at = p.emit("gpsimd", lambda eng, gg=gg, c=c, atb=atb: eng.dma_start(
                out=sb[atb][:, c * 512:(c + 1) * 512],
                in_=at_dram[c][:, gg * 512:(gg + 1) * 512]),
                waits=(w0 if c == 0 else []) + [at_spill[(gg, c)]], inc="qgp", n=16)
        ev_gload[gg] = ev_at
        if gg == 3:
            ev_lorab = p.emit("gpsimd", lambda eng: eng.dma_start(
                out=sb["loraB"][:, :], in_=di["loraB"]),
                waits=[ev_gdone[2]], inc="qgp", n=16)
            ev_rt = p.emit("gpsimd", lambda eng: eng.dma_start(
                out=sb["routerT"][:, :], in_=di["routerT"].rearrange("p c j -> p (c j)")),
                inc="qgp", n=16)
            ev_w5s = p.emit("gpsimd", lambda eng: eng.dma_start(
                out=sb["bbase_row"][:, :], in_=di["bbase_row"]), inc="qgp", n=16)
        last_mm = None
        for ti in range(4):
            tix = gg * 4 + ti
            row0 = tix * 128
            bft = tix % 2
            ev_x = p.emit("sync", lambda eng, row0=row0, bft=bft: eng.dma_start(
                out=sb["x2t"][:, bft], in_=di["xp"][row0:row0 + 128, :]),
                waits=[prev_tile.get(("sp", bft)), prev_tile.get(("rd", bft))],
                inc="qsp", n=16)
            ev_mm = None
            for c in range(16):
                wmm = []
                if c == 0:
                    wmm = [ev_gload[gg], ev_wout, prev_psbig]
                for nn in range(4):
                    ev_mm = p.emit("tensor", lambda eng, c=c, nn=nn, ti=ti, atb="atg0" if gg % 2 == 0 else "atg1": eng.matmul(
                        PSBIG[:, nn * 512:(nn + 1) * 512],
                        sb[atb][:, c * 512 + ti * 128: c * 512 + (ti + 1) * 128],
                        sb["wout"][:, c * 2048 + nn * 512: c * 2048 + (nn + 1) * 512],
                        start=(c == 0), stop=False),
                        waits=wmm if nn == 0 else [], inc="pe")
            for nn in range(4):
                ev_mm = p.emit("tensor", lambda eng, nn=nn: eng.matmul(
                    PSBIG[:, nn * 512:(nn + 1) * 512], sb["onestok"][0:1, :],
                    sb["bout_row"][0:1, nn * 512:(nn + 1) * 512],
                    start=False, stop=True), inc="pe")
            last_mm = ev_mm
            evs_x2 = []
            for nn in range(4):
                w2 = [ev_mm, ev_x] if nn == 0 else []
                evs_x2.append(p.emit("vector", lambda eng, nn=nn, bft=bft: eng.tensor_add(
                    sb["x2t"][:, bft][:, nn * 512:(nn + 1) * 512],
                    PSBIG[:, nn * 512:(nn + 1) * 512],
                    sb["x2t"][:, bft][:, nn * 512:(nn + 1) * 512]), waits=w2, inc="dve"))
            prev_psbig = evs_x2[3]
            ev_q4 = p.emit("scalar", lambda eng, bft=bft: eng.activation(
                sb["xsq4"][:, :], sb["x2t"][:, bft], AF.Square),
                waits=[evs_x2[3], prev_sqr4], inc="act")
            ev_r4[tix] = p.emit("vector", lambda eng, bft=bft, tix=tix: eng.reduce_sum(
                sb["mvall"][:, 2 * tix:2 * tix + 1], sb["x2t"][:, bft], AX.X),
                waits=[evs_x2[3]], inc="dve", nochain=True)
            ev_s4[tix] = p.emit("vector", lambda eng, tix=tix: eng.reduce_sum(
                sb["mvall"][:, 2 * tix + 1:2 * tix + 2], sb["xsq4"][:, :], AX.X),
                waits=[ev_q4], inc="dve", nochain=True)
            prev_sqr4 = ev_s4[tix]
            ev_x2sp[tix] = p.emit("sync", lambda eng, row0=row0, bft=bft: eng.dma_start(
                out=x2_dram[row0:row0 + 128, :], in_=sb["x2t"][:, bft]),
                waits=[evs_x2[3]], inc="qsp", n=16)
            prev_tile[("sp", bft)] = ev_x2sp[tix]
            prev_tile[("rd", bft)] = ev_r4[tix]
        ev_gdone[gg] = last_mm
    last_mm15 = ev_gdone[3]
    ev_la = p.emit("gpsimd", lambda eng: eng.dma_start(
        out=sb["loraA"][:, :], in_=di["loraA"].rearrange("p c j -> p (c j)")),
        waits=[last_mm15], inc="qgp", n=16)
    ev_wb = [None] * 16
    for c in range(16):
        ev_wb[c] = p.emit("gpsimd", lambda eng, c=c: eng.dma_start(
            out=sb["wbase"][:, c * 2048:(c + 1) * 2048], in_=di["wbaseT"][c]),
            inc="qgp", n=16)

    p.emit("vector", lambda eng: eng.scalar_tensor_tensor(
        sb["sqw"][:, 0:16], sb["mvall"][:, 0:32:2],
        -1.0 / (2048.0 * 2048.0), sb["mvall"][:, 0:32:2],
        ALU.mult, ALU.mult), waits=[ev_s4[15]], inc="dve")
    p.emit("vector", lambda eng: eng.scalar_tensor_tensor(
        sb["var32"][:, 0:16], sb["mvall"][:, 1:32:2],
        1.0 / 2048.0, sb["sqw"][:, 0:16], ALU.mult, ALU.add), inc="dve")
    ev_o3 = p.emit("vector", lambda eng: eng.tensor_scalar(
        sb["var32"][:, 0:16], sb["var32"][:, 0:16], 1e-5, None, ALU.add), inc="dve")
    ev_ln4 = p.emit("scalar", lambda eng: eng.activation(
        sb["var32"][:, 0:16], sb["var32"][:, 0:16], AF.Ln), waits=[ev_o3], inc="act")
    ev_ex4 = p.emit("scalar", lambda eng: eng.activation(
        sb["rstd32"][:, 0:16], sb["var32"][:, 0:16], AF.Exp, scale=-0.5), inc="act")
    ev_m4 = p.emit("vector", lambda eng: eng.tensor_scalar(
        sb["m32"][:, 0:16], sb["mvall"][:, 0:32:2], 1.0 / 2048.0, None, ALU.mult),
        waits=[ev_ex4], inc="dve")

    ev_ld4 = [None] * 16
    ev_ts2 = [None] * 16
    ev_tr2 = [None] * 16
    ev_ev2 = [None] * 16
    ev_mme = [None] * 16
    ev_lge = [None] * 16
    for i in range(16):
        bf = i % 2
        enq, qn = ("sync", "qsp") if bf == 0 else ("scalar", "qsc")
        wl = [ev_x2sp[i]]
        if i >= 2:
            wl.append(ev_ts2[i - 2])
        else:
            wl += [ev_r4[14 + i], ev_x2sp[14 + i]]
        ev_ld4[i] = p.emit(enq, lambda eng, i=i, bf=bf: eng.dma_start(
            out=sb["x2t"][:, bf], in_=x2_dram[i * 128:(i + 1) * 128, :]),
            waits=wl, inc=qn, n=16)
        ev_ts2[i] = p.emit("vector", lambda eng, i=i, bf=bf: eng.tensor_scalar(
            sb["xn2t"][:, bf], sb["x2t"][:, bf], sb["m32"][:, i:i + 1],
            sb["rstd32"][:, i:i + 1], ALU.subtract, ALU.mult),
            waits=[ev_ld4[i], ev_m4, ev_tr2[i - 2] if i >= 2 else None],
            inc="dve", nochain=True)
        bks = (PM1b, PM2b) if bf == 0 else (PN_Ab, PN_Bb)
        ev_tr7 = None
        evt = None
        for c in range(16):
            wtr = []
            if c == 0:
                wtr = [ev_ts2[i], ev_ev2[i - 2] if i >= 2 else None]
            evt = p.emit("tensor", lambda eng, c=c, bf=bf, bks=bks: eng.transpose(
                bks[c // 8][:, (c % 8) * 128:(c % 8 + 1) * 128],
                sb["xn2t"][:, bf][:, c * 128:(c + 1) * 128], sb["ident"][:, :]),
                waits=wtr, inc="pe")
            if c == 7:
                ev_tr7 = evt
        ev_tr2[i] = evt
        p.emit("scalar", lambda eng, i=i, bks=bks: eng.activation(
            sb["xn2T"][:, i * 2048:i * 2048 + 1024], bks[0][:, 0:1024], AF.Copy),
            waits=[ev_tr7], inc="act")
        ev_ev2[i] = p.emit("scalar", lambda eng, i=i, bks=bks: eng.activation(
            sb["xn2T"][:, i * 2048 + 1024:(i + 1) * 2048], bks[1][:, 0:1024], AF.Copy),
            waits=[ev_tr2[i]], inc="act")
        mcol = 0 if bf == 0 else 1024
        lcol = 512 if bf == 0 else 1536
        evm = None
        evlg = None
        for c in range(16):
            wm = []
            if c == 0:
                wm = [ev_ev2[i], ev_la,
                      ev_mme[i - 2] if i >= 2 else prev_psbig]
            evm = p.emit("tensor", lambda eng, c=c, i=i, mcol=mcol: eng.matmul(
                PSBIG[:, mcol:mcol + 128],
                sb["xn2T"][:, i * 2048 + c * 128:i * 2048 + (c + 1) * 128],
                sb["loraA"][:, c * 128:(c + 1) * 128],
                start=(c == 0), stop=(c == 15)),
                waits=wm, inc="pe")
            wl2 = []
            if c == 0:
                wl2 = [ev_rt, ev_lge[i - 2] if i >= 2 else None]
            evlg = p.emit("tensor", lambda eng, c=c, i=i, lcol=lcol: eng.matmul(
                PSBIG[:, lcol:lcol + 8],
                sb["xn2T"][:, i * 2048 + c * 128:i * 2048 + (c + 1) * 128],
                sb["routerT"][:, c * 8:(c + 1) * 8],
                start=(c == 0), stop=(c == 15)),
                waits=wl2, inc="pe")
        ev_mme[i] = p.emit("vector", lambda eng, i=i, mcol=mcol: eng.tensor_copy(
            sb["midall"][:, i * 128:(i + 1) * 128], PSBIG[:, mcol:mcol + 128]),
            waits=[evm], inc="dve", nochain=True)
        ev_lge[i] = p.emit("vector", lambda eng, i=i, lcol=lcol: eng.tensor_add(
            sb["lgall"][:, i * 8:(i + 1) * 8], PSBIG[:, lcol:lcol + 8], sb["rbb"][:, :]),
            waits=[evlg], inc="dve", nochain=True)
    def v16(ap):
        return ap.rearrange("p (t e) -> p t e", t=16)

    p.emit("vector", lambda eng: eng.reduce_max(
        sb["mx16"][:, :].unsqueeze(2), v16(sb["lgall"][:, :]), AX.X),
        waits=[ev_lge[15]], inc="dve")
    ev_p2s = p.emit("vector", lambda eng: eng.tensor_sub(
        v16(sb["lgall"][:, :]), v16(sb["lgall"][:, :]),
        sb["mx16"][:, :].unsqueeze(2).broadcast_to([128, 16, 8])), inc="dve")
    ev_p2e = p.emit("scalar", lambda eng: eng.activation(
        sb["lgall"][:, :], sb["lgall"][:, :], AF.Exp), waits=[ev_p2s], inc="act")
    p.emit("vector", lambda eng: eng.reduce_max(
        sb["mx16"][:, :].unsqueeze(2), v16(sb["lgall"][:, :]), AX.X),
        waits=[ev_p2e], inc="dve")
    p.emit("vector", lambda eng: eng.tensor_tensor(
        v16(sb["gmaskb"][:, :]), v16(sb["lgall"][:, :]),
        sb["mx16"][:, :].unsqueeze(2).broadcast_to([128, 16, 8]), ALU.is_ge), inc="dve")
    p.emit("vector", lambda eng: eng.scalar_tensor_tensor(
        sb["gateb"][:, :], sb["gmaskb"][:, :], -2.0, sb["lgall"][:, :],
        ALU.mult, ALU.add), inc="dve")
    p.emit("vector", lambda eng: eng.reduce_max(
        sb["m2_16"][:, :].unsqueeze(2), v16(sb["gateb"][:, :]), AX.X), inc="dve")
    p.emit("vector", lambda eng: eng.tensor_tensor(
        v16(sb["gmaskb"][:, :]), v16(sb["lgall"][:, :]),
        sb["m2_16"][:, :].unsqueeze(2).broadcast_to([128, 16, 8]), ALU.is_ge), inc="dve")
    p.emit("vector", lambda eng: eng.tensor_mul(
        sb["gateb"][:, :], sb["lgall"][:, :], sb["gmaskb"][:, :]), inc="dve")
    p.emit("vector", lambda eng: eng.reduce_sum(
        sb["gs16"][:, :].unsqueeze(2), v16(sb["gateb"][:, :]), AX.X), inc="dve")
    p.emit("vector", lambda eng: eng.reciprocal(sb["rg16"][:, :], sb["gs16"][:, :]), inc="dve")
    ev_g2f = p.emit("vector", lambda eng: eng.scalar_tensor_tensor(
        v16(sb["gateb"][:, :]), v16(sb["gateb"][:, :]), 2.0,
        sb["rg16"][:, :].unsqueeze(2).broadcast_to([128, 16, 8]),
        ALU.mult, ALU.mult), inc="dve")
    ev_mg = [None] * 16
    for i2 in range(16):
        ev_mg[i2] = p.emit("vector", lambda eng, i2=i2: eng.tensor_tensor(
            sb["midall"][:, i2 * 128:(i2 + 1) * 128].rearrange("p (e r) -> p e r", e=8),
            sb["midall"][:, i2 * 128:(i2 + 1) * 128].rearrange("p (e r) -> p e r", e=8),
            sb["gateb"][:, i2 * 8:(i2 + 1) * 8].unsqueeze(2).broadcast_to([128, 8, 16]),
            ALU.mult), waits=[ev_g2f], inc="dve", nochain=True)

    ev_store = [None] * 16
    ev_mtc = [None] * 16
    ev_lbm = [None] * 16
    prev_epn = None
    for ti in range(16):
        row0 = ti * 128
        bf = ti % 2
        enq, qn = ("sync", "qsp") if bf == 0 else ("scalar", "qsc")
        wx = [ev_ts2[14 + bf]] if ti < 2 else [ev_epn_map[ti - 2][0]]
        ev_xld = p.emit(enq, lambda eng, row0=row0, bf=bf: eng.dma_start(
            out=sb["xti"][:, bf], in_=x2_dram[row0:row0 + 128, :]),
            waits=wx, inc=qn, n=16)
        evb = None
        for c in range(16):
            for nn in range(4):
                wb_ = []
                if nn == 0:
                    wb_ = [ev_wb[c]]
                if c == 0:
                    wb_.append(ev_lge[15] if ti == 0 else prev_epn[nn])
                evb = p.emit("tensor", lambda eng, c=c, nn=nn, ti=ti: eng.matmul(
                    PSBIG[:, nn * 512:(nn + 1) * 512],
                    sb["xn2T"][:, ti * 2048 + c * 128:ti * 2048 + (c + 1) * 128],
                    sb["wbase"][:, c * 2048 + nn * 512:c * 2048 + (nn + 1) * 512],
                    start=(c == 0), stop=False),
                    waits=wb_, inc="pe")
        for nn in range(4):
            evb = p.emit("tensor", lambda eng, nn=nn: eng.matmul(
                PSBIG[:, nn * 512:(nn + 1) * 512], sb["onestok"][0:1, :],
                sb["bbase_row"][0:1, nn * 512:(nn + 1) * 512],
                start=False, stop=False), waits=[ev_w5s] if nn == 0 else [], inc="pe")
        evtr5 = p.emit("tensor", lambda eng, ti=ti, bf=bf: eng.transpose(
            (PM1b if bf == 0 else PN_Ab)[:, 0:128],
            sb["midall"][:, ti * 128:(ti + 1) * 128], sb["ident"][:, :]),
            waits=[ev_mg[ti], ev_mtc[ti - 2] if ti >= 2 else None], inc="pe")
        ev_mtc[ti] = p.emit("vector", lambda eng, ti=ti, bf=bf: eng.tensor_copy(
            sb["midT2"][:, bf], (PM1b if bf == 0 else PN_Ab)[:, 0:128]),
            waits=[evtr5, ev_lbm[ti - 2] if ti >= 2 else None], inc="dve", nochain=True)
        evlb = None
        for nn in range(4):
            wlb = [ev_mtc[ti], ev_lorab] if nn == 0 else []
            evlb = p.emit("tensor", lambda eng, nn=nn, bf=bf: eng.matmul(
                PSBIG[:, nn * 512:(nn + 1) * 512], sb["midT2"][:, bf],
                sb["loraB"][:, nn * 512:(nn + 1) * 512],
                start=False, stop=True), waits=wlb, inc="pe")
        ev_lbm[ti] = evlb
        ev_epn = []
        for nn in range(4):
            w2 = [evlb, ev_xld] if nn == 0 else []
            if nn == 0 and ti >= 2:
                w2.append(ev_store[ti - 2])
            ev_epn.append(p.emit("vector", lambda eng, nn=nn, bf=bf: eng.tensor_add(
                sb["outt"][:, bf][:, nn * 512:(nn + 1) * 512],
                PSBIG[:, nn * 512:(nn + 1) * 512],
                sb["xti"][:, bf][:, nn * 512:(nn + 1) * 512]), waits=w2, inc="dve"))
        prev_epn = ev_epn
        if ti == 0:
            ev_epn_map = {}
        ev_epn_map[ti] = ev_epn
        ev_store[ti] = p.emit(enq, lambda eng, row0=row0, bf=bf: eng.dma_start(
            out=out_dram[row0:row0 + 128, :], in_=sb["outt"][:, bf]),
            waits=[ev_epn[3]], inc=qn, n=16)

    p.emit("vector", lambda eng: eng.engine_nop(), waits=[ev_store[-1]])
    p.build()
    return nc


def _rope_tables():
    e = np.arange(0, 32, 2, dtype=np.float32) / np.float32(32)
    inv = (np.float32(1.0) / np.float32(10000.0) ** e).astype(np.float32)
    t = np.arange(S, dtype=np.float32)
    fr = np.outer(t, inv).astype(np.float32)
    return np.cos(fr).astype(np.float32), np.sin(fr).astype(np.float32)


def _pair_heads(t):
    kt, i = divmod(t, 4)
    return 8 * kt + i, 8 * kt + 4 + i


def _qcol_perm():
    cols = []
    for t in range(16):
        hA, hB = _pair_heads(t)
        cols += list(range(hA * 64, hA * 64 + 16))
        cols += list(range(hB * 64, hB * 64 + 16))
        cols += list(range(hA * 64 + 16, hA * 64 + 32))
        cols += list(range(hB * 64 + 16, hB * 64 + 32))
        cols += list(range(hA * 64 + 32, hA * 64 + 64))
        cols += list(range(hB * 64 + 32, hB * 64 + 64))
    return np.array(cols)


def _kcol_perm():
    cols = []
    for kt in range(4):
        a, b = 2 * kt, 2 * kt + 1
        cols += list(range(a * 64, a * 64 + 16))
        cols += list(range(b * 64, b * 64 + 16))
        cols += list(range(a * 64 + 16, a * 64 + 32))
        cols += list(range(b * 64 + 16, b * 64 + 32))
        cols += list(range(a * 64 + 32, a * 64 + 64))
        cols += list(range(b * 64 + 32, b * 64 + 64))
    return np.array(cols)


def _grot_gpass():
    G = np.zeros((128, 128), dtype=np.float32)
    P = np.zeros((128, 128), dtype=np.float32)
    for i in range(16):
        G[i, i] = 1.0
        G[32 + i, i] = -1.0
        G[64 + i, 16 + i] = 1.0
        G[96 + i, 16 + i] = 1.0
        G[16 + i, 64 + i] = 1.0
        G[48 + i, 64 + i] = -1.0
        G[80 + i, 80 + i] = 1.0
        G[112 + i, 80 + i] = 1.0
    for i in range(32):
        P[64 + i, 32 + i] = 1.0
        P[96 + i, 96 + i] = 1.0
    return G.astype(BF), P.astype(BF)


def _wout_row_perm():
    rows = np.zeros(2048, dtype=np.int64)
    for t in range(16):
        hA, hB = _pair_heads(t)
        rows[t * 128: t * 128 + 64] = np.arange(hA * 64, hA * 64 + 64)
        rows[t * 128 + 64: t * 128 + 128] = np.arange(hB * 64, hB * 64 + 64)
    return rows


def _tile4(w):
    d, n = w.shape
    nt = n // 128
    r = w.reshape(16, 128, nt, 128)
    return np.ascontiguousarray(r.transpose(2, 1, 0, 3))


def _masks():
    k = np.arange(128)[:, None]
    q = np.arange(512)[None, :]
    out = []
    for j in range(4):
        mj = (q >= 128 * j + k).astype(np.float32).astype(BF)
        out.append(np.concatenate([mj, mj], axis=1))
    return np.ascontiguousarray(np.concatenate(out, axis=1))


def _biasmat(par):
    bm = np.full((128, 80), SOFT_SHIFT, dtype=np.float32)
    if par == 0:
        base = 0
        for s_ in range(4):
            Ks = KS[s_]
            bm[:, base + Ks - 4: base + Ks] = -10000.0
            base += Ks
    return bm


def host_prepare(inputs):
    x = np.asarray(inputs["L0_attn_x"], dtype=np.float32)
    Wqkv = np.asarray(inputs["L0_attn_Wqkv"], dtype=np.float32)
    bqkv = np.asarray(inputs["L0_attn_bqkv"], dtype=np.float32)
    Wout = np.asarray(inputs["L0_attn_Wout"], dtype=np.float32)
    bout = np.asarray(inputs["L0_attn_bout"], dtype=np.float32)
    base_w = np.asarray(inputs["L0_ffn_base_w"], dtype=np.float32)
    base_b = np.asarray(inputs["L0_ffn_base_b"], dtype=np.float32)
    router_w = np.asarray(inputs["L0_ffn_router_w"], dtype=np.float32)
    router_b = np.asarray(inputs["L0_ffn_router_b"], dtype=np.float32)
    lora_A = np.asarray(inputs["L0_ffn_lora_A"], dtype=np.float32)
    lora_B = np.asarray(inputs["L0_ffn_lora_B"], dtype=np.float32)

    cos, sin = _rope_tables()
    qperm = _qcol_perm()
    kperm = _kcol_perm()
    woutperm = _wout_row_perm()

    wq_h = _tile4(np.ascontiguousarray(Wqkv[:, :2048][:, qperm]))
    wk_h = _tile4(np.ascontiguousarray(Wqkv[:, 2048:2560][:, kperm]))
    wv_h = _tile4(np.ascontiguousarray(Wqkv[:, 2560:3072]))
    wout_h = np.ascontiguousarray(Wout[woutperm].reshape(16, 128, D))
    wbaseT_h = np.ascontiguousarray(base_w.T.reshape(16, 128, D))
    A_all = np.ascontiguousarray(lora_A.transpose(1, 0, 2).reshape(D, 128))
    loraA_h = np.ascontiguousarray(A_all.reshape(16, 128, 128).transpose(1, 0, 2))
    loraB_h = np.ascontiguousarray(lora_B.reshape(128, D))
    routerT_h = np.ascontiguousarray(router_w.T.reshape(16, 128, 8).transpose(1, 0, 2)).astype(BF)

    bq_h = np.ascontiguousarray(bqkv[:2048][qperm].reshape(16, 128).T)
    bk_h = np.ascontiguousarray(bqkv[2048:2560][kperm].reshape(4, 128).T)
    bv_h = np.ascontiguousarray(bqkv[2560:3072].reshape(4, 128).T)
    rbb_h = np.ascontiguousarray(np.broadcast_to(router_b[None, :], (128, 8)))
    grot, gpass = _grot_gpass()
    shared = {
        "masks": _masks(), "grot": grot, "gpass": gpass, "ident": np.eye(128, dtype=BF),
        "ones1": np.ones((1, 64), dtype=np.float32),
        "onestok": np.ones((1, 128), dtype=BF),
        "wq": wq_h, "wk": wk_h, "wv": wv_h, "wout": wout_h,
        "wbaseT": wbaseT_h, "loraA": loraA_h, "loraB": loraB_h,
        "routerT": routerT_h, "bq": bq_h, "bk": bk_h, "bv": bv_h,
        "rbb": rbb_h, "bout_row": bout[None, :].astype(BF),
        "bbase_row": base_b[None, :].astype(BF),
    }

    in_maps = []
    metas = []
    for core in range(8):
        b, par = divmod(core, 2)
        own = [2 * i + par for i in range(4)]
        partner = [2 * i + (1 - par) for i in range(4)]
        rows = np.concatenate([np.arange(k * CH, (k + 1) * CH) for k in own + partner])
        xp = np.ascontiguousarray(x[b][rows])
        ct = cos[rows].T
        st_ = sin[rows].T
        ccss = np.ascontiguousarray(np.concatenate([ct, ct, st_, st_], axis=0))
        sscc = np.ascontiguousarray(np.concatenate([st_, st_, ct, ct], axis=0))
        m = dict(shared)
        m.update({"xp": xp, "ccss": ccss, "sscc": sscc, "biasmat": _biasmat(par)})
        in_maps.append(m)
        metas.append((b, own))
    return in_maps, metas


def kernel(**inputs):
    if "nc" not in _CACHE:
        _CACHE["nc"] = build_nc()
    nc = _CACHE["nc"]
    in_maps, metas = host_prepare(inputs)
    res = run_bass_kernel_spmd(nc, in_maps, core_ids=list(range(8)))
    _CACHE["last"] = res
    out = np.zeros((B, S, D), dtype=np.float32)
    for core in range(8):
        b, own = metas[core]
        o = np.asarray(res.results[core]["out"], dtype=np.float32)
        for i, k in enumerate(own):
            out[b, k * CH:(k + 1) * CH] = o[i * CH:(i + 1) * CH]
    return out


# version 17
# speedup vs baseline: 1.2488x; 1.0147x over previous
import numpy as np
import ml_dtypes
import concourse.bass as bass
import concourse.mybir as mybir
from concourse.bass_utils import run_bass_kernel_spmd

F32 = mybir.dt.float32
BF16 = mybir.dt.bfloat16
AF = mybir.ActivationFunctionType
ALU = mybir.AluOpType
AX = mybir.AxisListType
BF = ml_dtypes.bfloat16

B, S, D = 4, 4096, 2048
CH = 512
T_OWN = 2048
P_ALL = 4096
NSLOT = 4
KS = [8, 16, 24, 32]
SOFT_SHIFT = -8.0

_TAPS = False
_PHASES = 5
_CACHE = {}


class Prog:
    ENGINES = ("sync", "scalar", "vector", "tensor", "gpsimd")

    def __init__(self, nc):
        self.nc = nc
        self.steps = []
        self.sems = {}
        self.counts = {}
        self.last_ev = {}
        self._pending = {}

    def sem(self, name):
        if name not in self.sems:
            self.sems[name] = self.nc.alloc_semaphore(name)
            self.counts[name] = 0
        return self.sems[name]

    def emit(self, engine, fn, waits=(), inc=None, n=1, nochain=False):
        w = [x for x in waits if x is not None]
        if engine in self._pending:
            w += self._pending.pop(engine)
        ev = None
        incs = []
        if inc is not None:
            self.sem(inc)
            self.counts[inc] += n
            ev = (inc, self.counts[inc])
            incs = [(inc, n)]
            self.last_ev[engine] = ev
        self.steps.append((engine, fn, w, incs, nochain))
        return ev

    def barrier(self):
        evs = list(self.last_ev.values())
        for e in self.ENGINES:
            self._pending[e] = list(evs)

    def build(self):
        nc = self.nc
        with nc.Block() as block:
            for eng_name in self.ENGINES:
                if not any(s[0] == eng_name for s in self.steps):
                    continue

                def make(en):
                    def emit_stream(eng):
                        waited = {}
                        own = {"vector": "dve", "scalar": "act"}.get(en)
                        cnt = 0
                        for (e, fn, waits, incs, nochain) in self.steps:
                            if e != en:
                                continue
                            if own is not None and cnt > 0 and own in self.sems \
                                    and not nochain and waited.get(own, -1) < cnt:
                                eng.wait_ge(self.sems[own], cnt)
                                waited[own] = cnt
                            for (s, v) in waits:
                                if waited.get(s, -1) < v:
                                    eng.wait_ge(self.sems[s], v)
                                    waited[s] = v
                            ins = fn(eng)
                            for (s, a) in incs:
                                ins = ins.then_inc(self.sems[s], a)
                                if s == own:
                                    cnt += a
                    return emit_stream

                getattr(block, eng_name)(make(eng_name))


def build_nc():
    nc = bass.Bass("TRN2", target_bir_lowering=False)
    p = Prog(nc)

    di = {}

    def inp(name, shape, dt):
        di[name] = nc.dram_tensor(name, list(shape), dt, kind="ExternalInput").ap()

    inp("xp", [P_ALL, D], F32)
    inp("ccss", [64, P_ALL], F32)
    inp("sscc", [64, P_ALL], F32)
    inp("biasmat", [128, 80], F32)
    inp("masks", [128, 4096], BF16)
    inp("grot", [128, 128], BF16)
    inp("gpass", [128, 128], BF16)
    inp("ident", [128, 128], BF16)
    inp("ones1", [1, 64], F32)
    inp("onestok", [1, 128], BF16)
    inp("wq", [16, 128, 16, 128], F32)
    inp("wk", [4, 128, 16, 128], F32)
    inp("wv", [4, 128, 16, 128], F32)
    inp("wout", [16, 128, D], F32)
    inp("wbaseT", [16, 128, D], F32)
    inp("loraA", [128, 16, 128], F32)
    inp("loraB", [128, D], F32)
    inp("routerT", [128, 16, 8], BF16)
    inp("bq", [128, 16], F32)
    inp("bk", [128, 4], F32)
    inp("bv", [128, 4], F32)
    inp("rbb", [128, 8], F32)
    inp("bout_row", [1, D], BF16)
    inp("bbase_row", [1, D], BF16)

    ikind = "ExternalOutput" if _TAPS else "Internal"
    qt_dram = nc.dram_tensor("qt_dram", [16, 128, T_OWN], BF16, kind=ikind).ap()
    at_dram = nc.dram_tensor("at_dram", [16, 128, T_OWN], BF16, kind=ikind).ap()
    x2_dram = nc.dram_tensor("x2_dram", [T_OWN, D], F32, kind=ikind).ap()
    xn2_dram = nc.dram_tensor("xn2_dram", [T_OWN, D], BF16, kind=ikind).ap()
    dn_dram = nc.dram_tensor("dn_dram", [NSLOT, 16384], F32, kind=ikind).ap()
    rr_dram = nc.dram_tensor("rr_dram", [NSLOT, 16384], F32, kind="Internal").ap()
    out_dram = nc.dram_tensor("out", [T_OWN, D], F32, kind="ExternalOutput").ap()

    sb = {}
    cursors = {}

    def sba(name, shape, dt, region, reg_start):
        if region not in cursors:
            cursors[region] = reg_start
        esz = 2 if dt == BF16 else 4
        nb = 1
        for s_ in shape[1:]:
            nb *= s_
        nb *= esz
        off = cursors[region]
        cursors[region] = off + ((nb + 31) // 32) * 32
        sb[name] = nc.alloc_sbuf_tensor_at(name, list(shape), dt, offset=off)
        return sb[name]

    K0 = 18432
    A0 = 38912
    B0 = 108544
    C0 = 178176
    LIMIT = 229376

    sba("ident", [128, 128], BF16, "k", K0)
    sba("grot", [128, 128], BF16, "k", K0)
    sba("gpass", [128, 128], BF16, "k", K0)
    sba("ones1", [1, 64], F32, "k", K0)
    sba("onestok", [1, 128], BF16, "k", K0)
    sba("bq", [128, 16], F32, "k", K0)
    sba("bk", [128, 4], F32, "k", K0)
    sba("bv", [128, 4], F32, "k", K0)
    sba("rbb", [128, 8], F32, "k", K0)
    sba("biasmat", [128, 80], F32, "k", K0)
    sba("masks", [128, 4096], BF16, "k", K0)
    sba("trep", [128, 8], F32, "k", K0)
    sba("mvall", [128, 64], F32, "k", K0)
    sba("sqw", [128, 16], F32, "k", K0)
    sba("var32", [128, 32], F32, "k", K0)
    sba("rstd32", [128, 32], F32, "k", K0)
    sba("m32", [128, 32], F32, "k", K0)
    off_ccs = cursors["k"]
    sba("ccs", [64, 2, 512], F32, "k", K0)
    sba("ssc", [64, 2, 512], F32, "k", K0)

    sba("KT", [128, 4 * P_ALL], BF16, "A23", A0)
    sba("VP", [128, 32 * 520], BF16, "A23", A0)
    sba("rrsl", [128, 32], F32, "A23", A0)
    sba("u", [64, 512], F32, "A23", A0)
    sba("vf", [128, 512], BF16, "A23", A0)
    sba("xn2T", [128, 16 * T_OWN], BF16, "A5", A0)
    assert max(cursors["A23"], cursors["A5"]) <= B0

    sba("wkv", [128, 8 * D], BF16, "B2", B0)
    sba("xn1T", [128, 2, 16 * 512], BF16, "B2", B0)
    sba("dnrow", [1, 1024], F32, "B2", B0)
    sba("wout", [128, 16 * D], BF16, "B4", B0)
    sba("wbase", [128, 16 * D], BF16, "B5", B0)
    assert max(cursors["B2"], cursors["B4"], cursors["B5"]) <= C0

    sba("xpt", [128, 2, D], F32, "C1", C0)
    sba("xnt", [128, 2, D], BF16, "C1", C0)
    sba("xsq", [128, D], BF16, "C1", C0)
    sba("mv", [128, 2], F32, "k", None)
    sba("sq1", [128, 1], F32, "k", None)
    sba("stdc", [128, 1], F32, "k", None)
    sba("rstd", [128, 1], F32, "k", None)
    C2S = C0 + 32768
    sba("wqs", [128, 2, 2048], BF16, "C2", C2S)
    sba("wrope", [128, 512], BF16, "C2", C2S)
    sba("qrope", [128, 512], BF16, "C2", C2S)
    sba("qstage", [128, 2, 512], BF16, "C2", C2S)
    sba("PT", [128, 4, 1024], BF16, "C3a", C0)
    C3B = C0 + 8192
    sba("QTp", [128, 2, 512], BF16, "C3b", C3B)
    sba("Anum", [128, 4 * 512], F32, "C3b", C3B)
    sba("AT", [128, 4 * 512], BF16, "C3b", C3B)
    sba("rrow", [128, 4, 512], F32, "C3b", C3B)
    sb["atg0"] = nc.alloc_sbuf_tensor_at("atg0", [128, 16 * 512], BF16, offset=C2S)
    sb["atg1"] = nc.alloc_sbuf_tensor_at("atg1", [128, 16 * 512], BF16, offset=A0 + 32768)
    sba("x2t", [128, 2, D], F32, "C4", C0)
    sba("xn2t", [128, 2, D], BF16, "C4", C0)
    sba("xsq4", [128, D], F32, "C4", C0)
    sb["bout_row"] = nc.alloc_sbuf_tensor_at("bout_row", [1, D], BF16, offset=off_ccs)
    sba("xti", [128, 2, D], F32, "C5", C0)
    sba("outt", [128, 2, D], F32, "C5", C0)
    sba("loraA", [128, 16 * 128], BF16, "C5", C0)
    sba("loraB", [128, D], BF16, "C5", C0)
    sba("routerT", [128, 128], BF16, "C5", C0)
    sba("midall", [128, 16 * 128], BF16, "C5", C0)
    sba("midT2", [128, 2, 128], BF16, "C5", C0)
    sba("bbase_row", [1, D], BF16, "C5", C0)
    C5G = C0 + 24576
    sba("lgall", [128, 128], F32, "C5g", C5G)
    sba("gmaskb", [128, 128], F32, "C5g", C5G)
    sba("gateb", [128, 128], F32, "C5g", C5G)
    sba("mx16", [128, 16], F32, "C5g", C5G)
    sba("m2_16", [128, 16], F32, "C5g", C5G)
    sba("gs16", [128, 16], F32, "C5g", C5G)
    sba("rg16", [128, 16], F32, "C5g", C5G)
    assert cursors["C5g"] <= C0 + 32768
    assert cursors["k"] <= A0, cursors["k"]
    for r in ("C1", "C2", "C3b", "C4", "C5", "k", "A23"):
        assert cursors[r] <= LIMIT, (r, cursors[r])

    PSBIG = nc.alloc_psum_tensor("PSBIG", [128, 2048], F32)
    PN_A = nc.alloc_psum_tensor("PN_A", [128, 512], F32)
    PN_B = nc.alloc_psum_tensor("PN_B", [128, 512], F32)
    PM1 = nc.alloc_psum_tensor("PM1", [128, 512], F32)
    PM2 = nc.alloc_psum_tensor("PM2", [128, 512], F32)
    PM1b = PM1[:, :].bitcast(BF16)
    PM2b = PM2[:, :].bitcast(BF16)
    PN_Ab = PN_A[:, :].bitcast(BF16)
    PN_Bb = PN_B[:, :].bitcast(BF16)
    PSBIGb = PSBIG[:, :].bitcast(BF16)

    for name in ["ident", "grot", "gpass", "ones1", "onestok", "bq", "bk", "bv", "rbb",
                 "biasmat", "masks"]:
        p.emit("sync", lambda eng, n=name: eng.dma_start(out=sb[n][:, :], in_=di[n]),
               inc="qsp", n=16)
    ld0 = ("qsp", p.counts["qsp"])
    p.emit("vector", lambda eng: eng.memset(sb["trep"][:, 2:8], -1.0), inc="dve")
    p.emit("vector", lambda eng: eng.memset(sb["qrope"][0:64, :], 0.0), inc="dve")
    ev_vones = p.emit("vector", lambda eng: eng.memset(
        sb["VP"][:, :].rearrange("p (k g c) -> p k g c", k=32, g=8)[:, :, :, 64:65], 1.0),
        inc="dve")

    NT1 = P_ALL // 128
    ev_tr1 = [None] * NT1
    ev_ev1 = [None] * NT1
    ev_sqr = [None] * (NT1 + 16)
    ev_xn = [None] * NT1

    def emit_p1a_tile(i):
        bf = i % 2
        evl = p.emit("sync", lambda eng, i=i, bf=bf: eng.dma_start(
            out=sb["xpt"][:, bf], in_=di["xp"][i * 128:(i + 1) * 128, :]),
            waits=[ev_sqr[i - 2]] if i % 8 >= 2 else [ev_xn[i - 2]], inc="qsp", n=16)
        ev_q = p.emit("scalar", lambda eng, bf=bf: eng.activation(
            sb["xsq"][:, :], sb["xpt"][:, bf], AF.Square),
            waits=[evl, ev_sqr[i - 1] if i >= 1 else None], inc="act", nochain=True)
        p.emit("vector", lambda eng, i=i, bf=bf: eng.reduce_sum(
            sb["mvall"][:, 2 * (i % 16):2 * (i % 16) + 1], sb["xpt"][:, bf], AX.X),
            waits=[evl], inc="dve", nochain=True)
        ev_sqr[i] = p.emit("vector", lambda eng, i=i: eng.reduce_sum(
            sb["mvall"][:, 2 * (i % 16) + 1:2 * (i % 16) + 2], sb["xsq"][:, :], AX.X),
            waits=[ev_q], inc="dve", nochain=True)

    def emit_p1b_wave(w):
        c0 = 16 * (w % 2)
        ev_o1 = p.emit("vector", lambda eng, c0=c0: eng.scalar_tensor_tensor(
            sb["sqw"][:, 0:8], sb["mvall"][:, c0:c0 + 16:2],
            -1.0 / (2048.0 * 2048.0), sb["mvall"][:, c0:c0 + 16:2],
            ALU.mult, ALU.mult), waits=[ev_sqr[8 * w + 7]], inc="dve")
        ev_o2 = p.emit("vector", lambda eng, w=w, c0=c0: eng.scalar_tensor_tensor(
            sb["var32"][:, 8 * w:8 * w + 8], sb["mvall"][:, c0 + 1:c0 + 16:2],
            1.0 / 2048.0, sb["sqw"][:, 0:8], ALU.mult, ALU.add), inc="dve")
        ev_o3 = p.emit("vector", lambda eng, w=w: eng.tensor_scalar(
            sb["var32"][:, 8 * w:8 * w + 8], sb["var32"][:, 8 * w:8 * w + 8],
            1e-5, None, ALU.add), inc="dve")
        ev_ln = p.emit("scalar", lambda eng, w=w: eng.activation(
            sb["var32"][:, 8 * w:8 * w + 8], sb["var32"][:, 8 * w:8 * w + 8],
            AF.Ln), waits=[ev_o3], inc="act")
        ev_ex = p.emit("scalar", lambda eng, w=w: eng.activation(
            sb["rstd32"][:, 8 * w:8 * w + 8], sb["var32"][:, 8 * w:8 * w + 8],
            AF.Exp, scale=-0.5), inc="act")
        ev_m = p.emit("vector", lambda eng, w=w, c0=c0: eng.tensor_scalar(
            sb["m32"][:, 8 * w:8 * w + 8], sb["mvall"][:, c0:c0 + 16:2],
            1.0 / 2048.0, None, ALU.mult), waits=[ev_ex], inc="dve")
        return ev_m

    def emit_p1c_tile(i, ev_w):
        bf = i % 2
        g, j = divmod(i, 4)
        evl = p.emit("scalar", lambda eng, i=i, bf=bf: eng.dma_start(
            out=sb["xpt"][:, bf], in_=di["xp"][i * 128:(i + 1) * 128, :]),
            waits=[ev_xn[i - 2], ev_sqr[(i // 8) * 8 + 7]], inc="qsc", n=16)
        ev_xn[i] = p.emit("vector", lambda eng, i=i, bf=bf: eng.tensor_scalar(
            sb["xnt"][:, bf], sb["xpt"][:, bf], sb["m32"][:, i:i + 1],
            sb["rstd32"][:, i:i + 1], ALU.subtract, ALU.mult),
            waits=[evl, ev_w, ev_tr1[i - 2]], inc="dve", nochain=True)
        evt = None
        for c in range(16):
            wtr = [ev_xn[i], ev_ev1[i - 2]] if c == 0 else []
            evt = p.emit("tensor", lambda eng, i=i, c=c, bf=bf: eng.transpose(
                PSBIGb[:, (i % 2) * 2048 + c * 128:(i % 2) * 2048 + (c + 1) * 128],
                sb["xnt"][:, bf][:, c * 128:(c + 1) * 128], sb["ident"][:, :]),
                waits=wtr, inc="pe")
        ev_tr1[i] = evt
        wev = [ev_tr1[i]]
        if j == 0 and g >= 2:
            wev.append(ev_xn1t_done[g - 2])
        ev_ev1[i] = p.emit("vector", lambda eng, i=i, g=g, j=j: eng.tensor_copy(
            sb["xn1T"][:, g % 2].rearrange("p (c x) -> p c x", c=16)[:, :, j * 128:(j + 1) * 128],
            PSBIGb[:, (i % 2) * 2048:(i % 2 + 1) * 2048].rearrange("p (c x) -> p c x", c=16)),
            waits=wev, inc="dve", nochain=True)

    for t in range(4):
        p.emit("gpsimd", lambda eng, t=t: eng.dma_start(
            out=sb["wkv"][:, t * 2048:(t + 1) * 2048],
            in_=di["wk"][t].rearrange("p c j -> p (c j)")), inc="qgp", n=16)
        p.emit("gpsimd", lambda eng, t=t: eng.dma_start(
            out=sb["wkv"][:, (4 + t) * 2048:(5 + t) * 2048],
            in_=di["wv"][t].rearrange("p c j -> p (c j)")), inc="qgp", n=16)
    ev_wkv = ("qgp", p.counts["qgp"])

    NG = P_ALL // 512
    ev_kv_ready = [None] * NG
    ev_xn1t = [None] * NG
    ev_xn1t_done = [None] * NG
    ev_cs = [None] * NG
    ev_cs_done = [None] * NG
    ev_qspill = {}
    qload = {}
    qdone = {}
    prev_perm = None
    prev_vtr = None
    ps_free = {"PN_A": None, "PN_B": None, "PM1": None, "PM2": None}

    def emit_qload(qi):
        g, t = divmod(qi, 16)
        w = [qdone.get(qi - 2), ev_wkv if qi == 0 else None]
        qload[qi] = p.emit("gpsimd", lambda eng, t=t: eng.dma_start(
            out=sb["wqs"][:, t % 2],
            in_=di["wq"][t].rearrange("p c j -> p (c j)")),
            waits=w, inc="qgp", n=16)

    emit_qload(0)
    emit_qload(1)

    def emit_p2_group(g):
        nonlocal prev_vtr
        own = g < 4
        bf = g % 2
        ev_xn1t[g] = ev_ev1[4 * g + 3]
        csw = [ev_cs_done[g - 2]] if g >= 2 else []
        p.emit("sync", lambda eng, g=g, bf=bf: eng.dma_start(
            out=sb["ccs"][:, bf], in_=di["ccss"][:, g * 512:(g + 1) * 512]),
            waits=csw, inc="qsp", n=16)
        ev_cs[g] = p.emit("sync", lambda eng, g=g, bf=bf: eng.dma_start(
            out=sb["ssc"][:, bf], in_=di["sscc"][:, g * 512:(g + 1) * 512]),
            inc="qsp", n=16)

        def proj_mms(psum_ap, w_fn, fw):
            ev = None
            for c in range(16):
                ev = p.emit("tensor", lambda eng, c=c, psum_ap=psum_ap, w_fn=w_fn, bf=bf: eng.matmul(
                    psum_ap, w_fn(c), sb["xn1T"][:, bf][:, c * 512:(c + 1) * 512],
                    start=(c == 0), stop=(c == 15)),
                    waits=list(fw) if c == 0 else [], inc="pe")
            return ev

        def rope_evac(psrc, bias_col, dst_copy_fn, extra_pm_waits, ev_mm):
            nonlocal prev_perm
            ev_u = p.emit("vector", lambda eng, psrc=psrc, bias_col=bias_col: eng.tensor_scalar(
                sb["u"][:, :], psrc[0:64, :], bias_col[0:64, :], None, ALU.add),
                waits=[ev_cs[g], prev_perm, ev_mm], inc="dve")
            ev_pass = p.emit("vector", lambda eng, psrc=psrc, bias_col=bias_col: eng.tensor_scalar(
                sb["qrope"][64:128, :], psrc[64:128, :], bias_col[64:128, :], None, ALU.add),
                inc="dve")
            p.emit("vector", lambda eng, bf=bf: eng.tensor_mul(
                sb["wrope"][0:64, :], sb["u"][:, :], sb["ccs"][:, bf]), inc="dve")
            ev_b = p.emit("vector", lambda eng, bf=bf: eng.tensor_mul(
                sb["wrope"][64:128, :], sb["u"][:, :], sb["ssc"][:, bf]), inc="dve")
            ev_cs_done[g] = ev_b
            p.emit("tensor", lambda eng: eng.matmul(
                PM2[:, :], sb["grot"][:, :], sb["wrope"][:, :], start=True, stop=False),
                waits=[ev_b, ps_free["PM2"]] + list(extra_pm_waits), inc="pe")
            ev_pm = p.emit("tensor", lambda eng: eng.matmul(
                PM2[:, :], sb["gpass"][:, :], sb["qrope"][:, :], start=False, stop=True),
                inc="pe")
            prev_perm = ev_pm
            ev_cp = dst_copy_fn(ev_pm)
            ps_free["PM2"] = ev_cp
            return ev_pass, ev_cp

        for kt in range(4):
            fw = [ev_wkv, ev_xn1t[g], ps_free["PN_A"]]
            if g == 0 and kt == 0:
                fw.append(ld0)
            ev_mm = proj_mms(PN_A[:, :],
                             lambda c, kt=kt: sb["wkv"][:, kt * 2048 + c * 128: kt * 2048 + (c + 1) * 128],
                             fw)
            def kcopy(ev_pm, kt=kt, g=g):
                return p.emit("vector", lambda eng, kt=kt, g=g: eng.tensor_copy(
                    sb["KT"][:, kt * P_ALL + g * 512: kt * P_ALL + (g + 1) * 512], PM2[:, :]),
                    waits=[ev_pm], inc="dve")
            ev_pass, _ = rope_evac(PN_A, sb["bk"][:, kt:kt + 1], kcopy, [], ev_mm)
            ps_free["PN_A"] = ev_pass

        for kt in range(4):
            fw = [ev_wkv, ev_xn1t[g], ps_free["PN_B"]]
            ev_mm = proj_mms(PN_B[:, :],
                             lambda c, kt=kt: sb["wkv"][:, (4 + kt) * 2048 + c * 128: (4 + kt) * 2048 + (c + 1) * 128],
                             fw)
            ev_vf = p.emit("vector", lambda eng, kt=kt: eng.tensor_scalar(
                sb["vf"][:, :], PN_B[:, :], sb["bv"][:, kt:kt + 1], None, ALU.add),
                waits=[ev_mm, ev_vones, prev_vtr], inc="dve")
            ps_free["PN_B"] = ev_vf
            for j in range(4):
                ktile = g * 4 + j
                ev_tr = p.emit("tensor", lambda eng, j=j: eng.transpose(
                    PM1b[:, 0:128], sb["vf"][:, j * 128:(j + 1) * 128], sb["ident"][:, :]),
                    waits=[ev_vf, ps_free["PM1"]], inc="pe")
                prev_vtr = ev_tr
                e2 = p.emit("vector", lambda eng, ktile=ktile, kt=kt: eng.tensor_copy(
                    sb["VP"][:, ktile * 520 + (2 * kt) * 65: ktile * 520 + (2 * kt) * 65 + 130]
                    .rearrange("q (h c) -> q h c", h=2)[:, :, 0:64],
                    PM1b[:, 0:128].rearrange("q (h c) -> q h c", h=2)),
                    waits=[ev_tr], inc="dve")
                ps_free["PM1"] = e2

        if own:
            for t in range(16):
                qi = g * 16 + t
                fw = [qload[qi], ev_xn1t[g], ps_free["PM1"]]
                ev_mm = None
                for c in range(16):
                    ev_mm = p.emit("tensor", lambda eng, c=c, t=t, bf=bf: eng.matmul(
                        PM1[:, :], sb["wqs"][:, t % 2][:, c * 128:(c + 1) * 128],
                        sb["xn1T"][:, bf][:, c * 512:(c + 1) * 512],
                        start=(c == 0), stop=(c == 15)),
                        waits=list(fw) if c == 0 else [], inc="pe")
                qdone[qi] = ev_mm

                def qcopy(ev_pm, g=g, t=t, qi=qi):
                    ev_cp = p.emit("vector", lambda eng, t=t: eng.tensor_copy(
                        sb["qstage"][:, t % 2], PM2[:, :]),
                        waits=[ev_pm, ev_qspill.get(qi - 2)], inc="dve")
                    ev_qspill[qi] = p.emit("sync", lambda eng, t=t, g=g: eng.dma_start(
                        out=qt_dram[t][:, g * 512:(g + 1) * 512], in_=sb["qstage"][:, t % 2]),
                        waits=[ev_cp], inc="qsp", n=16)
                    return ev_cp

                ev_pass, _ = rope_evac(PM1, sb["bq"][:, t:t + 1], qcopy, [], ev_mm)
                ps_free["PM1"] = ev_pass
                if qi + 2 < 64:
                    emit_qload(qi + 2)
        ev_xn1t_done[g] = p.last_ev["tensor"]
        ev_kv_ready[g] = p.last_ev["vector"]

    slot_base = [0, 8, 24, 48]
    ev_exp_hist = []
    ev_av_hist = []
    prev_evac = {0: None, 1: None}
    prev_nm_w = [None] * 4
    prev_at_w = [None] * 4
    at_spill = {}
    prev_dn = None
    ev_rr_prev = [None] * 4
    qt_load = {}
    qt_done = {}

    def emit_qtload(s, t):
        sti = s * 16 + t
        w = [qt_done.get(sti - 2), ev_qspill[s * 16 + t]]
        if sti == 0:
            w.append(ev_tr1[NT1 - 1])
        qt_load[sti] = p.emit("sync", lambda eng, s=s, t=t: eng.dma_start(
            out=sb["QTp"][:, t % 2], in_=qt_dram[t][:, s * 512:(s + 1) * 512]),
            waits=w, inc="qsp", n=16)

    def emit_p3_slot(s):
        nonlocal prev_dn
        Ks = KS[s]
        if s == 0:
            emit_qtload(0, 0)
            emit_qtload(0, 1)
        ev_evacB = [None] * 16
        ev_dn_ev = [None] * 16
        for t in range(16):
            sti = s * 16 + t
            kt = t // 4
            ev_exp_u = {}
            ev_dep_u = {}
            ev_av_u = {}

            NB_A = PN_A if t % 2 == 0 else PM1
            NB_B = PN_B if t % 2 == 0 else PM2

            def emit_av(pp, NB_A=NB_A, NB_B=NB_B):
                nown = 4 * (s + 1)
                kb = pp if pp < nown else 16 + (pp - nown)
                bufP = pp % 4
                avw = [ev_dep_u[pp]]
                if pp == 0:
                    avw.append(prev_evac[t % 2])
                p.emit("tensor", lambda eng, kb=kb, kt=kt, bufP=bufP, pp=pp, Ks=Ks: eng.matmul(
                    NB_A[0:65, :],
                    sb["VP"][:, kb * 520 + (2 * kt) * 65: kb * 520 + (2 * kt) * 65 + 65],
                    sb["PT"][:, bufP][:, 0:512], start=(pp == 0), stop=(pp == Ks - 1)),
                    waits=avw, inc="pe")
                ev_av_u[pp] = p.emit("tensor", lambda eng, kb=kb, kt=kt, bufP=bufP, pp=pp, Ks=Ks: eng.matmul(
                    NB_B[0:65, :],
                    sb["VP"][:, kb * 520 + (2 * kt + 1) * 65: kb * 520 + (2 * kt + 1) * 65 + 65],
                    sb["PT"][:, bufP][:, 512:1024], start=(pp == 0), stop=(pp == Ks - 1)),
                    inc="pe")

            for pp in range(Ks):
                nown = 4 * (s + 1)
                kb = pp if pp < nown else 16 + (pp - nown)
                diag = (4 * s <= pp < nown)
                mj = pp - 4 * s if diag else 0
                bufS = pp % 2
                bufP = pp % 4
                w = [ev_kv_ready[kb // 4]]
                if pp == 0:
                    w.append(qt_load[sti])
                    if sti == 0:
                        w.append(ev_ev1[NT1 - 1])
                if pp >= 2:
                    w.append(ev_exp_u[pp - 2])
                elif len(ev_exp_hist) >= 2:
                    w.append(ev_exp_hist[-2])
                p.emit("tensor", lambda eng, kt=kt, kb=kb, bufS=bufS, t=t: eng.matmul(
                    PSBIG[:, bufS * 1024: bufS * 1024 + 512],
                    sb["KT"][0:64, kt * P_ALL + kb * 128: kt * P_ALL + (kb + 1) * 128],
                    sb["QTp"][0:64, t % 2], start=True, stop=True),
                    waits=w, inc="pe")
                ev_s2 = p.emit("tensor", lambda eng, kt=kt, kb=kb, bufS=bufS, t=t: eng.matmul(
                    PSBIG[:, bufS * 1024 + 512: bufS * 1024 + 1024],
                    sb["KT"][64:128, kt * P_ALL + kb * 128: kt * P_ALL + (kb + 1) * 128],
                    sb["QTp"][64:128, t % 2], start=True, stop=True), inc="pe")
                ew = [ev_s2]
                if pp >= 4:
                    ew.append(ev_av_u[pp - 4])
                elif ev_av_hist:
                    ew.append(ev_av_hist[-1])
                if not ev_exp_hist:
                    ew.append(ld0)
                    ew.append(ev_tr1[NT1 - 1])
                ev_exp = p.emit("scalar", lambda eng, s=s, pp=pp, bufS=bufS, bufP=bufP: eng.activation(
                    sb["PT"][:, bufP], PSBIG[:, bufS * 1024:(bufS + 1) * 1024], AF.Exp,
                    bias=sb["biasmat"][:, slot_base[s] + pp: slot_base[s] + pp + 1],
                    scale=0.125), waits=ew, inc="act", nochain=True)
                ev_exp_hist.append(ev_exp)
                ev_exp_u[pp] = ev_exp
                dep = ev_exp
                if diag:
                    dep = p.emit("vector", lambda eng, mj=mj, bufP=bufP: eng.tensor_mul(
                        sb["PT"][:, bufP], sb["PT"][:, bufP],
                        sb["masks"][:, mj * 1024:(mj + 1) * 1024]),
                        waits=[ev_exp], inc="dve")
                ev_dep_u[pp] = dep
                if pp >= 1:
                    emit_av(pp - 1)
            emit_av(Ks - 1)
            ev_av_hist.append(ev_av_u[Ks - 1])
            qt_done[sti] = ev_av_u[Ks - 1]
            if sti + 2 < 64:
                emit_qtload((sti + 2) // 16, (sti + 2) % 16)
            j = t % 4
            p.emit("vector", lambda eng, j=j, NB_A=NB_A: eng.tensor_copy(
                sb["Anum"][0:64, j * 512:(j + 1) * 512], NB_A[0:64, :]),
                waits=[ev_av_hist[-1], prev_nm_w[j]], inc="dve")
            p.emit("vector", lambda eng, j=j, NB_B=NB_B: eng.tensor_copy(
                sb["Anum"][64:128, j * 512:(j + 1) * 512], NB_B[0:64, :]), inc="dve")
            p.emit("vector", lambda eng, NB_A=NB_A: eng.tensor_copy(
                sb["dnrow"][0:1, 0:512], NB_A[64:65, :]), waits=[prev_dn], inc="dve")
            ev_d2 = p.emit("vector", lambda eng, NB_B=NB_B: eng.tensor_copy(
                sb["dnrow"][0:1, 512:1024], NB_B[64:65, :]), inc="dve")
            ev_evacB[t] = ev_d2
            prev_evac[t % 2] = ev_d2
            ev_dn_ev[t] = p.emit("sync", lambda eng, s=s, t=t: eng.dma_start(
                out=dn_dram[s:s + 1, t * 1024:(t + 1) * 1024], in_=sb["dnrow"][0:1, :]),
                waits=[ev_d2], inc="qsp", n=16)
            prev_dn = ev_dn_ev[t]
            if t % 4 == 3:
                t0 = t - 3
                ev_ld = p.emit("sync", lambda eng, s=s, t0=t0: eng.dma_start(
                    out=sb["rrsl"][:, :],
                    in_=dn_dram[s, t0 * 1024:(t0 + 4) * 1024].rearrange("(p f) -> p f", p=128)),
                    waits=[ev_dn_ev[t]], inc="qsp", n=16)
                ev_rc = p.emit("vector", lambda eng: eng.reciprocal(
                    sb["rrsl"][:, :], sb["rrsl"][:, :]), waits=[ev_ld], inc="dve")
                ev_st = p.emit("sync", lambda eng, s=s, t0=t0: eng.dma_start(
                    out=rr_dram[s, t0 * 1024:(t0 + 4) * 1024].rearrange("(p f) -> p f", p=128),
                    in_=sb["rrsl"][:, :]), waits=[ev_rc], inc="qsp", n=16)
                for u in range(t0, t0 + 4):
                    j2 = u % 4
                    p.emit("sync", lambda eng, s=s, u=u: eng.dma_start(
                        out=sb["rrow"][0:64, u % 4],
                        in_=rr_dram[s, u * 1024:u * 1024 + 512].unsqueeze(0).broadcast_to([64, 512])),
                        waits=[ev_st, ev_rr_prev[u % 4]], inc="qsp", n=16)
                    ev_rr = p.emit("sync", lambda eng, s=s, u=u: eng.dma_start(
                        out=sb["rrow"][64:128, u % 4],
                        in_=rr_dram[s, u * 1024 + 512:(u + 1) * 1024].unsqueeze(0).broadcast_to([64, 512])),
                        inc="qsp", n=16)
                    p.emit("vector", lambda eng, j2=j2, u=u: eng.tensor_mul(
                        sb["AT"][0:64, j2 * 512:(j2 + 1) * 512],
                        sb["Anum"][0:64, j2 * 512:(j2 + 1) * 512],
                        sb["rrow"][0:64, u % 4]),
                        waits=[ev_rr, prev_at_w[j2]], inc="dve")
                    ev_nm = p.emit("vector", lambda eng, j2=j2, u=u: eng.tensor_mul(
                        sb["AT"][64:128, j2 * 512:(j2 + 1) * 512],
                        sb["Anum"][64:128, j2 * 512:(j2 + 1) * 512],
                        sb["rrow"][64:128, u % 4]), inc="dve")
                    ev_rr_prev[u % 4] = ev_nm
                    prev_nm_w[j2] = ev_nm
                    evsp = p.emit("sync", lambda eng, s=s, u=u, j2=j2: eng.dma_start(
                        out=at_dram[u][:, s * 512:(s + 1) * 512],
                        in_=sb["AT"][:, j2 * 512:(j2 + 1) * 512]),
                        waits=[ev_nm], inc="qsp", n=16)
                    prev_at_w[j2] = evsp
                    at_spill[(s, u)] = evsp

    for w in range(4):
        for i in range(8 * w, 8 * w + 8):
            emit_p1a_tile(i)
        ev_w = emit_p1b_wave(w)
        for i in range(8 * w, 8 * w + 8):
            emit_p1c_tile(i, ev_w)
        emit_p2_group(2 * w)
        emit_p2_group(2 * w + 1)
    emit_p3_slot(0)
    ev_gload0 = None
    for c in range(16):
        w0 = [ev_qspill[63], ev_xn1t_done[7]] if c == 0 else []
        ev_gload0 = p.emit("gpsimd", lambda eng, c=c: eng.dma_start(
            out=sb["atg0"][:, c * 512:(c + 1) * 512], in_=at_dram[c][:, 0:512]),
            waits=w0 + [at_spill[(0, c)]], inc="qgp", n=16)
    emit_p3_slot(1)
    for c in range(8):
        p.emit("gpsimd", lambda eng, c=c: eng.dma_start(
            out=sb["wout"][:, c * 2048:(c + 1) * 2048], in_=di["wout"][c]),
            waits=[ev_xn1t_done[7]] if c == 0 else [], inc="qgp", n=16)
    p.emit("gpsimd", lambda eng: eng.dma_start(
        out=sb["bout_row"][:, :], in_=di["bout_row"]),
        waits=[ev_cs_done[7]], inc="qgp", n=16)
    emit_p3_slot(2)
    for c in range(8, 16):
        p.emit("gpsimd", lambda eng, c=c: eng.dma_start(
            out=sb["wout"][:, c * 2048:(c + 1) * 2048], in_=di["wout"][c]), inc="qgp", n=16)
    ev_wout = ("qgp", p.counts["qgp"])
    emit_p3_slot(3)
    p.barrier()

    ev_gdone = [None] * 4
    ev_gload = [None] * 4
    ev_x2sp = [None] * 16
    ev_r4 = [None] * 16
    ev_s4 = [None] * 16
    prev_psbig = None
    prev_sqr4 = None
    prev_tile = {}
    for gg in range(4):
        atb = "atg0" if gg % 2 == 0 else "atg1"
        if gg == 0:
            ev_gload[0] = ev_gload0
        else:
            if gg == 1:
                w0 = [qt_done[63]]
            else:
                w0 = [ev_gdone[gg - 2]]
            ev_at = None
            for c in range(16):
                ev_at = p.emit("gpsimd", lambda eng, gg=gg, c=c, atb=atb: eng.dma_start(
                    out=sb[atb][:, c * 512:(c + 1) * 512],
                    in_=at_dram[c][:, gg * 512:(gg + 1) * 512]),
                    waits=(w0 if c == 0 else []) + [at_spill[(gg, c)]], inc="qgp", n=16)
            ev_gload[gg] = ev_at
        if gg == 3:
            ev_lorab = p.emit("gpsimd", lambda eng: eng.dma_start(
                out=sb["loraB"][:, :], in_=di["loraB"]),
                waits=[ev_gdone[2]], inc="qgp", n=16)
            ev_rt = p.emit("gpsimd", lambda eng: eng.dma_start(
                out=sb["routerT"][:, :], in_=di["routerT"].rearrange("p c j -> p (c j)")),
                inc="qgp", n=16)
            ev_w5s = p.emit("gpsimd", lambda eng: eng.dma_start(
                out=sb["bbase_row"][:, :], in_=di["bbase_row"]), inc="qgp", n=16)
        last_mm = None
        for ti in range(4):
            tix = gg * 4 + ti
            row0 = tix * 128
            bft = tix % 2
            ev_x = p.emit("sync", lambda eng, row0=row0, bft=bft: eng.dma_start(
                out=sb["x2t"][:, bft], in_=di["xp"][row0:row0 + 128, :]),
                waits=[prev_tile.get(("sp", bft)), prev_tile.get(("rd", bft))],
                inc="qsp", n=16)
            ev_mm = None
            for c in range(16):
                wmm = []
                if c == 0:
                    wmm = [ev_gload[gg], ev_wout, prev_psbig]
                for nn in range(4):
                    ev_mm = p.emit("tensor", lambda eng, c=c, nn=nn, ti=ti, atb="atg0" if gg % 2 == 0 else "atg1": eng.matmul(
                        PSBIG[:, nn * 512:(nn + 1) * 512],
                        sb[atb][:, c * 512 + ti * 128: c * 512 + (ti + 1) * 128],
                        sb["wout"][:, c * 2048 + nn * 512: c * 2048 + (nn + 1) * 512],
                        start=(c == 0), stop=False),
                        waits=wmm if nn == 0 else [], inc="pe")
            for nn in range(4):
                ev_mm = p.emit("tensor", lambda eng, nn=nn: eng.matmul(
                    PSBIG[:, nn * 512:(nn + 1) * 512], sb["onestok"][0:1, :],
                    sb["bout_row"][0:1, nn * 512:(nn + 1) * 512],
                    start=False, stop=True), inc="pe")
            last_mm = ev_mm
            evs_x2 = []
            for nn in range(4):
                w2 = [ev_mm, ev_x] if nn == 0 else []
                evs_x2.append(p.emit("vector", lambda eng, nn=nn, bft=bft: eng.tensor_add(
                    sb["x2t"][:, bft][:, nn * 512:(nn + 1) * 512],
                    PSBIG[:, nn * 512:(nn + 1) * 512],
                    sb["x2t"][:, bft][:, nn * 512:(nn + 1) * 512]), waits=w2, inc="dve"))
            prev_psbig = evs_x2[3]
            ev_q4 = p.emit("scalar", lambda eng, bft=bft: eng.activation(
                sb["xsq4"][:, :], sb["x2t"][:, bft], AF.Square),
                waits=[evs_x2[3], prev_sqr4], inc="act")
            ev_r4[tix] = p.emit("vector", lambda eng, bft=bft, tix=tix: eng.reduce_sum(
                sb["mvall"][:, 2 * tix:2 * tix + 1], sb["x2t"][:, bft], AX.X),
                waits=[evs_x2[3]], inc="dve", nochain=True)
            ev_s4[tix] = p.emit("vector", lambda eng, tix=tix: eng.reduce_sum(
                sb["mvall"][:, 2 * tix + 1:2 * tix + 2], sb["xsq4"][:, :], AX.X),
                waits=[ev_q4], inc="dve", nochain=True)
            prev_sqr4 = ev_s4[tix]
            ev_x2sp[tix] = p.emit("sync", lambda eng, row0=row0, bft=bft: eng.dma_start(
                out=x2_dram[row0:row0 + 128, :], in_=sb["x2t"][:, bft]),
                waits=[evs_x2[3]], inc="qsp", n=16)
            prev_tile[("sp", bft)] = ev_x2sp[tix]
            prev_tile[("rd", bft)] = ev_r4[tix]
        ev_gdone[gg] = last_mm
    last_mm15 = ev_gdone[3]
    ev_la = p.emit("gpsimd", lambda eng: eng.dma_start(
        out=sb["loraA"][:, :], in_=di["loraA"].rearrange("p c j -> p (c j)")),
        waits=[last_mm15], inc="qgp", n=16)
    ev_wb = [None] * 16
    for c in range(16):
        ev_wb[c] = p.emit("gpsimd", lambda eng, c=c: eng.dma_start(
            out=sb["wbase"][:, c * 2048:(c + 1) * 2048], in_=di["wbaseT"][c]),
            inc="qgp", n=16)

    p.emit("vector", lambda eng: eng.scalar_tensor_tensor(
        sb["sqw"][:, 0:16], sb["mvall"][:, 0:32:2],
        -1.0 / (2048.0 * 2048.0), sb["mvall"][:, 0:32:2],
        ALU.mult, ALU.mult), waits=[ev_s4[15]], inc="dve")
    p.emit("vector", lambda eng: eng.scalar_tensor_tensor(
        sb["var32"][:, 0:16], sb["mvall"][:, 1:32:2],
        1.0 / 2048.0, sb["sqw"][:, 0:16], ALU.mult, ALU.add), inc="dve")
    ev_o3 = p.emit("vector", lambda eng: eng.tensor_scalar(
        sb["var32"][:, 0:16], sb["var32"][:, 0:16], 1e-5, None, ALU.add), inc="dve")
    ev_ln4 = p.emit("scalar", lambda eng: eng.activation(
        sb["var32"][:, 0:16], sb["var32"][:, 0:16], AF.Ln), waits=[ev_o3], inc="act")
    ev_ex4 = p.emit("scalar", lambda eng: eng.activation(
        sb["rstd32"][:, 0:16], sb["var32"][:, 0:16], AF.Exp, scale=-0.5), inc="act")
    ev_m4 = p.emit("vector", lambda eng: eng.tensor_scalar(
        sb["m32"][:, 0:16], sb["mvall"][:, 0:32:2], 1.0 / 2048.0, None, ALU.mult),
        waits=[ev_ex4], inc="dve")

    ev_ld4 = [None] * 16
    ev_ts2 = [None] * 16
    ev_tr2 = [None] * 16
    ev_ev2 = [None] * 16
    ev_mme = [None] * 16
    ev_lge = [None] * 16
    for i in range(16):
        bf = i % 2
        enq, qn = ("sync", "qsp") if bf == 0 else ("scalar", "qsc")
        wl = [ev_x2sp[i]]
        if i >= 2:
            wl.append(ev_ts2[i - 2])
        else:
            wl += [ev_r4[14 + i], ev_x2sp[14 + i]]
        ev_ld4[i] = p.emit(enq, lambda eng, i=i, bf=bf: eng.dma_start(
            out=sb["x2t"][:, bf], in_=x2_dram[i * 128:(i + 1) * 128, :]),
            waits=wl, inc=qn, n=16)
        ev_ts2[i] = p.emit("vector", lambda eng, i=i, bf=bf: eng.tensor_scalar(
            sb["xn2t"][:, bf], sb["x2t"][:, bf], sb["m32"][:, i:i + 1],
            sb["rstd32"][:, i:i + 1], ALU.subtract, ALU.mult),
            waits=[ev_ld4[i], ev_m4, ev_tr2[i - 2] if i >= 2 else None],
            inc="dve", nochain=True)
        bks = (PM1b, PM2b) if bf == 0 else (PN_Ab, PN_Bb)
        ev_tr7 = None
        evt = None
        for c in range(16):
            wtr = []
            if c == 0:
                wtr = [ev_ts2[i], ev_ev2[i - 2] if i >= 2 else None]
            evt = p.emit("tensor", lambda eng, c=c, bf=bf, bks=bks: eng.transpose(
                bks[c // 8][:, (c % 8) * 128:(c % 8 + 1) * 128],
                sb["xn2t"][:, bf][:, c * 128:(c + 1) * 128], sb["ident"][:, :]),
                waits=wtr, inc="pe")
            if c == 7:
                ev_tr7 = evt
        ev_tr2[i] = evt
        p.emit("scalar", lambda eng, i=i, bks=bks: eng.activation(
            sb["xn2T"][:, i * 2048:i * 2048 + 1024], bks[0][:, 0:1024], AF.Copy),
            waits=[ev_tr7], inc="act")
        ev_ev2[i] = p.emit("scalar", lambda eng, i=i, bks=bks: eng.activation(
            sb["xn2T"][:, i * 2048 + 1024:(i + 1) * 2048], bks[1][:, 0:1024], AF.Copy),
            waits=[ev_tr2[i]], inc="act")
        mcol = 0 if bf == 0 else 1024
        lcol = 512 if bf == 0 else 1536
        evm = None
        evlg = None
        for c in range(16):
            wm = []
            if c == 0:
                wm = [ev_ev2[i], ev_la,
                      ev_mme[i - 2] if i >= 2 else prev_psbig]
            evm = p.emit("tensor", lambda eng, c=c, i=i, mcol=mcol: eng.matmul(
                PSBIG[:, mcol:mcol + 128],
                sb["xn2T"][:, i * 2048 + c * 128:i * 2048 + (c + 1) * 128],
                sb["loraA"][:, c * 128:(c + 1) * 128],
                start=(c == 0), stop=(c == 15)),
                waits=wm, inc="pe")
            wl2 = []
            if c == 0:
                wl2 = [ev_rt, ev_lge[i - 2] if i >= 2 else None]
            evlg = p.emit("tensor", lambda eng, c=c, i=i, lcol=lcol: eng.matmul(
                PSBIG[:, lcol:lcol + 8],
                sb["xn2T"][:, i * 2048 + c * 128:i * 2048 + (c + 1) * 128],
                sb["routerT"][:, c * 8:(c + 1) * 8],
                start=(c == 0), stop=(c == 15)),
                waits=wl2, inc="pe")
        ev_mme[i] = p.emit("vector", lambda eng, i=i, mcol=mcol: eng.tensor_copy(
            sb["midall"][:, i * 128:(i + 1) * 128], PSBIG[:, mcol:mcol + 128]),
            waits=[evm], inc="dve", nochain=True)
        ev_lge[i] = p.emit("vector", lambda eng, i=i, lcol=lcol: eng.tensor_add(
            sb["lgall"][:, i * 8:(i + 1) * 8], PSBIG[:, lcol:lcol + 8], sb["rbb"][:, :]),
            waits=[evlg], inc="dve", nochain=True)
    def v16(ap):
        return ap.rearrange("p (t e) -> p t e", t=16)

    p.emit("vector", lambda eng: eng.reduce_max(
        sb["mx16"][:, :].unsqueeze(2), v16(sb["lgall"][:, :]), AX.X),
        waits=[ev_lge[15]], inc="dve")
    ev_p2s = p.emit("vector", lambda eng: eng.tensor_sub(
        v16(sb["lgall"][:, :]), v16(sb["lgall"][:, :]),
        sb["mx16"][:, :].unsqueeze(2).broadcast_to([128, 16, 8])), inc="dve")
    ev_p2e = p.emit("scalar", lambda eng: eng.activation(
        sb["lgall"][:, :], sb["lgall"][:, :], AF.Exp), waits=[ev_p2s], inc="act")
    p.emit("vector", lambda eng: eng.reduce_max(
        sb["mx16"][:, :].unsqueeze(2), v16(sb["lgall"][:, :]), AX.X),
        waits=[ev_p2e], inc="dve")
    p.emit("vector", lambda eng: eng.tensor_tensor(
        v16(sb["gmaskb"][:, :]), v16(sb["lgall"][:, :]),
        sb["mx16"][:, :].unsqueeze(2).broadcast_to([128, 16, 8]), ALU.is_ge), inc="dve")
    p.emit("vector", lambda eng: eng.scalar_tensor_tensor(
        sb["gateb"][:, :], sb["gmaskb"][:, :], -2.0, sb["lgall"][:, :],
        ALU.mult, ALU.add), inc="dve")
    p.emit("vector", lambda eng: eng.reduce_max(
        sb["m2_16"][:, :].unsqueeze(2), v16(sb["gateb"][:, :]), AX.X), inc="dve")
    p.emit("vector", lambda eng: eng.tensor_tensor(
        v16(sb["gmaskb"][:, :]), v16(sb["lgall"][:, :]),
        sb["m2_16"][:, :].unsqueeze(2).broadcast_to([128, 16, 8]), ALU.is_ge), inc="dve")
    p.emit("vector", lambda eng: eng.tensor_mul(
        sb["gateb"][:, :], sb["lgall"][:, :], sb["gmaskb"][:, :]), inc="dve")
    p.emit("vector", lambda eng: eng.reduce_sum(
        sb["gs16"][:, :].unsqueeze(2), v16(sb["gateb"][:, :]), AX.X), inc="dve")
    p.emit("vector", lambda eng: eng.reciprocal(sb["rg16"][:, :], sb["gs16"][:, :]), inc="dve")
    ev_g2f = p.emit("vector", lambda eng: eng.scalar_tensor_tensor(
        v16(sb["gateb"][:, :]), v16(sb["gateb"][:, :]), 2.0,
        sb["rg16"][:, :].unsqueeze(2).broadcast_to([128, 16, 8]),
        ALU.mult, ALU.mult), inc="dve")
    ev_mg = [None] * 16
    for i2 in range(16):
        ev_mg[i2] = p.emit("vector", lambda eng, i2=i2: eng.tensor_tensor(
            sb["midall"][:, i2 * 128:(i2 + 1) * 128].rearrange("p (e r) -> p e r", e=8),
            sb["midall"][:, i2 * 128:(i2 + 1) * 128].rearrange("p (e r) -> p e r", e=8),
            sb["gateb"][:, i2 * 8:(i2 + 1) * 8].unsqueeze(2).broadcast_to([128, 8, 16]),
            ALU.mult), waits=[ev_g2f], inc="dve", nochain=True)

    ev_store = [None] * 16
    ev_mtc = [None] * 16
    ev_lbm = [None] * 16
    prev_epn = None
    for ti in range(16):
        row0 = ti * 128
        bf = ti % 2
        enq, qn = ("sync", "qsp") if bf == 0 else ("scalar", "qsc")
        wx = [ev_ts2[14 + bf]] if ti < 2 else [ev_epn_map[ti - 2][0]]
        ev_xld = p.emit(enq, lambda eng, row0=row0, bf=bf: eng.dma_start(
            out=sb["xti"][:, bf], in_=x2_dram[row0:row0 + 128, :]),
            waits=wx, inc=qn, n=16)
        evb = None
        for c in range(16):
            for nn in range(4):
                wb_ = []
                if nn == 0:
                    wb_ = [ev_wb[c]]
                if c == 0:
                    wb_.append(ev_lge[15] if ti == 0 else prev_epn[nn])
                evb = p.emit("tensor", lambda eng, c=c, nn=nn, ti=ti: eng.matmul(
                    PSBIG[:, nn * 512:(nn + 1) * 512],
                    sb["xn2T"][:, ti * 2048 + c * 128:ti * 2048 + (c + 1) * 128],
                    sb["wbase"][:, c * 2048 + nn * 512:c * 2048 + (nn + 1) * 512],
                    start=(c == 0), stop=False),
                    waits=wb_, inc="pe")
        for nn in range(4):
            evb = p.emit("tensor", lambda eng, nn=nn: eng.matmul(
                PSBIG[:, nn * 512:(nn + 1) * 512], sb["onestok"][0:1, :],
                sb["bbase_row"][0:1, nn * 512:(nn + 1) * 512],
                start=False, stop=False), waits=[ev_w5s] if nn == 0 else [], inc="pe")
        evtr5 = p.emit("tensor", lambda eng, ti=ti, bf=bf: eng.transpose(
            (PM1b if bf == 0 else PN_Ab)[:, 0:128],
            sb["midall"][:, ti * 128:(ti + 1) * 128], sb["ident"][:, :]),
            waits=[ev_mg[ti], ev_mtc[ti - 2] if ti >= 2 else None], inc="pe")
        ev_mtc[ti] = p.emit("vector", lambda eng, ti=ti, bf=bf: eng.tensor_copy(
            sb["midT2"][:, bf], (PM1b if bf == 0 else PN_Ab)[:, 0:128]),
            waits=[evtr5, ev_lbm[ti - 2] if ti >= 2 else None], inc="dve", nochain=True)
        evlb = None
        for nn in range(4):
            wlb = [ev_mtc[ti], ev_lorab] if nn == 0 else []
            evlb = p.emit("tensor", lambda eng, nn=nn, bf=bf: eng.matmul(
                PSBIG[:, nn * 512:(nn + 1) * 512], sb["midT2"][:, bf],
                sb["loraB"][:, nn * 512:(nn + 1) * 512],
                start=False, stop=True), waits=wlb, inc="pe")
        ev_lbm[ti] = evlb
        ev_epn = []
        for nn in range(4):
            w2 = [evlb, ev_xld] if nn == 0 else []
            if nn == 0 and ti >= 2:
                w2.append(ev_store[ti - 2])
            ev_epn.append(p.emit("vector", lambda eng, nn=nn, bf=bf: eng.tensor_add(
                sb["outt"][:, bf][:, nn * 512:(nn + 1) * 512],
                PSBIG[:, nn * 512:(nn + 1) * 512],
                sb["xti"][:, bf][:, nn * 512:(nn + 1) * 512]), waits=w2, inc="dve"))
        prev_epn = ev_epn
        if ti == 0:
            ev_epn_map = {}
        ev_epn_map[ti] = ev_epn
        ev_store[ti] = p.emit(enq, lambda eng, row0=row0, bf=bf: eng.dma_start(
            out=out_dram[row0:row0 + 128, :], in_=sb["outt"][:, bf]),
            waits=[ev_epn[3]], inc=qn, n=16)

    p.emit("vector", lambda eng: eng.engine_nop(), waits=[ev_store[-1]])
    p.build()
    return nc


def _rope_tables():
    e = np.arange(0, 32, 2, dtype=np.float32) / np.float32(32)
    inv = (np.float32(1.0) / np.float32(10000.0) ** e).astype(np.float32)
    t = np.arange(S, dtype=np.float32)
    fr = np.outer(t, inv).astype(np.float32)
    return np.cos(fr).astype(np.float32), np.sin(fr).astype(np.float32)


def _pair_heads(t):
    kt, i = divmod(t, 4)
    return 8 * kt + i, 8 * kt + 4 + i


def _qcol_perm():
    cols = []
    for t in range(16):
        hA, hB = _pair_heads(t)
        cols += list(range(hA * 64, hA * 64 + 16))
        cols += list(range(hB * 64, hB * 64 + 16))
        cols += list(range(hA * 64 + 16, hA * 64 + 32))
        cols += list(range(hB * 64 + 16, hB * 64 + 32))
        cols += list(range(hA * 64 + 32, hA * 64 + 64))
        cols += list(range(hB * 64 + 32, hB * 64 + 64))
    return np.array(cols)


def _kcol_perm():
    cols = []
    for kt in range(4):
        a, b = 2 * kt, 2 * kt + 1
        cols += list(range(a * 64, a * 64 + 16))
        cols += list(range(b * 64, b * 64 + 16))
        cols += list(range(a * 64 + 16, a * 64 + 32))
        cols += list(range(b * 64 + 16, b * 64 + 32))
        cols += list(range(a * 64 + 32, a * 64 + 64))
        cols += list(range(b * 64 + 32, b * 64 + 64))
    return np.array(cols)


def _grot_gpass():
    G = np.zeros((128, 128), dtype=np.float32)
    P = np.zeros((128, 128), dtype=np.float32)
    for i in range(16):
        G[i, i] = 1.0
        G[32 + i, i] = -1.0
        G[64 + i, 16 + i] = 1.0
        G[96 + i, 16 + i] = 1.0
        G[16 + i, 64 + i] = 1.0
        G[48 + i, 64 + i] = -1.0
        G[80 + i, 80 + i] = 1.0
        G[112 + i, 80 + i] = 1.0
    for i in range(32):
        P[64 + i, 32 + i] = 1.0
        P[96 + i, 96 + i] = 1.0
    return G.astype(BF), P.astype(BF)


def _wout_row_perm():
    rows = np.zeros(2048, dtype=np.int64)
    for t in range(16):
        hA, hB = _pair_heads(t)
        rows[t * 128: t * 128 + 64] = np.arange(hA * 64, hA * 64 + 64)
        rows[t * 128 + 64: t * 128 + 128] = np.arange(hB * 64, hB * 64 + 64)
    return rows


def _tile4(w):
    d, n = w.shape
    nt = n // 128
    r = w.reshape(16, 128, nt, 128)
    return np.ascontiguousarray(r.transpose(2, 1, 0, 3))


def _masks():
    k = np.arange(128)[:, None]
    q = np.arange(512)[None, :]
    out = []
    for j in range(4):
        mj = (q >= 128 * j + k).astype(np.float32).astype(BF)
        out.append(np.concatenate([mj, mj], axis=1))
    return np.ascontiguousarray(np.concatenate(out, axis=1))


def _biasmat(par):
    bm = np.full((128, 80), SOFT_SHIFT, dtype=np.float32)
    if par == 0:
        base = 0
        for s_ in range(4):
            Ks = KS[s_]
            bm[:, base + Ks - 4: base + Ks] = -10000.0
            base += Ks
    return bm


def host_prepare(inputs):
    x = np.asarray(inputs["L0_attn_x"], dtype=np.float32)
    Wqkv = np.asarray(inputs["L0_attn_Wqkv"], dtype=np.float32)
    bqkv = np.asarray(inputs["L0_attn_bqkv"], dtype=np.float32)
    Wout = np.asarray(inputs["L0_attn_Wout"], dtype=np.float32)
    bout = np.asarray(inputs["L0_attn_bout"], dtype=np.float32)
    base_w = np.asarray(inputs["L0_ffn_base_w"], dtype=np.float32)
    base_b = np.asarray(inputs["L0_ffn_base_b"], dtype=np.float32)
    router_w = np.asarray(inputs["L0_ffn_router_w"], dtype=np.float32)
    router_b = np.asarray(inputs["L0_ffn_router_b"], dtype=np.float32)
    lora_A = np.asarray(inputs["L0_ffn_lora_A"], dtype=np.float32)
    lora_B = np.asarray(inputs["L0_ffn_lora_B"], dtype=np.float32)

    cos, sin = _rope_tables()
    qperm = _qcol_perm()
    kperm = _kcol_perm()
    woutperm = _wout_row_perm()

    wq_h = _tile4(np.ascontiguousarray(Wqkv[:, :2048][:, qperm]))
    wk_h = _tile4(np.ascontiguousarray(Wqkv[:, 2048:2560][:, kperm]))
    wv_h = _tile4(np.ascontiguousarray(Wqkv[:, 2560:3072]))
    wout_h = np.ascontiguousarray(Wout[woutperm].reshape(16, 128, D))
    wbaseT_h = np.ascontiguousarray(base_w.T.reshape(16, 128, D))
    A_all = np.ascontiguousarray(lora_A.transpose(1, 0, 2).reshape(D, 128))
    loraA_h = np.ascontiguousarray(A_all.reshape(16, 128, 128).transpose(1, 0, 2))
    loraB_h = np.ascontiguousarray(lora_B.reshape(128, D))
    routerT_h = np.ascontiguousarray(router_w.T.reshape(16, 128, 8).transpose(1, 0, 2)).astype(BF)

    bq_h = np.ascontiguousarray(bqkv[:2048][qperm].reshape(16, 128).T)
    bk_h = np.ascontiguousarray(bqkv[2048:2560][kperm].reshape(4, 128).T)
    bv_h = np.ascontiguousarray(bqkv[2560:3072].reshape(4, 128).T)
    rbb_h = np.ascontiguousarray(np.broadcast_to(router_b[None, :], (128, 8)))
    grot, gpass = _grot_gpass()
    shared = {
        "masks": _masks(), "grot": grot, "gpass": gpass, "ident": np.eye(128, dtype=BF),
        "ones1": np.ones((1, 64), dtype=np.float32),
        "onestok": np.ones((1, 128), dtype=BF),
        "wq": wq_h, "wk": wk_h, "wv": wv_h, "wout": wout_h,
        "wbaseT": wbaseT_h, "loraA": loraA_h, "loraB": loraB_h,
        "routerT": routerT_h, "bq": bq_h, "bk": bk_h, "bv": bv_h,
        "rbb": rbb_h, "bout_row": bout[None, :].astype(BF),
        "bbase_row": base_b[None, :].astype(BF),
    }

    in_maps = []
    metas = []
    for core in range(8):
        b, par = divmod(core, 2)
        own = [2 * i + par for i in range(4)]
        partner = [2 * i + (1 - par) for i in range(4)]
        rows = np.concatenate([np.arange(k * CH, (k + 1) * CH) for k in own + partner])
        xp = np.ascontiguousarray(x[b][rows])
        ct = cos[rows].T
        st_ = sin[rows].T
        ccss = np.ascontiguousarray(np.concatenate([ct, ct, st_, st_], axis=0))
        sscc = np.ascontiguousarray(np.concatenate([st_, st_, ct, ct], axis=0))
        m = dict(shared)
        m.update({"xp": xp, "ccss": ccss, "sscc": sscc, "biasmat": _biasmat(par)})
        in_maps.append(m)
        metas.append((b, own))
    return in_maps, metas


def kernel(**inputs):
    if "nc" not in _CACHE:
        _CACHE["nc"] = build_nc()
    nc = _CACHE["nc"]
    in_maps, metas = host_prepare(inputs)
    res = run_bass_kernel_spmd(nc, in_maps, core_ids=list(range(8)))
    _CACHE["last"] = res
    out = np.zeros((B, S, D), dtype=np.float32)
    for core in range(8):
        b, own = metas[core]
        o = np.asarray(res.results[core]["out"], dtype=np.float32)
        for i, k in enumerate(own):
            out[b, k * CH:(k + 1) * CH] = o[i * CH:(i + 1) * CH]
    return out


# version 18
# speedup vs baseline: 1.3294x; 1.0645x over previous
import numpy as np
import ml_dtypes
import concourse.bass as bass
import concourse.mybir as mybir
from concourse.bass_utils import run_bass_kernel_spmd

F32 = mybir.dt.float32
BF16 = mybir.dt.bfloat16
AF = mybir.ActivationFunctionType
ALU = mybir.AluOpType
AX = mybir.AxisListType
BF = ml_dtypes.bfloat16

B, S, D = 4, 4096, 2048
CH = 512
T_OWN = 2048
P_ALL = 4096
NSLOT = 4
KS = [8, 16, 24, 32]
SOFT_SHIFT = -8.0

_TAPS = False
_PHASES = 5
_CACHE = {}


class Prog:
    ENGINES = ("sync", "scalar", "vector", "tensor", "gpsimd")

    def __init__(self, nc):
        self.nc = nc
        self.steps = []
        self.sems = {}
        self.counts = {}
        self.last_ev = {}
        self._pending = {}

    def sem(self, name):
        if name not in self.sems:
            self.sems[name] = self.nc.alloc_semaphore(name)
            self.counts[name] = 0
        return self.sems[name]

    def emit(self, engine, fn, waits=(), inc=None, n=1, nochain=False):
        w = [x for x in waits if x is not None]
        if engine in self._pending:
            w += self._pending.pop(engine)
        ev = None
        incs = []
        if inc is not None:
            self.sem(inc)
            self.counts[inc] += n
            ev = (inc, self.counts[inc])
            incs = [(inc, n)]
            self.last_ev[engine] = ev
        self.steps.append((engine, fn, w, incs, nochain))
        return ev

    def barrier(self):
        evs = list(self.last_ev.values())
        for e in self.ENGINES:
            self._pending[e] = list(evs)

    def build(self):
        nc = self.nc
        with nc.Block() as block:
            for eng_name in self.ENGINES:
                if not any(s[0] == eng_name for s in self.steps):
                    continue

                def make(en):
                    def emit_stream(eng):
                        waited = {}
                        own = {"vector": "dve", "scalar": "act"}.get(en)
                        cnt = 0
                        for (e, fn, waits, incs, nochain) in self.steps:
                            if e != en:
                                continue
                            if own is not None and cnt > 0 and own in self.sems \
                                    and not nochain and waited.get(own, -1) < cnt:
                                eng.wait_ge(self.sems[own], cnt)
                                waited[own] = cnt
                            for (s, v) in waits:
                                if waited.get(s, -1) < v:
                                    eng.wait_ge(self.sems[s], v)
                                    waited[s] = v
                            ins = fn(eng)
                            for (s, a) in incs:
                                ins = ins.then_inc(self.sems[s], a)
                                if s == own:
                                    cnt += a
                    return emit_stream

                getattr(block, eng_name)(make(eng_name))


def build_nc():
    nc = bass.Bass("TRN2", target_bir_lowering=False)
    p = Prog(nc)

    di = {}

    def inp(name, shape, dt):
        di[name] = nc.dram_tensor(name, list(shape), dt, kind="ExternalInput").ap()

    inp("xp", [P_ALL, D], F32)
    inp("ccss", [64, P_ALL], F32)
    inp("sscc", [64, P_ALL], F32)
    inp("biasmat", [128, 80], F32)
    inp("masks", [128, 4096], BF16)
    inp("grot", [128, 128], BF16)
    inp("gpass", [128, 128], BF16)
    inp("ident", [128, 128], BF16)
    inp("ones1", [1, 64], F32)
    inp("onestok", [1, 128], BF16)
    inp("wq", [16, 128, 16, 128], F32)
    inp("wk", [4, 128, 16, 128], F32)
    inp("wv", [4, 128, 16, 128], F32)
    inp("wout", [16, 128, D], F32)
    inp("wbaseT", [16, 128, D], F32)
    inp("loraA", [128, 16, 128], F32)
    inp("loraB", [128, D], F32)
    inp("routerT", [128, 16, 8], BF16)
    inp("bq", [128, 16], F32)
    inp("bk", [128, 4], F32)
    inp("bv", [128, 4], F32)
    inp("rbb", [128, 8], F32)
    inp("bout_row", [1, D], BF16)
    inp("bbase_row", [1, D], BF16)

    ikind = "ExternalOutput" if _TAPS else "Internal"
    qt_dram = nc.dram_tensor("qt_dram", [16, 128, T_OWN], BF16, kind=ikind).ap()
    at_dram = nc.dram_tensor("at_dram", [16, 128, T_OWN], BF16, kind=ikind).ap()
    x2_dram = nc.dram_tensor("x2_dram", [T_OWN, D], F32, kind=ikind).ap()
    xn2_dram = nc.dram_tensor("xn2_dram", [T_OWN, D], BF16, kind=ikind).ap()
    dn_dram = nc.dram_tensor("dn_dram", [NSLOT, 16384], F32, kind=ikind).ap()
    rr_dram = nc.dram_tensor("rr_dram", [NSLOT, 16384], F32, kind="Internal").ap()
    out_dram = nc.dram_tensor("out", [T_OWN, D], F32, kind="ExternalOutput").ap()

    sb = {}
    cursors = {}

    def sba(name, shape, dt, region, reg_start):
        if region not in cursors:
            cursors[region] = reg_start
        esz = 2 if dt == BF16 else 4
        nb = 1
        for s_ in shape[1:]:
            nb *= s_
        nb *= esz
        off = cursors[region]
        cursors[region] = off + ((nb + 31) // 32) * 32
        sb[name] = nc.alloc_sbuf_tensor_at(name, list(shape), dt, offset=off)
        return sb[name]

    K0 = 18432
    A0 = 38912
    B0 = 108544
    C0 = 178176
    LIMIT = 229376

    sba("ident", [128, 128], BF16, "k", K0)
    sba("grot", [128, 128], BF16, "k", K0)
    sba("gpass", [128, 128], BF16, "k", K0)
    sba("ones1", [1, 64], F32, "k", K0)
    sba("onestok", [1, 128], BF16, "k", K0)
    sba("bq", [128, 16], F32, "k", K0)
    sba("bk", [128, 4], F32, "k", K0)
    sba("bv", [128, 4], F32, "k", K0)
    sba("rbb", [128, 8], F32, "k", K0)
    sba("biasmat", [128, 80], F32, "k", K0)
    sba("masks", [128, 4096], BF16, "k", K0)
    sba("trep", [128, 8], F32, "k", K0)
    sba("mvall", [128, 64], F32, "k", K0)
    sba("sqw", [128, 16], F32, "k", K0)
    sba("var32", [128, 32], F32, "k", K0)
    sba("rstd32", [128, 32], F32, "k", K0)
    sba("m32", [128, 32], F32, "k", K0)
    off_ccs = cursors["k"]
    sba("ccs", [64, 2, 512], F32, "k", K0)
    sba("ssc", [64, 2, 512], F32, "k", K0)

    sba("KT", [128, 4 * P_ALL], BF16, "A23", A0)
    sba("VP", [128, 32 * 520], BF16, "A23", A0)
    sba("rrsl", [128, 32], F32, "A23", A0)
    sba("u", [64, 512], F32, "A23", A0)
    sba("vf", [128, 512], BF16, "A23", A0)
    sba("xn2T", [128, 16 * T_OWN], BF16, "A5", A0)
    assert max(cursors["A23"], cursors["A5"]) <= B0

    sba("wkv", [128, 8 * D], BF16, "B2", B0)
    sba("xn1T", [128, 2, 16 * 512], BF16, "B2", B0)
    sba("dnrow", [1, 1024], F32, "B2", B0)
    sba("wout", [128, 16 * D], BF16, "B4", B0)
    sba("wbase", [128, 16 * D], BF16, "B5", B0)
    assert max(cursors["B2"], cursors["B4"], cursors["B5"]) <= C0

    sba("xpt", [128, 2, D], F32, "C1", C0)
    sba("xnt", [128, 2, D], BF16, "C1", C0)
    sba("xsq", [128, D], BF16, "C1", C0)
    sba("mv", [128, 2], F32, "k", None)
    sba("sq1", [128, 1], F32, "k", None)
    sba("stdc", [128, 1], F32, "k", None)
    sba("rstd", [128, 1], F32, "k", None)
    C2S = C0 + 32768
    sba("wqs", [128, 2, 2048], BF16, "C2", C2S)
    sba("wrope", [128, 512], BF16, "C2", C2S)
    sba("qrope", [128, 512], BF16, "C2", C2S)
    sba("qstage", [128, 2, 512], BF16, "C2", C2S)
    sba("PT", [128, 4, 1024], BF16, "C3a", C0)
    C3B = C0 + 8192
    sba("QTp", [128, 2, 512], BF16, "C3b", C3B)
    sba("Anum", [128, 4 * 512], F32, "C3b", C3B)
    sba("AT", [128, 4 * 512], BF16, "C3b", C3B)
    sba("rrow", [128, 4, 512], F32, "C3b", C3B)
    sb["atg0"] = nc.alloc_sbuf_tensor_at("atg0", [128, 16 * 512], BF16, offset=C2S)
    sb["atg1"] = nc.alloc_sbuf_tensor_at("atg1", [128, 16 * 512], BF16, offset=A0 + 32768)
    sba("x2t", [128, 2, D], F32, "C4", C0)
    sba("xn2t", [128, 2, D], BF16, "C4", C0)
    sba("xsq4", [128, D], F32, "C4", C0)
    sb["bout_row"] = nc.alloc_sbuf_tensor_at("bout_row", [1, D], BF16, offset=off_ccs)
    sba("xti", [128, 2, D], F32, "C5", C0)
    sba("outt", [128, 2, D], F32, "C5", C0)
    sba("loraA", [128, 16 * 128], BF16, "C5", C0)
    sba("loraB", [128, D], BF16, "C5", C0)
    sba("routerT", [128, 128], BF16, "C5", C0)
    sba("midall", [128, 16 * 128], BF16, "C5", C0)
    sba("midT2", [128, 2, 128], BF16, "C5", C0)
    sba("bbase_row", [1, D], BF16, "C5", C0)
    C5G = C0 + 24576
    sba("lgall", [128, 128], F32, "C5g", C5G)
    sba("gmaskb", [128, 128], F32, "C5g", C5G)
    sba("gateb", [128, 128], F32, "C5g", C5G)
    sba("mx16", [128, 16], F32, "C5g", C5G)
    sba("m2_16", [128, 16], F32, "C5g", C5G)
    sba("gs16", [128, 16], F32, "C5g", C5G)
    sba("rg16", [128, 16], F32, "C5g", C5G)
    assert cursors["C5g"] <= C0 + 32768
    assert cursors["k"] <= A0, cursors["k"]
    for r in ("C1", "C2", "C3b", "C4", "C5", "k", "A23"):
        assert cursors[r] <= LIMIT, (r, cursors[r])

    PSBIG = nc.alloc_psum_tensor("PSBIG", [128, 2048], F32)
    PN_A = nc.alloc_psum_tensor("PN_A", [128, 512], F32)
    PN_B = nc.alloc_psum_tensor("PN_B", [128, 512], F32)
    PM1 = nc.alloc_psum_tensor("PM1", [128, 512], F32)
    PM2 = nc.alloc_psum_tensor("PM2", [128, 512], F32)
    PM1b = PM1[:, :].bitcast(BF16)
    PM2b = PM2[:, :].bitcast(BF16)
    PN_Ab = PN_A[:, :].bitcast(BF16)
    PN_Bb = PN_B[:, :].bitcast(BF16)
    PSBIGb = PSBIG[:, :].bitcast(BF16)

    for name in ["ident", "grot", "gpass", "ones1", "onestok", "bq", "bk", "bv", "rbb",
                 "biasmat", "masks"]:
        p.emit("sync", lambda eng, n=name: eng.dma_start(out=sb[n][:, :], in_=di[n]),
               inc="qsp", n=16)
    ld0 = ("qsp", p.counts["qsp"])
    p.emit("vector", lambda eng: eng.memset(sb["trep"][:, 2:8], -1.0), inc="dve")
    p.emit("vector", lambda eng: eng.memset(sb["qrope"][0:64, :], 0.0), inc="dve")
    ev_vones = p.emit("vector", lambda eng: eng.memset(
        sb["VP"][:, :].rearrange("p (k g c) -> p k g c", k=32, g=8)[:, :, :, 64:65], 1.0),
        inc="dve")

    NT1 = P_ALL // 128
    ev_tr1 = [None] * NT1
    ev_ev1 = [None] * NT1
    ev_sqr = [None] * (NT1 + 16)
    ev_xn = [None] * NT1

    def emit_p1a_tile(i):
        bf = i % 2
        evl = p.emit("sync", lambda eng, i=i, bf=bf: eng.dma_start(
            out=sb["xpt"][:, bf], in_=di["xp"][i * 128:(i + 1) * 128, :]),
            waits=[ev_sqr[i - 2]] if i % 8 >= 2 else [ev_xn[i - 2]], inc="qsp", n=16)
        ev_q = p.emit("scalar", lambda eng, bf=bf: eng.activation(
            sb["xsq"][:, :], sb["xpt"][:, bf], AF.Square),
            waits=[evl, ev_sqr[i - 1] if i >= 1 else None], inc="act", nochain=True)
        p.emit("vector", lambda eng, i=i, bf=bf: eng.reduce_sum(
            sb["mvall"][:, 2 * (i % 16):2 * (i % 16) + 1], sb["xpt"][:, bf], AX.X),
            waits=[evl], inc="dve", nochain=True)
        ev_sqr[i] = p.emit("vector", lambda eng, i=i: eng.reduce_sum(
            sb["mvall"][:, 2 * (i % 16) + 1:2 * (i % 16) + 2], sb["xsq"][:, :], AX.X),
            waits=[ev_q], inc="dve", nochain=True)

    def emit_p1b_wave(w):
        c0 = 16 * (w % 2)
        ev_o1 = p.emit("vector", lambda eng, c0=c0: eng.scalar_tensor_tensor(
            sb["sqw"][:, 0:8], sb["mvall"][:, c0:c0 + 16:2],
            -1.0 / (2048.0 * 2048.0), sb["mvall"][:, c0:c0 + 16:2],
            ALU.mult, ALU.mult), waits=[ev_sqr[8 * w + 7]], inc="dve")
        ev_o2 = p.emit("vector", lambda eng, w=w, c0=c0: eng.scalar_tensor_tensor(
            sb["var32"][:, 8 * w:8 * w + 8], sb["mvall"][:, c0 + 1:c0 + 16:2],
            1.0 / 2048.0, sb["sqw"][:, 0:8], ALU.mult, ALU.add), inc="dve")
        ev_o3 = p.emit("vector", lambda eng, w=w: eng.tensor_scalar(
            sb["var32"][:, 8 * w:8 * w + 8], sb["var32"][:, 8 * w:8 * w + 8],
            1e-5, None, ALU.add), inc="dve")
        ev_ln = p.emit("scalar", lambda eng, w=w: eng.activation(
            sb["var32"][:, 8 * w:8 * w + 8], sb["var32"][:, 8 * w:8 * w + 8],
            AF.Ln), waits=[ev_o3], inc="act")
        ev_ex = p.emit("scalar", lambda eng, w=w: eng.activation(
            sb["rstd32"][:, 8 * w:8 * w + 8], sb["var32"][:, 8 * w:8 * w + 8],
            AF.Exp, scale=-0.5), inc="act")
        ev_m = p.emit("vector", lambda eng, w=w, c0=c0: eng.tensor_scalar(
            sb["m32"][:, 8 * w:8 * w + 8], sb["mvall"][:, c0:c0 + 16:2],
            1.0 / 2048.0, None, ALU.mult), waits=[ev_ex], inc="dve")
        return ev_m

    def emit_p1c_tile(i, ev_w):
        bf = i % 2
        g, j = divmod(i, 4)
        evl = p.emit("scalar", lambda eng, i=i, bf=bf: eng.dma_start(
            out=sb["xpt"][:, bf], in_=di["xp"][i * 128:(i + 1) * 128, :]),
            waits=[ev_xn[i - 2], ev_sqr[(i // 8) * 8 + 7]], inc="qsc", n=16)
        ev_xn[i] = p.emit("vector", lambda eng, i=i, bf=bf: eng.tensor_scalar(
            sb["xnt"][:, bf], sb["xpt"][:, bf], sb["m32"][:, i:i + 1],
            sb["rstd32"][:, i:i + 1], ALU.subtract, ALU.mult),
            waits=[evl, ev_w, ev_tr1[i - 2]], inc="dve", nochain=True)
        evt = None
        for c in range(16):
            wtr = [ev_xn[i], ev_ev1[i - 2]] if c == 0 else []
            evt = p.emit("tensor", lambda eng, i=i, c=c, bf=bf: eng.transpose(
                PSBIGb[:, (i % 2) * 2048 + c * 128:(i % 2) * 2048 + (c + 1) * 128],
                sb["xnt"][:, bf][:, c * 128:(c + 1) * 128], sb["ident"][:, :]),
                waits=wtr, inc="pe")
        ev_tr1[i] = evt
        wev = [ev_tr1[i]]
        if j == 0 and g >= 2:
            wev.append(ev_xn1t_done[g - 2])
        ev_ev1[i] = p.emit("vector", lambda eng, i=i, g=g, j=j: eng.tensor_copy(
            sb["xn1T"][:, g % 2].rearrange("p (c x) -> p c x", c=16)[:, :, j * 128:(j + 1) * 128],
            PSBIGb[:, (i % 2) * 2048:(i % 2 + 1) * 2048].rearrange("p (c x) -> p c x", c=16)),
            waits=wev, inc="dve", nochain=True)

    for t in range(4):
        p.emit("gpsimd", lambda eng, t=t: eng.dma_start(
            out=sb["wkv"][:, t * 2048:(t + 1) * 2048],
            in_=di["wk"][t].rearrange("p c j -> p (c j)")), inc="qgp", n=16)
        p.emit("gpsimd", lambda eng, t=t: eng.dma_start(
            out=sb["wkv"][:, (4 + t) * 2048:(5 + t) * 2048],
            in_=di["wv"][t].rearrange("p c j -> p (c j)")), inc="qgp", n=16)
    ev_wkv = ("qgp", p.counts["qgp"])

    NG = P_ALL // 512
    ev_kv_ready = [None] * NG
    ev_xn1t = [None] * NG
    ev_xn1t_done = [None] * NG
    ev_cs = [None] * NG
    ev_cs_done = [None] * NG
    ev_qspill = {}
    qload = {}
    qdone = {}
    prev_perm = None
    prev_vtr = None
    ps_free = {"PN_A": None, "PN_B": None, "PM1": None, "PM2": None}

    def emit_qload(qi):
        g, t = divmod(qi, 16)
        w = [qdone.get(qi - 2), ev_wkv if qi == 0 else None]
        qload[qi] = p.emit("gpsimd", lambda eng, t=t: eng.dma_start(
            out=sb["wqs"][:, t % 2],
            in_=di["wq"][t].rearrange("p c j -> p (c j)")),
            waits=w, inc="qgp", n=16)

    emit_qload(0)
    emit_qload(1)

    def emit_p2_group(g):
        nonlocal prev_vtr
        own = g < 4
        bf = g % 2
        ev_xn1t[g] = ev_ev1[4 * g + 3]
        csw = [ev_cs_done[g - 2]] if g >= 2 else []
        p.emit("sync", lambda eng, g=g, bf=bf: eng.dma_start(
            out=sb["ccs"][:, bf], in_=di["ccss"][:, g * 512:(g + 1) * 512]),
            waits=csw, inc="qsp", n=16)
        ev_cs[g] = p.emit("sync", lambda eng, g=g, bf=bf: eng.dma_start(
            out=sb["ssc"][:, bf], in_=di["sscc"][:, g * 512:(g + 1) * 512]),
            inc="qsp", n=16)

        def proj_mms(psum_ap, w_fn, fw):
            ev = None
            for c in range(16):
                ev = p.emit("tensor", lambda eng, c=c, psum_ap=psum_ap, w_fn=w_fn, bf=bf: eng.matmul(
                    psum_ap, w_fn(c), sb["xn1T"][:, bf][:, c * 512:(c + 1) * 512],
                    start=(c == 0), stop=(c == 15)),
                    waits=list(fw) if c == 0 else [], inc="pe")
            return ev

        def rope_evac(psrc, bias_col, dst_copy_fn, extra_pm_waits, ev_mm):
            nonlocal prev_perm
            ev_u = p.emit("vector", lambda eng, psrc=psrc, bias_col=bias_col: eng.tensor_scalar(
                sb["u"][:, :], psrc[0:64, :], bias_col[0:64, :], None, ALU.add),
                waits=[ev_cs[g], prev_perm, ev_mm], inc="dve")
            ev_pass = p.emit("vector", lambda eng, psrc=psrc, bias_col=bias_col: eng.tensor_scalar(
                sb["qrope"][64:128, :], psrc[64:128, :], bias_col[64:128, :], None, ALU.add),
                inc="dve")
            p.emit("vector", lambda eng, bf=bf: eng.tensor_mul(
                sb["wrope"][0:64, :], sb["u"][:, :], sb["ccs"][:, bf]), inc="dve")
            ev_b = p.emit("vector", lambda eng, bf=bf: eng.tensor_mul(
                sb["wrope"][64:128, :], sb["u"][:, :], sb["ssc"][:, bf]), inc="dve")
            ev_cs_done[g] = ev_b
            p.emit("tensor", lambda eng: eng.matmul(
                PM2[:, :], sb["grot"][:, :], sb["wrope"][:, :], start=True, stop=False),
                waits=[ev_b, ps_free["PM2"]] + list(extra_pm_waits), inc="pe")
            ev_pm = p.emit("tensor", lambda eng: eng.matmul(
                PM2[:, :], sb["gpass"][:, :], sb["qrope"][:, :], start=False, stop=True),
                inc="pe")
            prev_perm = ev_pm
            ev_cp = dst_copy_fn(ev_pm)
            ps_free["PM2"] = ev_cp
            return ev_pass, ev_cp

        for kt in range(4):
            fw = [ev_wkv, ev_xn1t[g], ps_free["PN_A"]]
            if g == 0 and kt == 0:
                fw.append(ld0)
            ev_mm = proj_mms(PN_A[:, :],
                             lambda c, kt=kt: sb["wkv"][:, kt * 2048 + c * 128: kt * 2048 + (c + 1) * 128],
                             fw)
            def kcopy(ev_pm, kt=kt, g=g):
                return p.emit("vector", lambda eng, kt=kt, g=g: eng.tensor_copy(
                    sb["KT"][:, kt * P_ALL + g * 512: kt * P_ALL + (g + 1) * 512], PM2[:, :]),
                    waits=[ev_pm], inc="dve")
            ev_pass, _ = rope_evac(PN_A, sb["bk"][:, kt:kt + 1], kcopy, [], ev_mm)
            ps_free["PN_A"] = ev_pass

        for kt in range(4):
            fw = [ev_wkv, ev_xn1t[g], ps_free["PN_B"]]
            ev_mm = proj_mms(PN_B[:, :],
                             lambda c, kt=kt: sb["wkv"][:, (4 + kt) * 2048 + c * 128: (4 + kt) * 2048 + (c + 1) * 128],
                             fw)
            ev_vf = p.emit("vector", lambda eng, kt=kt: eng.tensor_scalar(
                sb["vf"][:, :], PN_B[:, :], sb["bv"][:, kt:kt + 1], None, ALU.add),
                waits=[ev_mm, ev_vones, prev_vtr], inc="dve")
            ps_free["PN_B"] = ev_vf
            for j in range(4):
                ktile = g * 4 + j
                ev_tr = p.emit("tensor", lambda eng, j=j: eng.transpose(
                    PM1b[:, 0:128], sb["vf"][:, j * 128:(j + 1) * 128], sb["ident"][:, :]),
                    waits=[ev_vf, ps_free["PM1"]], inc="pe")
                prev_vtr = ev_tr
                e2 = p.emit("vector", lambda eng, ktile=ktile, kt=kt: eng.tensor_copy(
                    sb["VP"][:, ktile * 520 + (2 * kt) * 65: ktile * 520 + (2 * kt) * 65 + 130]
                    .rearrange("q (h c) -> q h c", h=2)[:, :, 0:64],
                    PM1b[:, 0:128].rearrange("q (h c) -> q h c", h=2)),
                    waits=[ev_tr], inc="dve")
                ps_free["PM1"] = e2

        if own:
            for t in range(16):
                qi = g * 16 + t
                fw = [qload[qi], ev_xn1t[g], ps_free["PM1"]]
                ev_mm = None
                for c in range(16):
                    ev_mm = p.emit("tensor", lambda eng, c=c, t=t, bf=bf: eng.matmul(
                        PM1[:, :], sb["wqs"][:, t % 2][:, c * 128:(c + 1) * 128],
                        sb["xn1T"][:, bf][:, c * 512:(c + 1) * 512],
                        start=(c == 0), stop=(c == 15)),
                        waits=list(fw) if c == 0 else [], inc="pe")
                qdone[qi] = ev_mm

                def qcopy(ev_pm, g=g, t=t, qi=qi):
                    ev_cp = p.emit("vector", lambda eng, t=t: eng.tensor_copy(
                        sb["qstage"][:, t % 2], PM2[:, :]),
                        waits=[ev_pm, ev_qspill.get(qi - 2)], inc="dve")
                    ev_qspill[qi] = p.emit("sync", lambda eng, t=t, g=g: eng.dma_start(
                        out=qt_dram[t][:, g * 512:(g + 1) * 512], in_=sb["qstage"][:, t % 2]),
                        waits=[ev_cp], inc="qsp", n=16)
                    return ev_cp

                ev_pass, _ = rope_evac(PM1, sb["bq"][:, t:t + 1], qcopy, [], ev_mm)
                ps_free["PM1"] = ev_pass
                if qi + 2 < 64:
                    emit_qload(qi + 2)
        ev_xn1t_done[g] = p.last_ev["tensor"]
        ev_kv_ready[g] = p.last_ev["vector"]

    slot_base = [0, 8, 24, 48]
    ev_exp_hist = []
    ev_av_hist = []
    prev_evac = {0: None, 1: None}
    prev_nm_w = [None] * 4
    prev_at_w = [None] * 4
    at_spill = {}
    prev_dn = None
    ev_rr_prev = [None] * 4
    nm_deferred = []

    def flush_nm():
        if not nm_deferred:
            return
        s_, u, ev_rr = nm_deferred.pop(0)
        j2 = u % 4
        p.emit("vector", lambda eng, j2=j2, u=u: eng.tensor_mul(
            sb["AT"][0:64, j2 * 512:(j2 + 1) * 512],
            sb["Anum"][0:64, j2 * 512:(j2 + 1) * 512],
            sb["rrow"][0:64, u % 4]),
            waits=[ev_rr, prev_at_w[j2]], inc="dve")
        ev_nm = p.emit("vector", lambda eng, j2=j2, u=u: eng.tensor_mul(
            sb["AT"][64:128, j2 * 512:(j2 + 1) * 512],
            sb["Anum"][64:128, j2 * 512:(j2 + 1) * 512],
            sb["rrow"][64:128, u % 4]), inc="dve")
        ev_rr_prev[u % 4] = ev_nm
        prev_nm_w[j2] = ev_nm
        evsp = p.emit("sync", lambda eng, s_=s_, u=u, j2=j2: eng.dma_start(
            out=at_dram[u][:, s_ * 512:(s_ + 1) * 512],
            in_=sb["AT"][:, j2 * 512:(j2 + 1) * 512]),
            waits=[ev_nm], inc="qsp", n=16)
        prev_at_w[j2] = evsp
        at_spill[(s_, u)] = evsp
    qt_load = {}
    qt_done = {}

    def emit_qtload(s, t):
        sti = s * 16 + t
        w = [qt_done.get(sti - 2), ev_qspill[s * 16 + t]]
        if sti == 0:
            w.append(ev_tr1[NT1 - 1])
        qt_load[sti] = p.emit("sync", lambda eng, s=s, t=t: eng.dma_start(
            out=sb["QTp"][:, t % 2], in_=qt_dram[t][:, s * 512:(s + 1) * 512]),
            waits=w, inc="qsp", n=16)

    def emit_p3_slot(s):
        nonlocal prev_dn
        Ks = KS[s]
        if s == 0:
            emit_qtload(0, 0)
            emit_qtload(0, 1)
        ev_evacB = [None] * 16
        ev_dn_ev = [None] * 16
        for t in range(16):
            flush_nm()
            sti = s * 16 + t
            kt = t // 4
            ev_exp_u = {}
            ev_dep_u = {}
            ev_av_u = {}

            NB_A = PN_A if t % 2 == 0 else PM1
            NB_B = PN_B if t % 2 == 0 else PM2

            def emit_av(pp, NB_A=NB_A, NB_B=NB_B):
                nown = 4 * (s + 1)
                kb = pp if pp < nown else 16 + (pp - nown)
                bufP = pp % 4
                avw = [ev_dep_u[pp]]
                if pp == 0:
                    avw.append(prev_evac[t % 2])
                p.emit("tensor", lambda eng, kb=kb, kt=kt, bufP=bufP, pp=pp, Ks=Ks: eng.matmul(
                    NB_A[0:65, :],
                    sb["VP"][:, kb * 520 + (2 * kt) * 65: kb * 520 + (2 * kt) * 65 + 65],
                    sb["PT"][:, bufP][:, 0:512], start=(pp == 0), stop=(pp == Ks - 1)),
                    waits=avw, inc="pe")
                ev_av_u[pp] = p.emit("tensor", lambda eng, kb=kb, kt=kt, bufP=bufP, pp=pp, Ks=Ks: eng.matmul(
                    NB_B[0:65, :],
                    sb["VP"][:, kb * 520 + (2 * kt + 1) * 65: kb * 520 + (2 * kt + 1) * 65 + 65],
                    sb["PT"][:, bufP][:, 512:1024], start=(pp == 0), stop=(pp == Ks - 1)),
                    inc="pe")

            for pp in range(Ks):
                nown = 4 * (s + 1)
                kb = pp if pp < nown else 16 + (pp - nown)
                diag = (4 * s <= pp < nown)
                mj = pp - 4 * s if diag else 0
                bufS = pp % 2
                bufP = pp % 4
                w = [ev_kv_ready[kb // 4]]
                if pp == 0:
                    w.append(qt_load[sti])
                    if sti == 0:
                        w.append(ev_ev1[NT1 - 1])
                if pp >= 2:
                    w.append(ev_exp_u[pp - 2])
                elif len(ev_exp_hist) >= 2:
                    w.append(ev_exp_hist[-2])
                p.emit("tensor", lambda eng, kt=kt, kb=kb, bufS=bufS, t=t: eng.matmul(
                    PSBIG[:, bufS * 1024: bufS * 1024 + 512],
                    sb["KT"][0:64, kt * P_ALL + kb * 128: kt * P_ALL + (kb + 1) * 128],
                    sb["QTp"][0:64, t % 2], start=True, stop=True),
                    waits=w, inc="pe")
                ev_s2 = p.emit("tensor", lambda eng, kt=kt, kb=kb, bufS=bufS, t=t: eng.matmul(
                    PSBIG[:, bufS * 1024 + 512: bufS * 1024 + 1024],
                    sb["KT"][64:128, kt * P_ALL + kb * 128: kt * P_ALL + (kb + 1) * 128],
                    sb["QTp"][64:128, t % 2], start=True, stop=True), inc="pe")
                ew = [ev_s2]
                if pp >= 4:
                    ew.append(ev_av_u[pp - 4])
                elif ev_av_hist:
                    ew.append(ev_av_hist[-1])
                if not ev_exp_hist:
                    ew.append(ld0)
                    ew.append(ev_tr1[NT1 - 1])
                ev_exp = p.emit("scalar", lambda eng, s=s, pp=pp, bufS=bufS, bufP=bufP: eng.activation(
                    sb["PT"][:, bufP], PSBIG[:, bufS * 1024:(bufS + 1) * 1024], AF.Exp,
                    bias=sb["biasmat"][:, slot_base[s] + pp: slot_base[s] + pp + 1],
                    scale=0.125), waits=ew, inc="act", nochain=True)
                ev_exp_hist.append(ev_exp)
                ev_exp_u[pp] = ev_exp
                dep = ev_exp
                if diag:
                    dep = p.emit("vector", lambda eng, mj=mj, bufP=bufP: eng.tensor_mul(
                        sb["PT"][:, bufP], sb["PT"][:, bufP],
                        sb["masks"][:, mj * 1024:(mj + 1) * 1024]),
                        waits=[ev_exp], inc="dve")
                ev_dep_u[pp] = dep
                if pp >= 1:
                    emit_av(pp - 1)
            emit_av(Ks - 1)
            ev_av_hist.append(ev_av_u[Ks - 1])
            qt_done[sti] = ev_av_u[Ks - 1]
            if sti + 2 < 64:
                emit_qtload((sti + 2) // 16, (sti + 2) % 16)
            j = t % 4
            p.emit("vector", lambda eng, j=j, NB_A=NB_A: eng.tensor_copy(
                sb["Anum"][0:64, j * 512:(j + 1) * 512], NB_A[0:64, :]),
                waits=[ev_av_hist[-1], prev_nm_w[j]], inc="dve")
            p.emit("vector", lambda eng, j=j, NB_B=NB_B: eng.tensor_copy(
                sb["Anum"][64:128, j * 512:(j + 1) * 512], NB_B[0:64, :]), inc="dve")
            p.emit("vector", lambda eng, NB_A=NB_A: eng.tensor_copy(
                sb["dnrow"][0:1, 0:512], NB_A[64:65, :]), waits=[prev_dn], inc="dve")
            ev_d2 = p.emit("vector", lambda eng, NB_B=NB_B: eng.tensor_copy(
                sb["dnrow"][0:1, 512:1024], NB_B[64:65, :]), inc="dve")
            ev_evacB[t] = ev_d2
            prev_evac[t % 2] = ev_d2
            ev_dn_ev[t] = p.emit("sync", lambda eng, s=s, t=t: eng.dma_start(
                out=dn_dram[s:s + 1, t * 1024:(t + 1) * 1024], in_=sb["dnrow"][0:1, :]),
                waits=[ev_d2], inc="qsp", n=16)
            prev_dn = ev_dn_ev[t]
            if t % 4 == 3:
                t0 = t - 3
                ev_ld = p.emit("sync", lambda eng, s=s, t0=t0: eng.dma_start(
                    out=sb["rrsl"][:, :],
                    in_=dn_dram[s, t0 * 1024:(t0 + 4) * 1024].rearrange("(p f) -> p f", p=128)),
                    waits=[ev_dn_ev[t]], inc="qsp", n=16)
                ev_rc = p.emit("vector", lambda eng: eng.reciprocal(
                    sb["rrsl"][:, :], sb["rrsl"][:, :]), waits=[ev_ld], inc="dve")
                ev_st = p.emit("sync", lambda eng, s=s, t0=t0: eng.dma_start(
                    out=rr_dram[s, t0 * 1024:(t0 + 4) * 1024].rearrange("(p f) -> p f", p=128),
                    in_=sb["rrsl"][:, :]), waits=[ev_rc], inc="qsp", n=16)
                for u in range(t0, t0 + 4):
                    p.emit("sync", lambda eng, s=s, u=u: eng.dma_start(
                        out=sb["rrow"][0:64, u % 4],
                        in_=rr_dram[s, u * 1024:u * 1024 + 512].unsqueeze(0).broadcast_to([64, 512])),
                        waits=[ev_st, ev_rr_prev[u % 4]], inc="qsp", n=16)
                    ev_rr = p.emit("sync", lambda eng, s=s, u=u: eng.dma_start(
                        out=sb["rrow"][64:128, u % 4],
                        in_=rr_dram[s, u * 1024 + 512:(u + 1) * 1024].unsqueeze(0).broadcast_to([64, 512])),
                        inc="qsp", n=16)
                    nm_deferred.append((s, u, ev_rr))

    for w in range(4):
        for i in range(8 * w, 8 * w + 8):
            emit_p1a_tile(i)
        ev_w = emit_p1b_wave(w)
        for i in range(8 * w, 8 * w + 8):
            emit_p1c_tile(i, ev_w)
        emit_p2_group(2 * w)
        emit_p2_group(2 * w + 1)
    emit_p3_slot(0)
    emit_p3_slot(1)
    ev_gload0 = None
    for c in range(16):
        w0 = [ev_qspill[63], ev_xn1t_done[7]] if c == 0 else []
        ev_gload0 = p.emit("gpsimd", lambda eng, c=c: eng.dma_start(
            out=sb["atg0"][:, c * 512:(c + 1) * 512], in_=at_dram[c][:, 0:512]),
            waits=w0 + [at_spill[(0, c)]], inc="qgp", n=16)
    for c in range(8):
        p.emit("gpsimd", lambda eng, c=c: eng.dma_start(
            out=sb["wout"][:, c * 2048:(c + 1) * 2048], in_=di["wout"][c]),
            waits=[ev_xn1t_done[7]] if c == 0 else [], inc="qgp", n=16)
    p.emit("gpsimd", lambda eng: eng.dma_start(
        out=sb["bout_row"][:, :], in_=di["bout_row"]),
        waits=[ev_cs_done[7]], inc="qgp", n=16)
    emit_p3_slot(2)
    for c in range(8, 16):
        p.emit("gpsimd", lambda eng, c=c: eng.dma_start(
            out=sb["wout"][:, c * 2048:(c + 1) * 2048], in_=di["wout"][c]), inc="qgp", n=16)
    ev_wout = ("qgp", p.counts["qgp"])
    emit_p3_slot(3)
    while nm_deferred:
        flush_nm()
    p.barrier()

    ev_gdone = [None] * 4
    ev_gload = [None] * 4
    ev_x2sp = [None] * 16
    ev_r4 = [None] * 16
    ev_s4 = [None] * 16
    prev_psbig = None
    prev_sqr4 = None
    prev_tile = {}
    for gg in range(4):
        atb = "atg0" if gg % 2 == 0 else "atg1"
        if gg == 0:
            ev_gload[0] = ev_gload0
        else:
            if gg == 1:
                w0 = [qt_done[63]]
            else:
                w0 = [ev_gdone[gg - 2]]
            ev_at = None
            for c in range(16):
                ev_at = p.emit("gpsimd", lambda eng, gg=gg, c=c, atb=atb: eng.dma_start(
                    out=sb[atb][:, c * 512:(c + 1) * 512],
                    in_=at_dram[c][:, gg * 512:(gg + 1) * 512]),
                    waits=(w0 if c == 0 else []) + [at_spill[(gg, c)]], inc="qgp", n=16)
            ev_gload[gg] = ev_at
        if gg == 3:
            ev_lorab = p.emit("gpsimd", lambda eng: eng.dma_start(
                out=sb["loraB"][:, :], in_=di["loraB"]),
                waits=[ev_gdone[2]], inc="qgp", n=16)
            ev_rt = p.emit("gpsimd", lambda eng: eng.dma_start(
                out=sb["routerT"][:, :], in_=di["routerT"].rearrange("p c j -> p (c j)")),
                inc="qgp", n=16)
            ev_w5s = p.emit("gpsimd", lambda eng: eng.dma_start(
                out=sb["bbase_row"][:, :], in_=di["bbase_row"]), inc="qgp", n=16)
        last_mm = None
        for ti in range(4):
            tix = gg * 4 + ti
            row0 = tix * 128
            bft = tix % 2
            ev_x = p.emit("sync", lambda eng, row0=row0, bft=bft: eng.dma_start(
                out=sb["x2t"][:, bft], in_=di["xp"][row0:row0 + 128, :]),
                waits=[prev_tile.get(("sp", bft)), prev_tile.get(("rd", bft))],
                inc="qsp", n=16)
            ev_mm = None
            for c in range(16):
                wmm = []
                if c == 0:
                    wmm = [ev_gload[gg], ev_wout, prev_psbig]
                for nn in range(4):
                    ev_mm = p.emit("tensor", lambda eng, c=c, nn=nn, ti=ti, atb="atg0" if gg % 2 == 0 else "atg1": eng.matmul(
                        PSBIG[:, nn * 512:(nn + 1) * 512],
                        sb[atb][:, c * 512 + ti * 128: c * 512 + (ti + 1) * 128],
                        sb["wout"][:, c * 2048 + nn * 512: c * 2048 + (nn + 1) * 512],
                        start=(c == 0), stop=False),
                        waits=wmm if nn == 0 else [], inc="pe")
            for nn in range(4):
                ev_mm = p.emit("tensor", lambda eng, nn=nn: eng.matmul(
                    PSBIG[:, nn * 512:(nn + 1) * 512], sb["onestok"][0:1, :],
                    sb["bout_row"][0:1, nn * 512:(nn + 1) * 512],
                    start=False, stop=True), inc="pe")
            last_mm = ev_mm
            evs_x2 = []
            for nn in range(4):
                w2 = [ev_mm, ev_x] if nn == 0 else []
                evs_x2.append(p.emit("vector", lambda eng, nn=nn, bft=bft: eng.tensor_add(
                    sb["x2t"][:, bft][:, nn * 512:(nn + 1) * 512],
                    PSBIG[:, nn * 512:(nn + 1) * 512],
                    sb["x2t"][:, bft][:, nn * 512:(nn + 1) * 512]), waits=w2, inc="dve"))
            prev_psbig = evs_x2[3]
            ev_q4 = p.emit("scalar", lambda eng, bft=bft: eng.activation(
                sb["xsq4"][:, :], sb["x2t"][:, bft], AF.Square),
                waits=[evs_x2[3], prev_sqr4], inc="act")
            ev_r4[tix] = p.emit("vector", lambda eng, bft=bft, tix=tix: eng.reduce_sum(
                sb["mvall"][:, 2 * tix:2 * tix + 1], sb["x2t"][:, bft], AX.X),
                waits=[evs_x2[3]], inc="dve", nochain=True)
            ev_s4[tix] = p.emit("vector", lambda eng, tix=tix: eng.reduce_sum(
                sb["mvall"][:, 2 * tix + 1:2 * tix + 2], sb["xsq4"][:, :], AX.X),
                waits=[ev_q4], inc="dve", nochain=True)
            prev_sqr4 = ev_s4[tix]
            ev_x2sp[tix] = p.emit("sync", lambda eng, row0=row0, bft=bft: eng.dma_start(
                out=x2_dram[row0:row0 + 128, :], in_=sb["x2t"][:, bft]),
                waits=[evs_x2[3]], inc="qsp", n=16)
            prev_tile[("sp", bft)] = ev_x2sp[tix]
            prev_tile[("rd", bft)] = ev_r4[tix]
        ev_gdone[gg] = last_mm
    last_mm15 = ev_gdone[3]
    ev_la = p.emit("gpsimd", lambda eng: eng.dma_start(
        out=sb["loraA"][:, :], in_=di["loraA"].rearrange("p c j -> p (c j)")),
        waits=[last_mm15], inc="qgp", n=16)
    ev_wb = [None] * 16
    for c in range(16):
        ev_wb[c] = p.emit("gpsimd", lambda eng, c=c: eng.dma_start(
            out=sb["wbase"][:, c * 2048:(c + 1) * 2048], in_=di["wbaseT"][c]),
            inc="qgp", n=16)

    p.emit("vector", lambda eng: eng.scalar_tensor_tensor(
        sb["sqw"][:, 0:16], sb["mvall"][:, 0:32:2],
        -1.0 / (2048.0 * 2048.0), sb["mvall"][:, 0:32:2],
        ALU.mult, ALU.mult), waits=[ev_s4[15]], inc="dve")
    p.emit("vector", lambda eng: eng.scalar_tensor_tensor(
        sb["var32"][:, 0:16], sb["mvall"][:, 1:32:2],
        1.0 / 2048.0, sb["sqw"][:, 0:16], ALU.mult, ALU.add), inc="dve")
    ev_o3 = p.emit("vector", lambda eng: eng.tensor_scalar(
        sb["var32"][:, 0:16], sb["var32"][:, 0:16], 1e-5, None, ALU.add), inc="dve")
    ev_ln4 = p.emit("scalar", lambda eng: eng.activation(
        sb["var32"][:, 0:16], sb["var32"][:, 0:16], AF.Ln), waits=[ev_o3], inc="act")
    ev_ex4 = p.emit("scalar", lambda eng: eng.activation(
        sb["rstd32"][:, 0:16], sb["var32"][:, 0:16], AF.Exp, scale=-0.5), inc="act")
    ev_m4 = p.emit("vector", lambda eng: eng.tensor_scalar(
        sb["m32"][:, 0:16], sb["mvall"][:, 0:32:2], 1.0 / 2048.0, None, ALU.mult),
        waits=[ev_ex4], inc="dve")

    ev_ld4 = [None] * 16
    ev_ts2 = [None] * 16
    ev_tr2 = [None] * 16
    ev_ev2 = [None] * 16
    ev_mme = [None] * 16
    ev_lge = [None] * 16
    for i in range(16):
        bf = i % 2
        enq, qn = ("sync", "qsp") if bf == 0 else ("scalar", "qsc")
        wl = [ev_x2sp[i]]
        if i >= 2:
            wl.append(ev_ts2[i - 2])
        else:
            wl += [ev_r4[14 + i], ev_x2sp[14 + i]]
        ev_ld4[i] = p.emit(enq, lambda eng, i=i, bf=bf: eng.dma_start(
            out=sb["x2t"][:, bf], in_=x2_dram[i * 128:(i + 1) * 128, :]),
            waits=wl, inc=qn, n=16)
        ev_ts2[i] = p.emit("vector", lambda eng, i=i, bf=bf: eng.tensor_scalar(
            sb["xn2t"][:, bf], sb["x2t"][:, bf], sb["m32"][:, i:i + 1],
            sb["rstd32"][:, i:i + 1], ALU.subtract, ALU.mult),
            waits=[ev_ld4[i], ev_m4, ev_tr2[i - 2] if i >= 2 else None],
            inc="dve", nochain=True)
        bks = (PM1b, PM2b) if bf == 0 else (PN_Ab, PN_Bb)
        ev_tr7 = None
        evt = None
        for c in range(16):
            wtr = []
            if c == 0:
                wtr = [ev_ts2[i], ev_ev2[i - 2] if i >= 2 else None]
            evt = p.emit("tensor", lambda eng, c=c, bf=bf, bks=bks: eng.transpose(
                bks[c // 8][:, (c % 8) * 128:(c % 8 + 1) * 128],
                sb["xn2t"][:, bf][:, c * 128:(c + 1) * 128], sb["ident"][:, :]),
                waits=wtr, inc="pe")
            if c == 7:
                ev_tr7 = evt
        ev_tr2[i] = evt
        p.emit("scalar", lambda eng, i=i, bks=bks: eng.activation(
            sb["xn2T"][:, i * 2048:i * 2048 + 1024], bks[0][:, 0:1024], AF.Copy),
            waits=[ev_tr7], inc="act")
        ev_ev2[i] = p.emit("scalar", lambda eng, i=i, bks=bks: eng.activation(
            sb["xn2T"][:, i * 2048 + 1024:(i + 1) * 2048], bks[1][:, 0:1024], AF.Copy),
            waits=[ev_tr2[i]], inc="act")
        mcol = 0 if bf == 0 else 1024
        lcol = 512 if bf == 0 else 1536
        evm = None
        evlg = None
        for c in range(16):
            wm = []
            if c == 0:
                wm = [ev_ev2[i], ev_la,
                      ev_mme[i - 2] if i >= 2 else prev_psbig]
            evm = p.emit("tensor", lambda eng, c=c, i=i, mcol=mcol: eng.matmul(
                PSBIG[:, mcol:mcol + 128],
                sb["xn2T"][:, i * 2048 + c * 128:i * 2048 + (c + 1) * 128],
                sb["loraA"][:, c * 128:(c + 1) * 128],
                start=(c == 0), stop=(c == 15)),
                waits=wm, inc="pe")
            wl2 = []
            if c == 0:
                wl2 = [ev_rt, ev_lge[i - 2] if i >= 2 else None]
            evlg = p.emit("tensor", lambda eng, c=c, i=i, lcol=lcol: eng.matmul(
                PSBIG[:, lcol:lcol + 8],
                sb["xn2T"][:, i * 2048 + c * 128:i * 2048 + (c + 1) * 128],
                sb["routerT"][:, c * 8:(c + 1) * 8],
                start=(c == 0), stop=(c == 15)),
                waits=wl2, inc="pe")
        ev_mme[i] = p.emit("vector", lambda eng, i=i, mcol=mcol: eng.tensor_copy(
            sb["midall"][:, i * 128:(i + 1) * 128], PSBIG[:, mcol:mcol + 128]),
            waits=[evm], inc="dve", nochain=True)
        ev_lge[i] = p.emit("vector", lambda eng, i=i, lcol=lcol: eng.tensor_add(
            sb["lgall"][:, i * 8:(i + 1) * 8], PSBIG[:, lcol:lcol + 8], sb["rbb"][:, :]),
            waits=[evlg], inc="dve", nochain=True)
    def v16(ap):
        return ap.rearrange("p (t e) -> p t e", t=16)

    p.emit("vector", lambda eng: eng.reduce_max(
        sb["mx16"][:, :].unsqueeze(2), v16(sb["lgall"][:, :]), AX.X),
        waits=[ev_lge[15]], inc="dve")
    ev_p2s = p.emit("vector", lambda eng: eng.tensor_sub(
        v16(sb["lgall"][:, :]), v16(sb["lgall"][:, :]),
        sb["mx16"][:, :].unsqueeze(2).broadcast_to([128, 16, 8])), inc="dve")
    ev_p2e = p.emit("scalar", lambda eng: eng.activation(
        sb["lgall"][:, :], sb["lgall"][:, :], AF.Exp), waits=[ev_p2s], inc="act")
    p.emit("vector", lambda eng: eng.reduce_max(
        sb["mx16"][:, :].unsqueeze(2), v16(sb["lgall"][:, :]), AX.X),
        waits=[ev_p2e], inc="dve")
    p.emit("vector", lambda eng: eng.tensor_tensor(
        v16(sb["gmaskb"][:, :]), v16(sb["lgall"][:, :]),
        sb["mx16"][:, :].unsqueeze(2).broadcast_to([128, 16, 8]), ALU.is_ge), inc="dve")
    p.emit("vector", lambda eng: eng.scalar_tensor_tensor(
        sb["gateb"][:, :], sb["gmaskb"][:, :], -2.0, sb["lgall"][:, :],
        ALU.mult, ALU.add), inc="dve")
    p.emit("vector", lambda eng: eng.reduce_max(
        sb["m2_16"][:, :].unsqueeze(2), v16(sb["gateb"][:, :]), AX.X), inc="dve")
    p.emit("vector", lambda eng: eng.tensor_tensor(
        v16(sb["gmaskb"][:, :]), v16(sb["lgall"][:, :]),
        sb["m2_16"][:, :].unsqueeze(2).broadcast_to([128, 16, 8]), ALU.is_ge), inc="dve")
    p.emit("vector", lambda eng: eng.tensor_mul(
        sb["gateb"][:, :], sb["lgall"][:, :], sb["gmaskb"][:, :]), inc="dve")
    p.emit("vector", lambda eng: eng.reduce_sum(
        sb["gs16"][:, :].unsqueeze(2), v16(sb["gateb"][:, :]), AX.X), inc="dve")
    p.emit("vector", lambda eng: eng.reciprocal(sb["rg16"][:, :], sb["gs16"][:, :]), inc="dve")
    ev_g2f = p.emit("vector", lambda eng: eng.scalar_tensor_tensor(
        v16(sb["gateb"][:, :]), v16(sb["gateb"][:, :]), 2.0,
        sb["rg16"][:, :].unsqueeze(2).broadcast_to([128, 16, 8]),
        ALU.mult, ALU.mult), inc="dve")
    ev_mg = [None] * 16
    for i2 in range(16):
        ev_mg[i2] = p.emit("vector", lambda eng, i2=i2: eng.tensor_tensor(
            sb["midall"][:, i2 * 128:(i2 + 1) * 128].rearrange("p (e r) -> p e r", e=8),
            sb["midall"][:, i2 * 128:(i2 + 1) * 128].rearrange("p (e r) -> p e r", e=8),
            sb["gateb"][:, i2 * 8:(i2 + 1) * 8].unsqueeze(2).broadcast_to([128, 8, 16]),
            ALU.mult), waits=[ev_g2f], inc="dve", nochain=True)

    ev_store = [None] * 16
    ev_mtc = [None] * 16
    ev_lbm = [None] * 16
    prev_epn = None
    for ti in range(16):
        row0 = ti * 128
        bf = ti % 2
        enq, qn = ("sync", "qsp") if bf == 0 else ("scalar", "qsc")
        wx = [ev_ts2[14 + bf]] if ti < 2 else [ev_epn_map[ti - 2][0]]
        ev_xld = p.emit(enq, lambda eng, row0=row0, bf=bf: eng.dma_start(
            out=sb["xti"][:, bf], in_=x2_dram[row0:row0 + 128, :]),
            waits=wx, inc=qn, n=16)
        evb = None
        for c in range(16):
            for nn in range(4):
                wb_ = []
                if nn == 0:
                    wb_ = [ev_wb[c]]
                if c == 0:
                    wb_.append(ev_lge[15] if ti == 0 else prev_epn[nn])
                evb = p.emit("tensor", lambda eng, c=c, nn=nn, ti=ti: eng.matmul(
                    PSBIG[:, nn * 512:(nn + 1) * 512],
                    sb["xn2T"][:, ti * 2048 + c * 128:ti * 2048 + (c + 1) * 128],
                    sb["wbase"][:, c * 2048 + nn * 512:c * 2048 + (nn + 1) * 512],
                    start=(c == 0), stop=False),
                    waits=wb_, inc="pe")
        for nn in range(4):
            evb = p.emit("tensor", lambda eng, nn=nn: eng.matmul(
                PSBIG[:, nn * 512:(nn + 1) * 512], sb["onestok"][0:1, :],
                sb["bbase_row"][0:1, nn * 512:(nn + 1) * 512],
                start=False, stop=False), waits=[ev_w5s] if nn == 0 else [], inc="pe")
        evtr5 = p.emit("tensor", lambda eng, ti=ti, bf=bf: eng.transpose(
            (PM1b if bf == 0 else PN_Ab)[:, 0:128],
            sb["midall"][:, ti * 128:(ti + 1) * 128], sb["ident"][:, :]),
            waits=[ev_mg[ti], ev_mtc[ti - 2] if ti >= 2 else None], inc="pe")
        ev_mtc[ti] = p.emit("vector", lambda eng, ti=ti, bf=bf: eng.tensor_copy(
            sb["midT2"][:, bf], (PM1b if bf == 0 else PN_Ab)[:, 0:128]),
            waits=[evtr5, ev_lbm[ti - 2] if ti >= 2 else None], inc="dve", nochain=True)
        evlb = None
        for nn in range(4):
            wlb = [ev_mtc[ti], ev_lorab] if nn == 0 else []
            evlb = p.emit("tensor", lambda eng, nn=nn, bf=bf: eng.matmul(
                PSBIG[:, nn * 512:(nn + 1) * 512], sb["midT2"][:, bf],
                sb["loraB"][:, nn * 512:(nn + 1) * 512],
                start=False, stop=True), waits=wlb, inc="pe")
        ev_lbm[ti] = evlb
        ev_epn = []
        for nn in range(4):
            w2 = [evlb, ev_xld] if nn == 0 else []
            if nn == 0 and ti >= 2:
                w2.append(ev_store[ti - 2])
            ev_epn.append(p.emit("vector", lambda eng, nn=nn, bf=bf: eng.tensor_add(
                sb["outt"][:, bf][:, nn * 512:(nn + 1) * 512],
                PSBIG[:, nn * 512:(nn + 1) * 512],
                sb["xti"][:, bf][:, nn * 512:(nn + 1) * 512]), waits=w2, inc="dve"))
        prev_epn = ev_epn
        if ti == 0:
            ev_epn_map = {}
        ev_epn_map[ti] = ev_epn
        ev_store[ti] = p.emit(enq, lambda eng, row0=row0, bf=bf: eng.dma_start(
            out=out_dram[row0:row0 + 128, :], in_=sb["outt"][:, bf]),
            waits=[ev_epn[3]], inc=qn, n=16)

    p.emit("vector", lambda eng: eng.engine_nop(), waits=[ev_store[-1]])
    p.build()
    return nc


def _rope_tables():
    e = np.arange(0, 32, 2, dtype=np.float32) / np.float32(32)
    inv = (np.float32(1.0) / np.float32(10000.0) ** e).astype(np.float32)
    t = np.arange(S, dtype=np.float32)
    fr = np.outer(t, inv).astype(np.float32)
    return np.cos(fr).astype(np.float32), np.sin(fr).astype(np.float32)


def _pair_heads(t):
    kt, i = divmod(t, 4)
    return 8 * kt + i, 8 * kt + 4 + i


def _qcol_perm():
    cols = []
    for t in range(16):
        hA, hB = _pair_heads(t)
        cols += list(range(hA * 64, hA * 64 + 16))
        cols += list(range(hB * 64, hB * 64 + 16))
        cols += list(range(hA * 64 + 16, hA * 64 + 32))
        cols += list(range(hB * 64 + 16, hB * 64 + 32))
        cols += list(range(hA * 64 + 32, hA * 64 + 64))
        cols += list(range(hB * 64 + 32, hB * 64 + 64))
    return np.array(cols)


def _kcol_perm():
    cols = []
    for kt in range(4):
        a, b = 2 * kt, 2 * kt + 1
        cols += list(range(a * 64, a * 64 + 16))
        cols += list(range(b * 64, b * 64 + 16))
        cols += list(range(a * 64 + 16, a * 64 + 32))
        cols += list(range(b * 64 + 16, b * 64 + 32))
        cols += list(range(a * 64 + 32, a * 64 + 64))
        cols += list(range(b * 64 + 32, b * 64 + 64))
    return np.array(cols)


def _grot_gpass():
    G = np.zeros((128, 128), dtype=np.float32)
    P = np.zeros((128, 128), dtype=np.float32)
    for i in range(16):
        G[i, i] = 1.0
        G[32 + i, i] = -1.0
        G[64 + i, 16 + i] = 1.0
        G[96 + i, 16 + i] = 1.0
        G[16 + i, 64 + i] = 1.0
        G[48 + i, 64 + i] = -1.0
        G[80 + i, 80 + i] = 1.0
        G[112 + i, 80 + i] = 1.0
    for i in range(32):
        P[64 + i, 32 + i] = 1.0
        P[96 + i, 96 + i] = 1.0
    return G.astype(BF), P.astype(BF)


def _wout_row_perm():
    rows = np.zeros(2048, dtype=np.int64)
    for t in range(16):
        hA, hB = _pair_heads(t)
        rows[t * 128: t * 128 + 64] = np.arange(hA * 64, hA * 64 + 64)
        rows[t * 128 + 64: t * 128 + 128] = np.arange(hB * 64, hB * 64 + 64)
    return rows


def _tile4(w):
    d, n = w.shape
    nt = n // 128
    r = w.reshape(16, 128, nt, 128)
    return np.ascontiguousarray(r.transpose(2, 1, 0, 3))


def _masks():
    k = np.arange(128)[:, None]
    q = np.arange(512)[None, :]
    out = []
    for j in range(4):
        mj = (q >= 128 * j + k).astype(np.float32).astype(BF)
        out.append(np.concatenate([mj, mj], axis=1))
    return np.ascontiguousarray(np.concatenate(out, axis=1))


def _biasmat(par):
    bm = np.full((128, 80), SOFT_SHIFT, dtype=np.float32)
    if par == 0:
        base = 0
        for s_ in range(4):
            Ks = KS[s_]
            bm[:, base + Ks - 4: base + Ks] = -10000.0
            base += Ks
    return bm


def host_prepare(inputs):
    x = np.asarray(inputs["L0_attn_x"], dtype=np.float32)
    Wqkv = np.asarray(inputs["L0_attn_Wqkv"], dtype=np.float32)
    bqkv = np.asarray(inputs["L0_attn_bqkv"], dtype=np.float32)
    Wout = np.asarray(inputs["L0_attn_Wout"], dtype=np.float32)
    bout = np.asarray(inputs["L0_attn_bout"], dtype=np.float32)
    base_w = np.asarray(inputs["L0_ffn_base_w"], dtype=np.float32)
    base_b = np.asarray(inputs["L0_ffn_base_b"], dtype=np.float32)
    router_w = np.asarray(inputs["L0_ffn_router_w"], dtype=np.float32)
    router_b = np.asarray(inputs["L0_ffn_router_b"], dtype=np.float32)
    lora_A = np.asarray(inputs["L0_ffn_lora_A"], dtype=np.float32)
    lora_B = np.asarray(inputs["L0_ffn_lora_B"], dtype=np.float32)

    cos, sin = _rope_tables()
    qperm = _qcol_perm()
    kperm = _kcol_perm()
    woutperm = _wout_row_perm()

    wq_h = _tile4(np.ascontiguousarray(Wqkv[:, :2048][:, qperm]))
    wk_h = _tile4(np.ascontiguousarray(Wqkv[:, 2048:2560][:, kperm]))
    wv_h = _tile4(np.ascontiguousarray(Wqkv[:, 2560:3072]))
    wout_h = np.ascontiguousarray(Wout[woutperm].reshape(16, 128, D))
    wbaseT_h = np.ascontiguousarray(base_w.T.reshape(16, 128, D))
    A_all = np.ascontiguousarray(lora_A.transpose(1, 0, 2).reshape(D, 128))
    loraA_h = np.ascontiguousarray(A_all.reshape(16, 128, 128).transpose(1, 0, 2))
    loraB_h = np.ascontiguousarray(lora_B.reshape(128, D))
    routerT_h = np.ascontiguousarray(router_w.T.reshape(16, 128, 8).transpose(1, 0, 2)).astype(BF)

    bq_h = np.ascontiguousarray(bqkv[:2048][qperm].reshape(16, 128).T)
    bk_h = np.ascontiguousarray(bqkv[2048:2560][kperm].reshape(4, 128).T)
    bv_h = np.ascontiguousarray(bqkv[2560:3072].reshape(4, 128).T)
    rbb_h = np.ascontiguousarray(np.broadcast_to(router_b[None, :], (128, 8)))
    grot, gpass = _grot_gpass()
    shared = {
        "masks": _masks(), "grot": grot, "gpass": gpass, "ident": np.eye(128, dtype=BF),
        "ones1": np.ones((1, 64), dtype=np.float32),
        "onestok": np.ones((1, 128), dtype=BF),
        "wq": wq_h, "wk": wk_h, "wv": wv_h, "wout": wout_h,
        "wbaseT": wbaseT_h, "loraA": loraA_h, "loraB": loraB_h,
        "routerT": routerT_h, "bq": bq_h, "bk": bk_h, "bv": bv_h,
        "rbb": rbb_h, "bout_row": bout[None, :].astype(BF),
        "bbase_row": base_b[None, :].astype(BF),
    }

    in_maps = []
    metas = []
    for core in range(8):
        b, par = divmod(core, 2)
        own = [2 * i + par for i in range(4)]
        partner = [2 * i + (1 - par) for i in range(4)]
        rows = np.concatenate([np.arange(k * CH, (k + 1) * CH) for k in own + partner])
        xp = np.ascontiguousarray(x[b][rows])
        ct = cos[rows].T
        st_ = sin[rows].T
        ccss = np.ascontiguousarray(np.concatenate([ct, ct, st_, st_], axis=0))
        sscc = np.ascontiguousarray(np.concatenate([st_, st_, ct, ct], axis=0))
        m = dict(shared)
        m.update({"xp": xp, "ccss": ccss, "sscc": sscc, "biasmat": _biasmat(par)})
        in_maps.append(m)
        metas.append((b, own))
    return in_maps, metas


def kernel(**inputs):
    if "nc" not in _CACHE:
        _CACHE["nc"] = build_nc()
    nc = _CACHE["nc"]
    in_maps, metas = host_prepare(inputs)
    res = run_bass_kernel_spmd(nc, in_maps, core_ids=list(range(8)))
    _CACHE["last"] = res
    out = np.zeros((B, S, D), dtype=np.float32)
    for core in range(8):
        b, own = metas[core]
        o = np.asarray(res.results[core]["out"], dtype=np.float32)
        for i, k in enumerate(own):
            out[b, k * CH:(k + 1) * CH] = o[i * CH:(i + 1) * CH]
    return out


# version 19
# speedup vs baseline: 1.4103x; 1.0609x over previous
import numpy as np
import ml_dtypes
import concourse.bass as bass
import concourse.mybir as mybir
from concourse.bass_utils import run_bass_kernel_spmd

F32 = mybir.dt.float32
BF16 = mybir.dt.bfloat16
AF = mybir.ActivationFunctionType
ALU = mybir.AluOpType
AX = mybir.AxisListType
BF = ml_dtypes.bfloat16

B, S, D = 4, 4096, 2048
CH = 512
T_OWN = 2048
P_ALL = 4096
NSLOT = 4
KS = [8, 16, 24, 32]
SOFT_SHIFT = -8.0

_TAPS = False
_PHASES = 5
_CACHE = {}


class Prog:
    ENGINES = ("sync", "scalar", "vector", "tensor", "gpsimd")

    def __init__(self, nc):
        self.nc = nc
        self.steps = []
        self.sems = {}
        self.counts = {}
        self.last_ev = {}
        self._pending = {}

    def sem(self, name):
        if name not in self.sems:
            self.sems[name] = self.nc.alloc_semaphore(name)
            self.counts[name] = 0
        return self.sems[name]

    def emit(self, engine, fn, waits=(), inc=None, n=1, nochain=False):
        w = [x for x in waits if x is not None]
        if engine in self._pending:
            w += self._pending.pop(engine)
        ev = None
        incs = []
        if inc is not None:
            self.sem(inc)
            self.counts[inc] += n
            ev = (inc, self.counts[inc])
            incs = [(inc, n)]
            self.last_ev[engine] = ev
        self.steps.append((engine, fn, w, incs, nochain))
        return ev

    def barrier(self):
        evs = list(self.last_ev.values())
        for e in self.ENGINES:
            self._pending[e] = list(evs)

    def build(self):
        nc = self.nc
        with nc.Block() as block:
            for eng_name in self.ENGINES:
                if not any(s[0] == eng_name for s in self.steps):
                    continue

                def make(en):
                    def emit_stream(eng):
                        waited = {}
                        own = {"vector": "dve", "scalar": "act"}.get(en)
                        cnt = 0
                        for (e, fn, waits, incs, nochain) in self.steps:
                            if e != en:
                                continue
                            if own is not None and cnt > 0 and own in self.sems \
                                    and not nochain and waited.get(own, -1) < cnt:
                                eng.wait_ge(self.sems[own], cnt)
                                waited[own] = cnt
                            for (s, v) in waits:
                                if waited.get(s, -1) < v:
                                    eng.wait_ge(self.sems[s], v)
                                    waited[s] = v
                            ins = fn(eng)
                            for (s, a) in incs:
                                ins = ins.then_inc(self.sems[s], a)
                                if s == own:
                                    cnt += a
                    return emit_stream

                getattr(block, eng_name)(make(eng_name))


def build_nc():
    nc = bass.Bass("TRN2", target_bir_lowering=False)
    p = Prog(nc)

    di = {}

    def inp(name, shape, dt):
        di[name] = nc.dram_tensor(name, list(shape), dt, kind="ExternalInput").ap()

    inp("xp", [P_ALL, D], F32)
    inp("ccss", [64, P_ALL], F32)
    inp("sscc", [64, P_ALL], F32)
    inp("biasmat", [128, 80], F32)
    inp("masks", [128, 4096], BF16)
    inp("grot", [128, 128], BF16)
    inp("gpass", [128, 128], BF16)
    inp("ident", [128, 128], BF16)
    inp("ones1", [1, 64], F32)
    inp("onestok", [1, 128], BF16)
    inp("wq", [16, 128, 16, 128], F32)
    inp("wk", [4, 128, 16, 128], F32)
    inp("wv", [4, 128, 16, 128], F32)
    inp("wout", [16, 128, D], F32)
    inp("wbaseT", [16, 128, D], F32)
    inp("loraA", [128, 16, 128], F32)
    inp("loraB", [128, D], F32)
    inp("routerT", [128, 16, 8], BF16)
    inp("bq", [128, 16], F32)
    inp("bk", [128, 4], F32)
    inp("bv", [128, 4], F32)
    inp("rbb", [128, 8], F32)
    inp("bout_row", [1, D], BF16)
    inp("bbase_row", [1, D], BF16)

    ikind = "ExternalOutput" if _TAPS else "Internal"
    qt_dram = nc.dram_tensor("qt_dram", [16, 128, T_OWN], BF16, kind=ikind).ap()
    at_dram = nc.dram_tensor("at_dram", [16, 128, T_OWN], BF16, kind=ikind).ap()
    x2_dram = nc.dram_tensor("x2_dram", [T_OWN, D], F32, kind=ikind).ap()
    xn2_dram = nc.dram_tensor("xn2_dram", [T_OWN, D], BF16, kind=ikind).ap()
    dn_dram = nc.dram_tensor("dn_dram", [NSLOT, 16384], F32, kind=ikind).ap()
    rr_dram = nc.dram_tensor("rr_dram", [NSLOT, 16384], F32, kind="Internal").ap()
    out_dram = nc.dram_tensor("out", [T_OWN, D], F32, kind="ExternalOutput").ap()

    sb = {}
    cursors = {}

    def sba(name, shape, dt, region, reg_start):
        if region not in cursors:
            cursors[region] = reg_start
        esz = 2 if dt == BF16 else 4
        nb = 1
        for s_ in shape[1:]:
            nb *= s_
        nb *= esz
        off = cursors[region]
        cursors[region] = off + ((nb + 31) // 32) * 32
        sb[name] = nc.alloc_sbuf_tensor_at(name, list(shape), dt, offset=off)
        return sb[name]

    K0 = 18432
    A0 = 38912
    B0 = 108544
    C0 = 178176
    LIMIT = 229376

    sba("ident", [128, 128], BF16, "k", K0)
    sba("grot", [128, 128], BF16, "k", K0)
    sba("gpass", [128, 128], BF16, "k", K0)
    sba("ones1", [1, 64], F32, "k", K0)
    sba("onestok", [1, 128], BF16, "k", K0)
    sba("bq", [128, 16], F32, "k", K0)
    sba("bk", [128, 4], F32, "k", K0)
    sba("bv", [128, 4], F32, "k", K0)
    sba("rbb", [128, 8], F32, "k", K0)
    sba("biasmat", [128, 80], F32, "k", K0)
    sba("masks", [128, 4096], BF16, "k", K0)
    sba("trep", [128, 8], F32, "k", K0)
    sba("mvall", [128, 64], F32, "k", K0)
    sba("sqw", [128, 16], F32, "k", K0)
    sba("var32", [128, 32], F32, "k", K0)
    sba("rstd32", [128, 32], F32, "k", K0)
    sba("m32", [128, 32], F32, "k", K0)
    off_ccs = cursors["k"]
    sba("ccs", [64, 2, 512], F32, "k", K0)
    sba("ssc", [64, 2, 512], F32, "k", K0)

    sba("KT", [128, 4 * P_ALL], BF16, "A23", A0)
    sba("VP", [128, 32 * 520], BF16, "A23", A0)
    sba("rrsl", [128, 32], F32, "A23", A0)
    sba("u", [64, 512], F32, "A23", A0)
    sba("vf", [128, 512], BF16, "A23", A0)
    sba("xn2T", [128, 16 * T_OWN], BF16, "A5", A0)
    assert max(cursors["A23"], cursors["A5"]) <= B0

    sba("wkv", [128, 8 * D], BF16, "B2", B0)
    sba("xn1T", [128, 2, 16 * 512], BF16, "B2", B0)
    sba("dnrow", [1, 1024], F32, "B2", B0)
    sba("wout", [128, 16 * D], BF16, "B4", B0)
    sba("wbase", [128, 16 * D], BF16, "B5", B0)
    assert max(cursors["B2"], cursors["B4"], cursors["B5"]) <= C0

    sba("xpt", [128, 2, D], F32, "C1", C0)
    sba("xnt", [128, 2, D], BF16, "C1", C0)
    sba("xsq", [128, D], BF16, "C1", C0)
    sba("mv", [128, 2], F32, "k", None)
    sba("sq1", [128, 1], F32, "k", None)
    sba("stdc", [128, 1], F32, "k", None)
    sba("rstd", [128, 1], F32, "k", None)
    C2S = C0 + 32768
    sba("wqs", [128, 2, 2048], BF16, "C2", C2S)
    sba("wrope", [128, 512], BF16, "C2", C2S)
    sba("qrope", [128, 512], BF16, "C2", C2S)
    sba("qstage", [128, 2, 512], BF16, "C2", C2S)
    sba("PT", [128, 4, 1024], BF16, "C3a", C0)
    C3B = C0 + 8192
    sba("QTp", [128, 2, 512], BF16, "C3b", C3B)
    sba("Anum", [128, 4 * 512], F32, "C3b", C3B)
    sba("AT", [128, 4 * 512], BF16, "C3b", C3B)
    sba("rrow", [128, 4, 512], F32, "C3b", C3B)
    sb["atg0"] = nc.alloc_sbuf_tensor_at("atg0", [128, 16 * 512], BF16, offset=C2S)
    sb["atg1"] = nc.alloc_sbuf_tensor_at("atg1", [128, 16 * 512], BF16, offset=A0 + 32768)
    sba("x2t", [128, 2, D], F32, "C4", C0)
    sba("xn2t", [128, 2, D], BF16, "C4", C0)
    sba("xsq4", [128, D], F32, "C4", C0)
    sb["bout_row"] = nc.alloc_sbuf_tensor_at("bout_row", [1, D], BF16, offset=off_ccs)
    sba("xti", [128, 2, D], F32, "C5", C0)
    sba("outt", [128, 2, D], F32, "C5", C0)
    sba("loraA", [128, 16 * 128], BF16, "C5", C0)
    sba("loraB", [128, D], BF16, "C5", C0)
    sba("routerT", [128, 128], BF16, "C5", C0)
    sba("midall", [128, 16 * 128], BF16, "C5", C0)
    sba("midT2", [128, 2, 128], BF16, "C5", C0)
    sba("bbase_row", [1, D], BF16, "C5", C0)
    C5G = C0 + 24576
    sba("lgall", [128, 128], F32, "C5g", C5G)
    sba("gmaskb", [128, 128], F32, "C5g", C5G)
    sba("gateb", [128, 128], F32, "C5g", C5G)
    sba("mx16", [128, 16], F32, "C5g", C5G)
    sba("m2_16", [128, 16], F32, "C5g", C5G)
    sba("gs16", [128, 16], F32, "C5g", C5G)
    sba("rg16", [128, 16], F32, "C5g", C5G)
    assert cursors["C5g"] <= C0 + 32768
    assert cursors["k"] <= A0, cursors["k"]
    for r in ("C1", "C2", "C3b", "C4", "C5", "k", "A23"):
        assert cursors[r] <= LIMIT, (r, cursors[r])

    PSBIG = nc.alloc_psum_tensor("PSBIG", [128, 2048], F32)
    PN_A = nc.alloc_psum_tensor("PN_A", [128, 512], F32)
    PN_B = nc.alloc_psum_tensor("PN_B", [128, 512], F32)
    PM1 = nc.alloc_psum_tensor("PM1", [128, 512], F32)
    PM2 = nc.alloc_psum_tensor("PM2", [128, 512], F32)
    PM1b = PM1[:, :].bitcast(BF16)
    PM2b = PM2[:, :].bitcast(BF16)
    PN_Ab = PN_A[:, :].bitcast(BF16)
    PN_Bb = PN_B[:, :].bitcast(BF16)
    PSBIGb = PSBIG[:, :].bitcast(BF16)

    for name in ["ident", "grot", "gpass", "ones1", "onestok", "bq", "bk", "bv", "rbb",
                 "biasmat", "masks"]:
        p.emit("sync", lambda eng, n=name: eng.dma_start(out=sb[n][:, :], in_=di[n]),
               inc="qsp", n=16)
    ld0 = ("qsp", p.counts["qsp"])
    p.emit("vector", lambda eng: eng.memset(sb["trep"][:, 2:8], -1.0), inc="dve")
    p.emit("vector", lambda eng: eng.memset(sb["qrope"][0:64, :], 0.0), inc="dve")
    ev_vones = p.emit("vector", lambda eng: eng.memset(
        sb["VP"][:, :].rearrange("p (k g c) -> p k g c", k=32, g=8)[:, :, :, 64:65], 1.0),
        inc="dve")

    NT1 = P_ALL // 128
    ev_tr1 = [None] * NT1
    ev_ev1 = [None] * NT1
    ev_sqr = [None] * (NT1 + 16)
    ev_xn = [None] * NT1

    def emit_p1a_tile(i):
        bf = i % 2
        enq, qn = ("sync", "qsp") if bf == 0 else ("scalar", "qsc")
        evl = p.emit(enq, lambda eng, i=i, bf=bf: eng.dma_start(
            out=sb["xpt"][:, bf], in_=di["xp"][i * 128:(i + 1) * 128, :]),
            waits=[ev_sqr[i - 2]] if i % 8 >= 2 else [ev_xn[i - 2]], inc=qn, n=16)
        ev_q = p.emit("scalar", lambda eng, bf=bf: eng.activation(
            sb["xsq"][:, :], sb["xpt"][:, bf], AF.Square),
            waits=[evl, ev_sqr[i - 1] if i >= 1 else None], inc="act", nochain=True)
        p.emit("vector", lambda eng, i=i, bf=bf: eng.reduce_sum(
            sb["mvall"][:, 2 * (i % 16):2 * (i % 16) + 1], sb["xpt"][:, bf], AX.X),
            waits=[evl], inc="dve", nochain=True)
        ev_sqr[i] = p.emit("vector", lambda eng, i=i: eng.reduce_sum(
            sb["mvall"][:, 2 * (i % 16) + 1:2 * (i % 16) + 2], sb["xsq"][:, :], AX.X),
            waits=[ev_q], inc="dve", nochain=True)

    def emit_p1b_wave(w):
        c0 = 16 * (w % 2)
        ev_o1 = p.emit("vector", lambda eng, c0=c0: eng.scalar_tensor_tensor(
            sb["sqw"][:, 0:8], sb["mvall"][:, c0:c0 + 16:2],
            -1.0 / (2048.0 * 2048.0), sb["mvall"][:, c0:c0 + 16:2],
            ALU.mult, ALU.mult), waits=[ev_sqr[8 * w + 7]], inc="dve")
        ev_o2 = p.emit("vector", lambda eng, w=w, c0=c0: eng.scalar_tensor_tensor(
            sb["var32"][:, 8 * w:8 * w + 8], sb["mvall"][:, c0 + 1:c0 + 16:2],
            1.0 / 2048.0, sb["sqw"][:, 0:8], ALU.mult, ALU.add), inc="dve")
        ev_o3 = p.emit("vector", lambda eng, w=w: eng.tensor_scalar(
            sb["var32"][:, 8 * w:8 * w + 8], sb["var32"][:, 8 * w:8 * w + 8],
            1e-5, None, ALU.add), inc="dve")
        ev_ln = p.emit("scalar", lambda eng, w=w: eng.activation(
            sb["var32"][:, 8 * w:8 * w + 8], sb["var32"][:, 8 * w:8 * w + 8],
            AF.Ln), waits=[ev_o3], inc="act")
        ev_ex = p.emit("scalar", lambda eng, w=w: eng.activation(
            sb["rstd32"][:, 8 * w:8 * w + 8], sb["var32"][:, 8 * w:8 * w + 8],
            AF.Exp, scale=-0.5), inc="act")
        ev_m = p.emit("vector", lambda eng, w=w, c0=c0: eng.tensor_scalar(
            sb["m32"][:, 8 * w:8 * w + 8], sb["mvall"][:, c0:c0 + 16:2],
            1.0 / 2048.0, None, ALU.mult), waits=[ev_ex], inc="dve")
        return ev_m

    def emit_p1c_tile(i, ev_w):
        bf = i % 2
        g, j = divmod(i, 4)
        enq, qn = ("sync", "qsp") if bf == 0 else ("scalar", "qsc")
        evl = p.emit(enq, lambda eng, i=i, bf=bf: eng.dma_start(
            out=sb["xpt"][:, bf], in_=di["xp"][i * 128:(i + 1) * 128, :]),
            waits=[ev_xn[i - 2], ev_sqr[(i // 8) * 8 + 6 + bf]], inc=qn, n=16)
        ev_xn[i] = p.emit("vector", lambda eng, i=i, bf=bf: eng.tensor_scalar(
            sb["xnt"][:, bf], sb["xpt"][:, bf], sb["m32"][:, i:i + 1],
            sb["rstd32"][:, i:i + 1], ALU.subtract, ALU.mult),
            waits=[evl, ev_w, ev_tr1[i - 2]], inc="dve", nochain=True)
        evt = None
        for c in range(16):
            wtr = [ev_xn[i], ev_ev1[i - 2]] if c == 0 else []
            evt = p.emit("tensor", lambda eng, i=i, c=c, bf=bf: eng.transpose(
                PSBIGb[:, (i % 2) * 2048 + c * 128:(i % 2) * 2048 + (c + 1) * 128],
                sb["xnt"][:, bf][:, c * 128:(c + 1) * 128], sb["ident"][:, :]),
                waits=wtr, inc="pe")
        ev_tr1[i] = evt
        wev = [ev_tr1[i]]
        if j == 0 and g >= 2:
            wev.append(ev_xn1t_done[g - 2])
        ev_ev1[i] = p.emit("vector", lambda eng, i=i, g=g, j=j: eng.tensor_copy(
            sb["xn1T"][:, g % 2].rearrange("p (c x) -> p c x", c=16)[:, :, j * 128:(j + 1) * 128],
            PSBIGb[:, (i % 2) * 2048:(i % 2 + 1) * 2048].rearrange("p (c x) -> p c x", c=16)),
            waits=wev, inc="dve", nochain=True)

    for t in range(4):
        p.emit("gpsimd", lambda eng, t=t: eng.dma_start(
            out=sb["wkv"][:, t * 2048:(t + 1) * 2048],
            in_=di["wk"][t].rearrange("p c j -> p (c j)")), inc="qgp", n=16)
        p.emit("gpsimd", lambda eng, t=t: eng.dma_start(
            out=sb["wkv"][:, (4 + t) * 2048:(5 + t) * 2048],
            in_=di["wv"][t].rearrange("p c j -> p (c j)")), inc="qgp", n=16)
    ev_wkv = ("qgp", p.counts["qgp"])

    NG = P_ALL // 512
    ev_kv_ready = [None] * NG
    ev_xn1t = [None] * NG
    ev_xn1t_done = [None] * NG
    ev_cs = [None] * NG
    ev_cs_done = [None] * NG
    ev_qspill = {}
    qload = {}
    qdone = {}
    prev_perm = None
    prev_vtr = None
    ps_free = {"PN_A": None, "PN_B": None, "PM1": None, "PM2": None}

    def emit_qload(qi):
        g, t = divmod(qi, 16)
        w = [qdone.get(qi - 2), ev_wkv if qi == 0 else None]
        qload[qi] = p.emit("gpsimd", lambda eng, t=t: eng.dma_start(
            out=sb["wqs"][:, t % 2],
            in_=di["wq"][t].rearrange("p c j -> p (c j)")),
            waits=w, inc="qgp", n=16)

    emit_qload(0)
    emit_qload(1)

    def emit_p2_group(g):
        nonlocal prev_vtr
        own = g < 4
        bf = g % 2
        ev_xn1t[g] = ev_ev1[4 * g + 3]
        csw = [ev_cs_done[g - 2]] if g >= 2 else []
        p.emit("sync", lambda eng, g=g, bf=bf: eng.dma_start(
            out=sb["ccs"][:, bf], in_=di["ccss"][:, g * 512:(g + 1) * 512]),
            waits=csw, inc="qsp", n=16)
        ev_cs[g] = p.emit("sync", lambda eng, g=g, bf=bf: eng.dma_start(
            out=sb["ssc"][:, bf], in_=di["sscc"][:, g * 512:(g + 1) * 512]),
            inc="qsp", n=16)

        def proj_mms(psum_ap, w_fn, fw):
            ev = None
            for c in range(16):
                ev = p.emit("tensor", lambda eng, c=c, psum_ap=psum_ap, w_fn=w_fn, bf=bf: eng.matmul(
                    psum_ap, w_fn(c), sb["xn1T"][:, bf][:, c * 512:(c + 1) * 512],
                    start=(c == 0), stop=(c == 15)),
                    waits=list(fw) if c == 0 else [], inc="pe")
            return ev

        def rope_evac(psrc, bias_col, dst_copy_fn, extra_pm_waits, ev_mm):
            nonlocal prev_perm
            p.emit("vector", lambda eng, psrc=psrc, bias_col=bias_col: eng.tensor_scalar(
                sb["qrope"][64:128, :], psrc[64:128, :], bias_col[64:128, :], None, ALU.add),
                waits=[ev_cs[g], prev_perm, ev_mm], inc="dve")
            p.emit("vector", lambda eng, psrc=psrc, bias_col=bias_col, bf=bf: eng.scalar_tensor_tensor(
                sb["wrope"][0:64, :], psrc[0:64, :], bias_col[0:64, :],
                sb["ccs"][:, bf], ALU.add, ALU.mult), inc="dve")
            ev_b = p.emit("vector", lambda eng, psrc=psrc, bias_col=bias_col, bf=bf: eng.scalar_tensor_tensor(
                sb["wrope"][64:128, :], psrc[0:64, :], bias_col[0:64, :],
                sb["ssc"][:, bf], ALU.add, ALU.mult), inc="dve")
            ev_cs_done[g] = ev_b
            p.emit("tensor", lambda eng: eng.matmul(
                PM2[:, :], sb["grot"][:, :], sb["wrope"][:, :], start=True, stop=False),
                waits=[ev_b, ps_free["PM2"]] + list(extra_pm_waits), inc="pe")
            ev_pm = p.emit("tensor", lambda eng: eng.matmul(
                PM2[:, :], sb["gpass"][:, :], sb["qrope"][:, :], start=False, stop=True),
                inc="pe")
            prev_perm = ev_pm
            ev_cp = dst_copy_fn(ev_pm)
            ps_free["PM2"] = ev_cp
            return ev_b, ev_cp

        for kt in range(4):
            fw = [ev_wkv, ev_xn1t[g], ps_free["PN_A"]]
            if g == 0 and kt == 0:
                fw.append(ld0)
            ev_mm = proj_mms(PN_A[:, :],
                             lambda c, kt=kt: sb["wkv"][:, kt * 2048 + c * 128: kt * 2048 + (c + 1) * 128],
                             fw)
            def kcopy(ev_pm, kt=kt, g=g):
                return p.emit("vector", lambda eng, kt=kt, g=g: eng.tensor_copy(
                    sb["KT"][:, kt * P_ALL + g * 512: kt * P_ALL + (g + 1) * 512], PM2[:, :]),
                    waits=[ev_pm], inc="dve")
            ev_pass, _ = rope_evac(PN_A, sb["bk"][:, kt:kt + 1], kcopy, [], ev_mm)
            ps_free["PN_A"] = ev_pass

        for kt in range(4):
            fw = [ev_wkv, ev_xn1t[g], ps_free["PN_B"]]
            ev_mm = proj_mms(PN_B[:, :],
                             lambda c, kt=kt: sb["wkv"][:, (4 + kt) * 2048 + c * 128: (4 + kt) * 2048 + (c + 1) * 128],
                             fw)
            ev_vf = p.emit("vector", lambda eng, kt=kt: eng.tensor_scalar(
                sb["vf"][:, :], PN_B[:, :], sb["bv"][:, kt:kt + 1], None, ALU.add),
                waits=[ev_mm, ev_vones, prev_vtr], inc="dve")
            ps_free["PN_B"] = ev_vf
            for j in range(4):
                ktile = g * 4 + j
                ev_tr = p.emit("tensor", lambda eng, j=j: eng.transpose(
                    PM1b[:, 0:128], sb["vf"][:, j * 128:(j + 1) * 128], sb["ident"][:, :]),
                    waits=[ev_vf, ps_free["PM1"]], inc="pe")
                prev_vtr = ev_tr
                e2 = p.emit("vector", lambda eng, ktile=ktile, kt=kt: eng.tensor_copy(
                    sb["VP"][:, ktile * 520 + (2 * kt) * 65: ktile * 520 + (2 * kt) * 65 + 130]
                    .rearrange("q (h c) -> q h c", h=2)[:, :, 0:64],
                    PM1b[:, 0:128].rearrange("q (h c) -> q h c", h=2)),
                    waits=[ev_tr], inc="dve")
                ps_free["PM1"] = e2

        if own:
            for t in range(16):
                qi = g * 16 + t
                fw = [qload[qi], ev_xn1t[g], ps_free["PM1"]]
                ev_mm = None
                for c in range(16):
                    ev_mm = p.emit("tensor", lambda eng, c=c, t=t, bf=bf: eng.matmul(
                        PM1[:, :], sb["wqs"][:, t % 2][:, c * 128:(c + 1) * 128],
                        sb["xn1T"][:, bf][:, c * 512:(c + 1) * 512],
                        start=(c == 0), stop=(c == 15)),
                        waits=list(fw) if c == 0 else [], inc="pe")
                qdone[qi] = ev_mm

                def qcopy(ev_pm, g=g, t=t, qi=qi):
                    ev_cp = p.emit("vector", lambda eng, t=t: eng.tensor_copy(
                        sb["qstage"][:, t % 2], PM2[:, :]),
                        waits=[ev_pm, ev_qspill.get(qi - 2)], inc="dve")
                    ev_qspill[qi] = p.emit("sync", lambda eng, t=t, g=g: eng.dma_start(
                        out=qt_dram[t][:, g * 512:(g + 1) * 512], in_=sb["qstage"][:, t % 2]),
                        waits=[ev_cp], inc="qsp", n=16)
                    return ev_cp

                ev_pass, _ = rope_evac(PM1, sb["bq"][:, t:t + 1], qcopy, [], ev_mm)
                ps_free["PM1"] = ev_pass
                if qi + 2 < 64:
                    emit_qload(qi + 2)
        ev_xn1t_done[g] = p.last_ev["tensor"]
        ev_kv_ready[g] = p.last_ev["vector"]

    slot_base = [0, 8, 24, 48]
    ev_exp_hist = []
    ev_av_hist = []
    prev_evac = {0: None, 1: None}
    prev_nm_w = [None] * 4
    prev_at_w = [None] * 4
    at_spill = {}
    prev_dn = None
    ev_rr_prev = [None] * 4
    nm_deferred = []

    def flush_nm():
        if not nm_deferred:
            return
        s_, u, ev_rr = nm_deferred.pop(0)
        j2 = u % 4
        p.emit("vector", lambda eng, j2=j2, u=u: eng.tensor_mul(
            sb["AT"][0:64, j2 * 512:(j2 + 1) * 512],
            sb["Anum"][0:64, j2 * 512:(j2 + 1) * 512],
            sb["rrow"][0:64, u % 4]),
            waits=[ev_rr, prev_at_w[j2]], inc="dve")
        ev_nm = p.emit("vector", lambda eng, j2=j2, u=u: eng.tensor_mul(
            sb["AT"][64:128, j2 * 512:(j2 + 1) * 512],
            sb["Anum"][64:128, j2 * 512:(j2 + 1) * 512],
            sb["rrow"][64:128, u % 4]), inc="dve")
        ev_rr_prev[u % 4] = ev_nm
        prev_nm_w[j2] = ev_nm
        evsp = p.emit("sync", lambda eng, s_=s_, u=u, j2=j2: eng.dma_start(
            out=at_dram[u][:, s_ * 512:(s_ + 1) * 512],
            in_=sb["AT"][:, j2 * 512:(j2 + 1) * 512]),
            waits=[ev_nm], inc="qsp", n=16)
        prev_at_w[j2] = evsp
        at_spill[(s_, u)] = evsp
    qt_load = {}
    qt_done = {}

    def emit_qtload(s, t):
        sti = s * 16 + t
        w = [qt_done.get(sti - 2), ev_qspill[s * 16 + t]]
        if sti == 0:
            w.append(ev_tr1[NT1 - 1])
        qt_load[sti] = p.emit("sync", lambda eng, s=s, t=t: eng.dma_start(
            out=sb["QTp"][:, t % 2], in_=qt_dram[t][:, s * 512:(s + 1) * 512]),
            waits=w, inc="qsp", n=16)

    def emit_p3_slot(s):
        nonlocal prev_dn
        Ks = KS[s]
        if s == 0:
            emit_qtload(0, 0)
            emit_qtload(0, 1)
        ev_evacB = [None] * 16
        ev_dn_ev = [None] * 16
        for t in range(16):
            flush_nm()
            sti = s * 16 + t
            kt = t // 4
            ev_exp_u = {}
            ev_dep_u = {}
            ev_av_u = {}

            NB_A = PN_A if t % 2 == 0 else PM1
            NB_B = PN_B if t % 2 == 0 else PM2

            def emit_av(pp, NB_A=NB_A, NB_B=NB_B):
                nown = 4 * (s + 1)
                kb = pp if pp < nown else 16 + (pp - nown)
                bufP = pp % 4
                avw = [ev_dep_u[pp]]
                if pp == 0:
                    avw.append(prev_evac[t % 2])
                p.emit("tensor", lambda eng, kb=kb, kt=kt, bufP=bufP, pp=pp, Ks=Ks: eng.matmul(
                    NB_A[0:65, :],
                    sb["VP"][:, kb * 520 + (2 * kt) * 65: kb * 520 + (2 * kt) * 65 + 65],
                    sb["PT"][:, bufP][:, 0:512], start=(pp == 0), stop=(pp == Ks - 1)),
                    waits=avw, inc="pe")
                ev_av_u[pp] = p.emit("tensor", lambda eng, kb=kb, kt=kt, bufP=bufP, pp=pp, Ks=Ks: eng.matmul(
                    NB_B[0:65, :],
                    sb["VP"][:, kb * 520 + (2 * kt + 1) * 65: kb * 520 + (2 * kt + 1) * 65 + 65],
                    sb["PT"][:, bufP][:, 512:1024], start=(pp == 0), stop=(pp == Ks - 1)),
                    inc="pe")

            for pp in range(Ks):
                nown = 4 * (s + 1)
                kb = pp if pp < nown else 16 + (pp - nown)
                diag = (4 * s <= pp < nown)
                mj = pp - 4 * s if diag else 0
                bufS = pp % 2
                bufP = pp % 4
                w = [ev_kv_ready[kb // 4]]
                if pp == 0:
                    w.append(qt_load[sti])
                    if sti == 0:
                        w.append(ev_ev1[NT1 - 1])
                if pp >= 2:
                    w.append(ev_exp_u[pp - 2])
                elif len(ev_exp_hist) >= 2:
                    w.append(ev_exp_hist[-2])
                p.emit("tensor", lambda eng, kt=kt, kb=kb, bufS=bufS, t=t: eng.matmul(
                    PSBIG[:, bufS * 1024: bufS * 1024 + 512],
                    sb["KT"][0:64, kt * P_ALL + kb * 128: kt * P_ALL + (kb + 1) * 128],
                    sb["QTp"][0:64, t % 2], start=True, stop=True),
                    waits=w, inc="pe")
                ev_s2 = p.emit("tensor", lambda eng, kt=kt, kb=kb, bufS=bufS, t=t: eng.matmul(
                    PSBIG[:, bufS * 1024 + 512: bufS * 1024 + 1024],
                    sb["KT"][64:128, kt * P_ALL + kb * 128: kt * P_ALL + (kb + 1) * 128],
                    sb["QTp"][64:128, t % 2], start=True, stop=True), inc="pe")
                ew = [ev_s2]
                if pp >= 4:
                    ew.append(ev_av_u[pp - 4])
                elif ev_av_hist:
                    ew.append(ev_av_hist[-1])
                if not ev_exp_hist:
                    ew.append(ld0)
                    ew.append(ev_tr1[NT1 - 1])
                ev_exp = p.emit("scalar", lambda eng, s=s, pp=pp, bufS=bufS, bufP=bufP: eng.activation(
                    sb["PT"][:, bufP], PSBIG[:, bufS * 1024:(bufS + 1) * 1024], AF.Exp,
                    bias=sb["biasmat"][:, slot_base[s] + pp: slot_base[s] + pp + 1],
                    scale=0.125), waits=ew, inc="act", nochain=True)
                ev_exp_hist.append(ev_exp)
                ev_exp_u[pp] = ev_exp
                dep = ev_exp
                if diag:
                    dep = p.emit("vector", lambda eng, mj=mj, bufP=bufP: eng.tensor_mul(
                        sb["PT"][:, bufP], sb["PT"][:, bufP],
                        sb["masks"][:, mj * 1024:(mj + 1) * 1024]),
                        waits=[ev_exp], inc="dve")
                ev_dep_u[pp] = dep
                if pp >= 1:
                    emit_av(pp - 1)
            emit_av(Ks - 1)
            ev_av_hist.append(ev_av_u[Ks - 1])
            qt_done[sti] = ev_av_u[Ks - 1]
            if sti + 2 < 64:
                emit_qtload((sti + 2) // 16, (sti + 2) % 16)
            j = t % 4
            p.emit("vector", lambda eng, j=j, NB_A=NB_A: eng.tensor_copy(
                sb["Anum"][0:64, j * 512:(j + 1) * 512], NB_A[0:64, :]),
                waits=[ev_av_hist[-1], prev_nm_w[j]], inc="dve")
            p.emit("vector", lambda eng, j=j, NB_B=NB_B: eng.tensor_copy(
                sb["Anum"][64:128, j * 512:(j + 1) * 512], NB_B[0:64, :]), inc="dve")
            p.emit("vector", lambda eng, NB_A=NB_A: eng.tensor_copy(
                sb["dnrow"][0:1, 0:512], NB_A[64:65, :]), waits=[prev_dn], inc="dve")
            ev_d2 = p.emit("vector", lambda eng, NB_B=NB_B: eng.tensor_copy(
                sb["dnrow"][0:1, 512:1024], NB_B[64:65, :]), inc="dve")
            ev_evacB[t] = ev_d2
            prev_evac[t % 2] = ev_d2
            ev_dn_ev[t] = p.emit("sync", lambda eng, s=s, t=t: eng.dma_start(
                out=dn_dram[s:s + 1, t * 1024:(t + 1) * 1024], in_=sb["dnrow"][0:1, :]),
                waits=[ev_d2], inc="qsp", n=16)
            prev_dn = ev_dn_ev[t]
            if t % 4 == 3:
                t0 = t - 3
                ev_ld = p.emit("sync", lambda eng, s=s, t0=t0: eng.dma_start(
                    out=sb["rrsl"][:, :],
                    in_=dn_dram[s, t0 * 1024:(t0 + 4) * 1024].rearrange("(p f) -> p f", p=128)),
                    waits=[ev_dn_ev[t]], inc="qsp", n=16)
                ev_rc = p.emit("vector", lambda eng: eng.reciprocal(
                    sb["rrsl"][:, :], sb["rrsl"][:, :]), waits=[ev_ld], inc="dve")
                ev_st = p.emit("sync", lambda eng, s=s, t0=t0: eng.dma_start(
                    out=rr_dram[s, t0 * 1024:(t0 + 4) * 1024].rearrange("(p f) -> p f", p=128),
                    in_=sb["rrsl"][:, :]), waits=[ev_rc], inc="qsp", n=16)
                for u in range(t0, t0 + 4):
                    p.emit("sync", lambda eng, s=s, u=u: eng.dma_start(
                        out=sb["rrow"][0:64, u % 4],
                        in_=rr_dram[s, u * 1024:u * 1024 + 512].unsqueeze(0).broadcast_to([64, 512])),
                        waits=[ev_st, ev_rr_prev[u % 4]], inc="qsp", n=16)
                    ev_rr = p.emit("sync", lambda eng, s=s, u=u: eng.dma_start(
                        out=sb["rrow"][64:128, u % 4],
                        in_=rr_dram[s, u * 1024 + 512:(u + 1) * 1024].unsqueeze(0).broadcast_to([64, 512])),
                        inc="qsp", n=16)
                    nm_deferred.append((s, u, ev_rr))

    for w in range(4):
        for i in range(8 * w, 8 * w + 8):
            emit_p1a_tile(i)
        ev_w = emit_p1b_wave(w)
        for i in range(8 * w, 8 * w + 8):
            emit_p1c_tile(i, ev_w)
        emit_p2_group(2 * w)
        emit_p2_group(2 * w + 1)
    emit_p3_slot(0)
    emit_p3_slot(1)
    ev_gload0 = None
    for c in range(16):
        w0 = [ev_qspill[63], ev_xn1t_done[7]] if c == 0 else []
        ev_gload0 = p.emit("gpsimd", lambda eng, c=c: eng.dma_start(
            out=sb["atg0"][:, c * 512:(c + 1) * 512], in_=at_dram[c][:, 0:512]),
            waits=w0 + [at_spill[(0, c)]], inc="qgp", n=16)
    for c in range(8):
        p.emit("gpsimd", lambda eng, c=c: eng.dma_start(
            out=sb["wout"][:, c * 2048:(c + 1) * 2048], in_=di["wout"][c]),
            waits=[ev_xn1t_done[7]] if c == 0 else [], inc="qgp", n=16)
    p.emit("gpsimd", lambda eng: eng.dma_start(
        out=sb["bout_row"][:, :], in_=di["bout_row"]),
        waits=[ev_cs_done[7]], inc="qgp", n=16)
    emit_p3_slot(2)
    for c in range(8, 16):
        p.emit("gpsimd", lambda eng, c=c: eng.dma_start(
            out=sb["wout"][:, c * 2048:(c + 1) * 2048], in_=di["wout"][c]), inc="qgp", n=16)
    ev_wout = ("qgp", p.counts["qgp"])
    emit_p3_slot(3)
    while nm_deferred:
        flush_nm()
    p.barrier()

    ev_gdone = [None] * 4
    ev_gload = [None] * 4
    ev_x2sp = [None] * 16
    ev_r4 = [None] * 16
    ev_s4 = [None] * 16
    prev_psbig = None
    prev_sqr4 = None
    prev_tile = {}
    for gg in range(4):
        atb = "atg0" if gg % 2 == 0 else "atg1"
        if gg == 0:
            ev_gload[0] = ev_gload0
        else:
            if gg == 1:
                w0 = [qt_done[63]]
            else:
                w0 = [ev_gdone[gg - 2]]
            ev_at = None
            for c in range(16):
                ev_at = p.emit("gpsimd", lambda eng, gg=gg, c=c, atb=atb: eng.dma_start(
                    out=sb[atb][:, c * 512:(c + 1) * 512],
                    in_=at_dram[c][:, gg * 512:(gg + 1) * 512]),
                    waits=(w0 if c == 0 else []) + [at_spill[(gg, c)]], inc="qgp", n=16)
            ev_gload[gg] = ev_at
        if gg == 3:
            ev_lorab = p.emit("gpsimd", lambda eng: eng.dma_start(
                out=sb["loraB"][:, :], in_=di["loraB"]),
                waits=[ev_gdone[2]], inc="qgp", n=16)
            ev_rt = p.emit("gpsimd", lambda eng: eng.dma_start(
                out=sb["routerT"][:, :], in_=di["routerT"].rearrange("p c j -> p (c j)")),
                inc="qgp", n=16)
            ev_w5s = p.emit("gpsimd", lambda eng: eng.dma_start(
                out=sb["bbase_row"][:, :], in_=di["bbase_row"]), inc="qgp", n=16)
        last_mm = None
        for ti in range(4):
            tix = gg * 4 + ti
            row0 = tix * 128
            bft = tix % 2
            ev_x = p.emit("sync", lambda eng, row0=row0, bft=bft: eng.dma_start(
                out=sb["x2t"][:, bft], in_=di["xp"][row0:row0 + 128, :]),
                waits=[prev_tile.get(("sp", bft)), prev_tile.get(("rd", bft))],
                inc="qsp", n=16)
            ev_mm = None
            for c in range(16):
                wmm = []
                if c == 0:
                    wmm = [ev_gload[gg], ev_wout, prev_psbig]
                for nn in range(4):
                    ev_mm = p.emit("tensor", lambda eng, c=c, nn=nn, ti=ti, atb="atg0" if gg % 2 == 0 else "atg1": eng.matmul(
                        PSBIG[:, nn * 512:(nn + 1) * 512],
                        sb[atb][:, c * 512 + ti * 128: c * 512 + (ti + 1) * 128],
                        sb["wout"][:, c * 2048 + nn * 512: c * 2048 + (nn + 1) * 512],
                        start=(c == 0), stop=False),
                        waits=wmm if nn == 0 else [], inc="pe")
            for nn in range(4):
                ev_mm = p.emit("tensor", lambda eng, nn=nn: eng.matmul(
                    PSBIG[:, nn * 512:(nn + 1) * 512], sb["onestok"][0:1, :],
                    sb["bout_row"][0:1, nn * 512:(nn + 1) * 512],
                    start=False, stop=True), inc="pe")
            last_mm = ev_mm
            evs_x2 = []
            for nn in range(4):
                w2 = [ev_mm, ev_x] if nn == 0 else []
                evs_x2.append(p.emit("vector", lambda eng, nn=nn, bft=bft: eng.tensor_add(
                    sb["x2t"][:, bft][:, nn * 512:(nn + 1) * 512],
                    PSBIG[:, nn * 512:(nn + 1) * 512],
                    sb["x2t"][:, bft][:, nn * 512:(nn + 1) * 512]), waits=w2, inc="dve"))
            prev_psbig = evs_x2[3]
            ev_q4 = p.emit("scalar", lambda eng, bft=bft: eng.activation(
                sb["xsq4"][:, :], sb["x2t"][:, bft], AF.Square),
                waits=[evs_x2[3], prev_sqr4], inc="act")
            ev_r4[tix] = p.emit("vector", lambda eng, bft=bft, tix=tix: eng.reduce_sum(
                sb["mvall"][:, 2 * tix:2 * tix + 1], sb["x2t"][:, bft], AX.X),
                waits=[evs_x2[3]], inc="dve", nochain=True)
            ev_s4[tix] = p.emit("vector", lambda eng, tix=tix: eng.reduce_sum(
                sb["mvall"][:, 2 * tix + 1:2 * tix + 2], sb["xsq4"][:, :], AX.X),
                waits=[ev_q4], inc="dve", nochain=True)
            prev_sqr4 = ev_s4[tix]
            ev_x2sp[tix] = p.emit("sync", lambda eng, row0=row0, bft=bft: eng.dma_start(
                out=x2_dram[row0:row0 + 128, :], in_=sb["x2t"][:, bft]),
                waits=[evs_x2[3]], inc="qsp", n=16)
            prev_tile[("sp", bft)] = ev_x2sp[tix]
            prev_tile[("rd", bft)] = ev_r4[tix]
        ev_gdone[gg] = last_mm
    last_mm15 = ev_gdone[3]
    ev_la = p.emit("gpsimd", lambda eng: eng.dma_start(
        out=sb["loraA"][:, :], in_=di["loraA"].rearrange("p c j -> p (c j)")),
        waits=[last_mm15], inc="qgp", n=16)
    ev_wb = [None] * 16
    for c in range(16):
        ev_wb[c] = p.emit("gpsimd", lambda eng, c=c: eng.dma_start(
            out=sb["wbase"][:, c * 2048:(c + 1) * 2048], in_=di["wbaseT"][c]),
            inc="qgp", n=16)

    p.emit("vector", lambda eng: eng.scalar_tensor_tensor(
        sb["sqw"][:, 0:16], sb["mvall"][:, 0:32:2],
        -1.0 / (2048.0 * 2048.0), sb["mvall"][:, 0:32:2],
        ALU.mult, ALU.mult), waits=[ev_s4[15]], inc="dve")
    p.emit("vector", lambda eng: eng.scalar_tensor_tensor(
        sb["var32"][:, 0:16], sb["mvall"][:, 1:32:2],
        1.0 / 2048.0, sb["sqw"][:, 0:16], ALU.mult, ALU.add), inc="dve")
    ev_o3 = p.emit("vector", lambda eng: eng.tensor_scalar(
        sb["var32"][:, 0:16], sb["var32"][:, 0:16], 1e-5, None, ALU.add), inc="dve")
    ev_ln4 = p.emit("scalar", lambda eng: eng.activation(
        sb["var32"][:, 0:16], sb["var32"][:, 0:16], AF.Ln), waits=[ev_o3], inc="act")
    ev_ex4 = p.emit("scalar", lambda eng: eng.activation(
        sb["rstd32"][:, 0:16], sb["var32"][:, 0:16], AF.Exp, scale=-0.5), inc="act")
    ev_m4 = p.emit("vector", lambda eng: eng.tensor_scalar(
        sb["m32"][:, 0:16], sb["mvall"][:, 0:32:2], 1.0 / 2048.0, None, ALU.mult),
        waits=[ev_ex4], inc="dve")

    ev_ld4 = [None] * 16
    ev_ts2 = [None] * 16
    ev_tr2 = [None] * 16
    ev_ev2 = [None] * 16
    ev_mme = [None] * 16
    ev_lge = [None] * 16
    for i in range(16):
        bf = i % 2
        enq, qn = ("sync", "qsp") if bf == 0 else ("scalar", "qsc")
        wl = [ev_x2sp[i]]
        if i >= 2:
            wl.append(ev_ts2[i - 2])
        else:
            wl += [ev_r4[14 + i], ev_x2sp[14 + i]]
        ev_ld4[i] = p.emit(enq, lambda eng, i=i, bf=bf: eng.dma_start(
            out=sb["x2t"][:, bf], in_=x2_dram[i * 128:(i + 1) * 128, :]),
            waits=wl, inc=qn, n=16)
        ev_ts2[i] = p.emit("vector", lambda eng, i=i, bf=bf: eng.tensor_scalar(
            sb["xn2t"][:, bf], sb["x2t"][:, bf], sb["m32"][:, i:i + 1],
            sb["rstd32"][:, i:i + 1], ALU.subtract, ALU.mult),
            waits=[ev_ld4[i], ev_m4, ev_tr2[i - 2] if i >= 2 else None],
            inc="dve", nochain=True)
        bks = (PM1b, PM2b) if bf == 0 else (PN_Ab, PN_Bb)
        ev_tr7 = None
        evt = None
        for c in range(16):
            wtr = []
            if c == 0:
                wtr = [ev_ts2[i], ev_ev2[i - 2] if i >= 2 else None]
            evt = p.emit("tensor", lambda eng, c=c, bf=bf, bks=bks: eng.transpose(
                bks[c // 8][:, (c % 8) * 128:(c % 8 + 1) * 128],
                sb["xn2t"][:, bf][:, c * 128:(c + 1) * 128], sb["ident"][:, :]),
                waits=wtr, inc="pe")
            if c == 7:
                ev_tr7 = evt
        ev_tr2[i] = evt
        p.emit("scalar", lambda eng, i=i, bks=bks: eng.activation(
            sb["xn2T"][:, i * 2048:i * 2048 + 1024], bks[0][:, 0:1024], AF.Copy),
            waits=[ev_tr7], inc="act")
        ev_ev2[i] = p.emit("scalar", lambda eng, i=i, bks=bks: eng.activation(
            sb["xn2T"][:, i * 2048 + 1024:(i + 1) * 2048], bks[1][:, 0:1024], AF.Copy),
            waits=[ev_tr2[i]], inc="act")
        mcol = 0 if bf == 0 else 1024
        lcol = 512 if bf == 0 else 1536
        evm = None
        evlg = None
        for c in range(16):
            wm = []
            if c == 0:
                wm = [ev_ev2[i], ev_la,
                      ev_mme[i - 2] if i >= 2 else prev_psbig]
            evm = p.emit("tensor", lambda eng, c=c, i=i, mcol=mcol: eng.matmul(
                PSBIG[:, mcol:mcol + 128],
                sb["xn2T"][:, i * 2048 + c * 128:i * 2048 + (c + 1) * 128],
                sb["loraA"][:, c * 128:(c + 1) * 128],
                start=(c == 0), stop=(c == 15)),
                waits=wm, inc="pe")
            wl2 = []
            if c == 0:
                wl2 = [ev_rt, ev_lge[i - 2] if i >= 2 else None]
            evlg = p.emit("tensor", lambda eng, c=c, i=i, lcol=lcol: eng.matmul(
                PSBIG[:, lcol:lcol + 8],
                sb["xn2T"][:, i * 2048 + c * 128:i * 2048 + (c + 1) * 128],
                sb["routerT"][:, c * 8:(c + 1) * 8],
                start=(c == 0), stop=(c == 15)),
                waits=wl2, inc="pe")
        ev_mme[i] = p.emit("vector", lambda eng, i=i, mcol=mcol: eng.tensor_copy(
            sb["midall"][:, i * 128:(i + 1) * 128], PSBIG[:, mcol:mcol + 128]),
            waits=[evm], inc="dve", nochain=True)
        ev_lge[i] = p.emit("vector", lambda eng, i=i, lcol=lcol: eng.tensor_add(
            sb["lgall"][:, i * 8:(i + 1) * 8], PSBIG[:, lcol:lcol + 8], sb["rbb"][:, :]),
            waits=[evlg], inc="dve", nochain=True)
    def v16(ap):
        return ap.rearrange("p (t e) -> p t e", t=16)

    p.emit("vector", lambda eng: eng.reduce_max(
        sb["mx16"][:, :].unsqueeze(2), v16(sb["lgall"][:, :]), AX.X),
        waits=[ev_lge[15]], inc="dve")
    ev_p2s = p.emit("vector", lambda eng: eng.tensor_sub(
        v16(sb["lgall"][:, :]), v16(sb["lgall"][:, :]),
        sb["mx16"][:, :].unsqueeze(2).broadcast_to([128, 16, 8])), inc="dve")
    ev_p2e = p.emit("scalar", lambda eng: eng.activation(
        sb["lgall"][:, :], sb["lgall"][:, :], AF.Exp), waits=[ev_p2s], inc="act")
    p.emit("vector", lambda eng: eng.reduce_max(
        sb["mx16"][:, :].unsqueeze(2), v16(sb["lgall"][:, :]), AX.X),
        waits=[ev_p2e], inc="dve")
    p.emit("vector", lambda eng: eng.tensor_tensor(
        v16(sb["gmaskb"][:, :]), v16(sb["lgall"][:, :]),
        sb["mx16"][:, :].unsqueeze(2).broadcast_to([128, 16, 8]), ALU.is_ge), inc="dve")
    p.emit("vector", lambda eng: eng.scalar_tensor_tensor(
        sb["gateb"][:, :], sb["gmaskb"][:, :], -2.0, sb["lgall"][:, :],
        ALU.mult, ALU.add), inc="dve")
    p.emit("vector", lambda eng: eng.reduce_max(
        sb["m2_16"][:, :].unsqueeze(2), v16(sb["gateb"][:, :]), AX.X), inc="dve")
    p.emit("vector", lambda eng: eng.tensor_tensor(
        v16(sb["gmaskb"][:, :]), v16(sb["lgall"][:, :]),
        sb["m2_16"][:, :].unsqueeze(2).broadcast_to([128, 16, 8]), ALU.is_ge), inc="dve")
    p.emit("vector", lambda eng: eng.tensor_mul(
        sb["gateb"][:, :], sb["lgall"][:, :], sb["gmaskb"][:, :]), inc="dve")
    p.emit("vector", lambda eng: eng.reduce_sum(
        sb["gs16"][:, :].unsqueeze(2), v16(sb["gateb"][:, :]), AX.X), inc="dve")
    p.emit("vector", lambda eng: eng.reciprocal(sb["rg16"][:, :], sb["gs16"][:, :]), inc="dve")
    ev_g2f = p.emit("vector", lambda eng: eng.scalar_tensor_tensor(
        v16(sb["gateb"][:, :]), v16(sb["gateb"][:, :]), 2.0,
        sb["rg16"][:, :].unsqueeze(2).broadcast_to([128, 16, 8]),
        ALU.mult, ALU.mult), inc="dve")
    ev_mg = [None] * 16
    for i2 in range(16):
        ev_mg[i2] = p.emit("vector", lambda eng, i2=i2: eng.tensor_tensor(
            sb["midall"][:, i2 * 128:(i2 + 1) * 128].rearrange("p (e r) -> p e r", e=8),
            sb["midall"][:, i2 * 128:(i2 + 1) * 128].rearrange("p (e r) -> p e r", e=8),
            sb["gateb"][:, i2 * 8:(i2 + 1) * 8].unsqueeze(2).broadcast_to([128, 8, 16]),
            ALU.mult), waits=[ev_g2f], inc="dve", nochain=True)

    ev_store = [None] * 16
    ev_mtc = [None] * 16
    ev_lbm = [None] * 16
    prev_epn = None
    for ti in range(16):
        row0 = ti * 128
        bf = ti % 2
        enq, qn = ("sync", "qsp") if bf == 0 else ("scalar", "qsc")
        wx = [ev_ts2[14 + bf]] if ti < 2 else [ev_epn_map[ti - 2][0]]
        ev_xld = p.emit(enq, lambda eng, row0=row0, bf=bf: eng.dma_start(
            out=sb["xti"][:, bf], in_=x2_dram[row0:row0 + 128, :]),
            waits=wx, inc=qn, n=16)
        evb = None
        for c in range(16):
            for nn in range(4):
                wb_ = []
                if nn == 0:
                    wb_ = [ev_wb[c]]
                if c == 0:
                    wb_.append(ev_lge[15] if ti == 0 else prev_epn[nn])
                evb = p.emit("tensor", lambda eng, c=c, nn=nn, ti=ti: eng.matmul(
                    PSBIG[:, nn * 512:(nn + 1) * 512],
                    sb["xn2T"][:, ti * 2048 + c * 128:ti * 2048 + (c + 1) * 128],
                    sb["wbase"][:, c * 2048 + nn * 512:c * 2048 + (nn + 1) * 512],
                    start=(c == 0), stop=False),
                    waits=wb_, inc="pe")
        for nn in range(4):
            evb = p.emit("tensor", lambda eng, nn=nn: eng.matmul(
                PSBIG[:, nn * 512:(nn + 1) * 512], sb["onestok"][0:1, :],
                sb["bbase_row"][0:1, nn * 512:(nn + 1) * 512],
                start=False, stop=False), waits=[ev_w5s] if nn == 0 else [], inc="pe")
        evtr5 = p.emit("tensor", lambda eng, ti=ti, bf=bf: eng.transpose(
            (PM1b if bf == 0 else PN_Ab)[:, 0:128],
            sb["midall"][:, ti * 128:(ti + 1) * 128], sb["ident"][:, :]),
            waits=[ev_mg[ti], ev_mtc[ti - 2] if ti >= 2 else None], inc="pe")
        ev_mtc[ti] = p.emit("vector", lambda eng, ti=ti, bf=bf: eng.tensor_copy(
            sb["midT2"][:, bf], (PM1b if bf == 0 else PN_Ab)[:, 0:128]),
            waits=[evtr5, ev_lbm[ti - 2] if ti >= 2 else None], inc="dve", nochain=True)
        evlb = None
        for nn in range(4):
            wlb = [ev_mtc[ti], ev_lorab] if nn == 0 else []
            evlb = p.emit("tensor", lambda eng, nn=nn, bf=bf: eng.matmul(
                PSBIG[:, nn * 512:(nn + 1) * 512], sb["midT2"][:, bf],
                sb["loraB"][:, nn * 512:(nn + 1) * 512],
                start=False, stop=True), waits=wlb, inc="pe")
        ev_lbm[ti] = evlb
        ev_epn = []
        for nn in range(4):
            w2 = [evlb, ev_xld] if nn == 0 else []
            if nn == 0 and ti >= 2:
                w2.append(ev_store[ti - 2])
            ev_epn.append(p.emit("vector", lambda eng, nn=nn, bf=bf: eng.tensor_add(
                sb["outt"][:, bf][:, nn * 512:(nn + 1) * 512],
                PSBIG[:, nn * 512:(nn + 1) * 512],
                sb["xti"][:, bf][:, nn * 512:(nn + 1) * 512]), waits=w2, inc="dve"))
        prev_epn = ev_epn
        if ti == 0:
            ev_epn_map = {}
        ev_epn_map[ti] = ev_epn
        ev_store[ti] = p.emit(enq, lambda eng, row0=row0, bf=bf: eng.dma_start(
            out=out_dram[row0:row0 + 128, :], in_=sb["outt"][:, bf]),
            waits=[ev_epn[3]], inc=qn, n=16)

    p.emit("vector", lambda eng: eng.engine_nop(), waits=[ev_store[-1]])
    p.build()
    return nc


def _rope_tables():
    e = np.arange(0, 32, 2, dtype=np.float32) / np.float32(32)
    inv = (np.float32(1.0) / np.float32(10000.0) ** e).astype(np.float32)
    t = np.arange(S, dtype=np.float32)
    fr = np.outer(t, inv).astype(np.float32)
    return np.cos(fr).astype(np.float32), np.sin(fr).astype(np.float32)


def _pair_heads(t):
    kt, i = divmod(t, 4)
    return 8 * kt + i, 8 * kt + 4 + i


def _qcol_perm():
    cols = []
    for t in range(16):
        hA, hB = _pair_heads(t)
        cols += list(range(hA * 64, hA * 64 + 16))
        cols += list(range(hB * 64, hB * 64 + 16))
        cols += list(range(hA * 64 + 16, hA * 64 + 32))
        cols += list(range(hB * 64 + 16, hB * 64 + 32))
        cols += list(range(hA * 64 + 32, hA * 64 + 64))
        cols += list(range(hB * 64 + 32, hB * 64 + 64))
    return np.array(cols)


def _kcol_perm():
    cols = []
    for kt in range(4):
        a, b = 2 * kt, 2 * kt + 1
        cols += list(range(a * 64, a * 64 + 16))
        cols += list(range(b * 64, b * 64 + 16))
        cols += list(range(a * 64 + 16, a * 64 + 32))
        cols += list(range(b * 64 + 16, b * 64 + 32))
        cols += list(range(a * 64 + 32, a * 64 + 64))
        cols += list(range(b * 64 + 32, b * 64 + 64))
    return np.array(cols)


def _grot_gpass():
    G = np.zeros((128, 128), dtype=np.float32)
    P = np.zeros((128, 128), dtype=np.float32)
    for i in range(16):
        G[i, i] = 1.0
        G[32 + i, i] = -1.0
        G[64 + i, 16 + i] = 1.0
        G[96 + i, 16 + i] = 1.0
        G[16 + i, 64 + i] = 1.0
        G[48 + i, 64 + i] = -1.0
        G[80 + i, 80 + i] = 1.0
        G[112 + i, 80 + i] = 1.0
    for i in range(32):
        P[64 + i, 32 + i] = 1.0
        P[96 + i, 96 + i] = 1.0
    return G.astype(BF), P.astype(BF)


def _wout_row_perm():
    rows = np.zeros(2048, dtype=np.int64)
    for t in range(16):
        hA, hB = _pair_heads(t)
        rows[t * 128: t * 128 + 64] = np.arange(hA * 64, hA * 64 + 64)
        rows[t * 128 + 64: t * 128 + 128] = np.arange(hB * 64, hB * 64 + 64)
    return rows


def _tile4(w):
    d, n = w.shape
    nt = n // 128
    r = w.reshape(16, 128, nt, 128)
    return np.ascontiguousarray(r.transpose(2, 1, 0, 3))


def _masks():
    k = np.arange(128)[:, None]
    q = np.arange(512)[None, :]
    out = []
    for j in range(4):
        mj = (q >= 128 * j + k).astype(np.float32).astype(BF)
        out.append(np.concatenate([mj, mj], axis=1))
    return np.ascontiguousarray(np.concatenate(out, axis=1))


def _biasmat(par):
    bm = np.full((128, 80), SOFT_SHIFT, dtype=np.float32)
    if par == 0:
        base = 0
        for s_ in range(4):
            Ks = KS[s_]
            bm[:, base + Ks - 4: base + Ks] = -10000.0
            base += Ks
    return bm


def host_prepare(inputs):
    x = np.asarray(inputs["L0_attn_x"], dtype=np.float32)
    Wqkv = np.asarray(inputs["L0_attn_Wqkv"], dtype=np.float32)
    bqkv = np.asarray(inputs["L0_attn_bqkv"], dtype=np.float32)
    Wout = np.asarray(inputs["L0_attn_Wout"], dtype=np.float32)
    bout = np.asarray(inputs["L0_attn_bout"], dtype=np.float32)
    base_w = np.asarray(inputs["L0_ffn_base_w"], dtype=np.float32)
    base_b = np.asarray(inputs["L0_ffn_base_b"], dtype=np.float32)
    router_w = np.asarray(inputs["L0_ffn_router_w"], dtype=np.float32)
    router_b = np.asarray(inputs["L0_ffn_router_b"], dtype=np.float32)
    lora_A = np.asarray(inputs["L0_ffn_lora_A"], dtype=np.float32)
    lora_B = np.asarray(inputs["L0_ffn_lora_B"], dtype=np.float32)

    cos, sin = _rope_tables()
    qperm = _qcol_perm()
    kperm = _kcol_perm()
    woutperm = _wout_row_perm()

    wq_h = _tile4(np.ascontiguousarray(Wqkv[:, :2048][:, qperm]))
    wk_h = _tile4(np.ascontiguousarray(Wqkv[:, 2048:2560][:, kperm]))
    wv_h = _tile4(np.ascontiguousarray(Wqkv[:, 2560:3072]))
    wout_h = np.ascontiguousarray(Wout[woutperm].reshape(16, 128, D))
    wbaseT_h = np.ascontiguousarray(base_w.T.reshape(16, 128, D))
    A_all = np.ascontiguousarray(lora_A.transpose(1, 0, 2).reshape(D, 128))
    loraA_h = np.ascontiguousarray(A_all.reshape(16, 128, 128).transpose(1, 0, 2))
    loraB_h = np.ascontiguousarray(lora_B.reshape(128, D))
    routerT_h = np.ascontiguousarray(router_w.T.reshape(16, 128, 8).transpose(1, 0, 2)).astype(BF)

    bq_h = np.ascontiguousarray(bqkv[:2048][qperm].reshape(16, 128).T)
    bk_h = np.ascontiguousarray(bqkv[2048:2560][kperm].reshape(4, 128).T)
    bv_h = np.ascontiguousarray(bqkv[2560:3072].reshape(4, 128).T)
    rbb_h = np.ascontiguousarray(np.broadcast_to(router_b[None, :], (128, 8)))
    grot, gpass = _grot_gpass()
    shared = {
        "masks": _masks(), "grot": grot, "gpass": gpass, "ident": np.eye(128, dtype=BF),
        "ones1": np.ones((1, 64), dtype=np.float32),
        "onestok": np.ones((1, 128), dtype=BF),
        "wq": wq_h, "wk": wk_h, "wv": wv_h, "wout": wout_h,
        "wbaseT": wbaseT_h, "loraA": loraA_h, "loraB": loraB_h,
        "routerT": routerT_h, "bq": bq_h, "bk": bk_h, "bv": bv_h,
        "rbb": rbb_h, "bout_row": bout[None, :].astype(BF),
        "bbase_row": base_b[None, :].astype(BF),
    }

    in_maps = []
    metas = []
    for core in range(8):
        b, par = divmod(core, 2)
        own = [2 * i + par for i in range(4)]
        partner = [2 * i + (1 - par) for i in range(4)]
        rows = np.concatenate([np.arange(k * CH, (k + 1) * CH) for k in own + partner])
        xp = np.ascontiguousarray(x[b][rows])
        ct = cos[rows].T
        st_ = sin[rows].T
        ccss = np.ascontiguousarray(np.concatenate([ct, ct, st_, st_], axis=0))
        sscc = np.ascontiguousarray(np.concatenate([st_, st_, ct, ct], axis=0))
        m = dict(shared)
        m.update({"xp": xp, "ccss": ccss, "sscc": sscc, "biasmat": _biasmat(par)})
        in_maps.append(m)
        metas.append((b, own))
    return in_maps, metas


def kernel(**inputs):
    if "nc" not in _CACHE:
        _CACHE["nc"] = build_nc()
    nc = _CACHE["nc"]
    in_maps, metas = host_prepare(inputs)
    res = run_bass_kernel_spmd(nc, in_maps, core_ids=list(range(8)))
    _CACHE["last"] = res
    out = np.zeros((B, S, D), dtype=np.float32)
    for core in range(8):
        b, own = metas[core]
        o = np.asarray(res.results[core]["out"], dtype=np.float32)
        for i, k in enumerate(own):
            out[b, k * CH:(k + 1) * CH] = o[i * CH:(i + 1) * CH]
    return out
